# version 03
# baseline (speedup 1.0000x reference)
import numpy as np
import ml_dtypes
import concourse.bass as bass
import concourse.mybir as mybir
from concourse.bass_utils import run_bass_kernel_spmd

F32 = mybir.dt.float32
BF16 = mybir.dt.bfloat16
I32 = mybir.dt.int32
U8 = mybir.dt.uint8
AF = mybir.ActivationFunctionType
ALU = mybir.AluOpType
AXX = mybir.AxisListType.X

B, S, D = 2, 4096, 4096
NQ = 1024
H, NOPE, ROPE, VD = 32, 128, 64, 128
QKD = NOPE + ROPE
QR, KVR = 1536, 512
IH, IDD, TOPK = 16, 128, 1024
HID = 11008
GRP = 64
EPS = 1e-5
THETA = 1e4
NEG_BIG = -1.0e30
NEG_CAUSAL = -1.0e38
ISQ_QKD = float(QKD) ** -0.5
IW_SCALE = float(IH) ** -0.5 * float(IDD) ** -0.5

QT = NQ // 128
TT = S // 128
KBLK = 512
NKB = S // KBLK
HB1 = HID // 128
DB2 = D // 128
NP1 = D // 128
NP2 = HID // 128


VOPS = {"tensor_copy", "tensor_tensor", "tensor_scalar", "tensor_add",
        "tensor_sub", "tensor_mul", "tensor_max", "tensor_relu",
        "tensor_single_scalar", "tensor_reduce", "reduce_sum", "reduce_max",
        "scalar_tensor_tensor", "tensor_tensor_scan", "memset", "memzero",
        "select", "copy_predicated", "reciprocal", "transpose",
        "stream_shuffle", "max", "max_index", "bn_stats", "bn_aggr"}


class VProxy:

    def __init__(self, eng, ctx):
        self._e = eng
        self._c = ctx

    def __getattr__(self, name):
        f = getattr(self._e, name)
        if not callable(f) or name not in VOPS:
            return f

        def wrap(*a, **kw):
            self._c.vsync(self._e)
            r = f(*a, **kw)
            self._c.last_v = r
            self._c.last_v_inc = None
            return r
        return wrap


class Arena:
    def __init__(self, tensor, nbytes):
        self.t = tensor
        self.nbytes = nbytes
        self.off = 0
        self.marks = []

    def alloc(self, cols, dtype):
        size = {F32: 4, I32: 4, BF16: 2, U8: 1}[dtype]
        nb = cols * size
        self.off = (self.off + 63) // 64 * 64
        assert self.off + nb <= self.nbytes, \
            f"SBUF arena overflow: {self.off}+{nb} > {self.nbytes}"
        v = self.t[:, self.off:self.off + nb].bitcast(dtype)
        self.off += nb
        return v

    def push(self):
        self.marks.append(self.off)

    def pop(self):
        self.off = self.marks.pop()


class Ctx:
    def __init__(self, nc):
        self.nc = nc
        self.sems = {}
        self.cnt = {}
        self.tiles = {}
        self.scr = {}
        self.inp = {}
        self.psum = []
        self.arena = None
        self.last_v = None
        self.last_v_inc = None

    def sem(self, name):
        s = self.nc.alloc_semaphore(name)
        self.sems[name] = s
        self.cnt[name] = 0
        return s

    def inc(self, instr, sem, n=1):
        instr.then_inc(self.sems[sem], n)
        self.cnt[sem] += n
        if instr is self.last_v:
            self.last_v_inc = (sem, self.cnt[sem])
        return instr

    def wait(self, eng, sem, val):
        eng.wait_ge(self.sems[sem], val)

    def vsync(self, v):
        if self.last_v is None:
            return
        if self.last_v_inc is not None:
            sem, val = self.last_v_inc
            self.last_v = None
            self.last_v_inc = None
            v.wait_ge(self.sems[sem], val)
            return
        self.cnt["vB"] += 1
        self.last_v.then_inc(self.sems["vB"], 1)
        self.last_v = None
        self.last_v_inc = None
        v.wait_ge(self.sems["vB"], self.cnt["vB"])


def emit_p1(ctx):
    nc = ctx.nc
    T = ctx.tiles
    jobs = []
    for rt in range(HB1):
        jobs.append(("w1", rt, 0, NP1, "W1T"))
    for rt in range(HB1):
        jobs.append(("w3", rt, 0, NP1, "W3T"))
    for rt in range(DB2):
        jobs.append(("w2", rt, 0, 43, "W2T"))
        jobs.append(("w2", rt, 43, 43, "W2T"))
    njobs = len(jobs)

    ngroups = [(j[3] + 3) // 4 for j in jobs]
    g_before = np.concatenate([[0], np.cumsum(ngroups)]).astype(int)
    tp_before = np.concatenate([[0], np.cumsum([j[3] for j in jobs])]).astype(int)
    st_w13 = [1 if j[4] != "W2T" else 0 for j in jobs]
    st_w2 = [0 if j[4] != "W2T" else 43 for j in jobs]
    st_slot_w13 = [[0], [0]]
    st_slot_w2 = [[0], [0]]
    for k in range(len(jobs)):
        for bb in range(2):
            st_slot_w13[bb].append(st_slot_w13[bb][-1] + (st_w13[k] if k % 2 == bb else 0))
            st_slot_w2[bb].append(st_slot_w2[bb][-1] + (st_w2[k] if k % 2 == bb else 0))

    c0 = dict(ctx.cnt)

    with nc.Block("p1") as blk:
        @blk.sync
        def _(sy):
            def w2_store(k):
                key, rt, po, npairs, out = jobs[k]
                if out != "W2T":
                    return
                bb = k % 2
                st = T[f"p1st{bb}"]
                ctx.wait(sy, "sS", c0["sS"] + int(g_before[k + 1]))
                w2t = ctx.scr["W2T"]
                for hc in range(npairs):
                    hg = po + hc
                    dwin, dsub = rt // 4, rt % 4
                    ctx.inc(sy.dma_start(
                        out=w2t[dwin, hg * 128:(hg + 1) * 128,
                                dsub * 128:(dsub + 1) * 128],
                        in_=st[:, hc * 128:(hc + 1) * 128]), f"qV{bb}", 16)

            for k, (key, rt, po, npairs, out) in enumerate(jobs):
                bb = k % 2
                if k >= 2:
                    ctx.wait(sy, "vV", c0["vV"] + 5 * (k - 1))
                ntot = 32 if out != "W2T" else 86
                pv = ctx.inp[key].rearrange("(r n) c -> r (n c)", n=ntot)
                sv = ctx.inp[key.replace("w", "s")].rearrange(
                    "(r g) -> r g", g=2 * ntot)
                ctx.inc(sy.dma_start(out=T[f"p1pk{bb}"][:, 0:npairs * 64],
                                     in_=pv[rt * 128:(rt + 1) * 128,
                                            po * 64:(po + npairs) * 64]),
                        f"qL{bb}", 16)
                ctx.inc(sy.dma_start(out=T[f"p1sc{bb}"][:, 0:2 * npairs],
                                     in_=sv[rt * 128:(rt + 1) * 128,
                                            2 * po:2 * (po + npairs)]),
                        f"qL{bb}", 16)
                if k >= 1:
                    w2_store(k - 1)
            w2_store(len(jobs) - 1)
            for q in ("qL0", "qL1", "qV0", "qV1"):
                ctx.wait(sy, q, ctx.cnt[q])

        @blk.vector
        def _(v):
            v = VProxy(v, ctx)
            for k, (key, rt, po, npairs, out) in enumerate(jobs):
                bb = k % 2
                pk, sc = T[f"p1pk{bb}"], T[f"p1sc{bb}"]
                s8, msb, lsb, wt = T[f"p1s8{bb}"], T[f"p1msb{bb}"], T[f"p1lsb{bb}"], T[f"p1wt{bb}"]
                nco = npairs * 64
                ctx.wait(v, f"qL{bb}", c0[f"qL{bb}"] + 32 * (k // 2 + 1))
                if k >= 2:
                    ctx.wait(v, "tP", c0["tP"] + int(tp_before[k - 1]))
                v.tensor_scalar(out=msb[:, 0:nco], in0=pk[:, 0:nco], scalar1=4,
                                scalar2=None, op0=ALU.arith_shift_right)
                v.tensor_scalar(out=lsb[:, 0:nco], in0=pk[:, 0:nco], scalar1=15,
                                scalar2=8, op0=ALU.bitwise_and, op1=ALU.bitwise_xor)
                ctx.inc(v.tensor_scalar(out=lsb[:, 0:nco], in0=lsb[:, 0:nco],
                                        scalar1=8, scalar2=None, op0=ALU.subtract),
                        "vV", 3)
                wt4 = wt[:, 0:2 * nco].rearrange("p (j two ci) -> p j two ci", two=2, ci=64)
                m4 = msb[:, 0:nco].rearrange("p (j one ci) -> p j one ci", one=1, ci=64)
                l4 = lsb[:, 0:nco].rearrange("p (j one ci) -> p j one ci", one=1, ci=64)
                sc3 = sc[:, 0:2 * npairs].rearrange("p (j two) -> p j two", two=2)
                se = sc3[:, :, 0:1].unsqueeze(3).to_broadcast([128, npairs, 1, 64])
                so = sc3[:, :, 1:2].unsqueeze(3).to_broadcast([128, npairs, 1, 64])
                ctx.inc(v.tensor_tensor(out=wt4[:, :, 0:1, :], in0=m4, in1=se,
                                        op=ALU.mult), "vV", 1)
                ctx.inc(v.tensor_tensor(out=wt4[:, :, 1:2, :], in0=l4, in1=so,
                                        op=ALU.mult), "vV", 1)

        @blk.tensor
        def _(t):
            ident = T["ident"]
            for k, (key, rt, po, npairs, out) in enumerate(jobs):
                bb = k % 2
                wt = T[f"p1wt{bb}"]
                ctx.wait(t, "vV", c0["vV"] + 5 * (k + 1))
                for g4 in range(ngroups[k]):
                    gidx = int(g_before[k]) + g4
                    bank = ctx.psum[gidx % 2]
                    if gidx >= 2:
                        ctx.wait(t, "sS", c0["sS"] + gidx - 1)
                    nin = min(4, npairs - g4 * 4)
                    for i in range(nin):
                        j = g4 * 4 + i
                        ctx.inc(t.matmul(bank[:, i * 128:(i + 1) * 128],
                                         wt[:, j * 128:(j + 1) * 128],
                                         ident[:], start=True, stop=True), "tP", 1)

        @blk.scalar
        def _(s):
            for k, (key, rt, po, npairs, out) in enumerate(jobs):
                bb = k % 2
                st = T[f"p1st{bb}"]
                for g4 in range(ngroups[k]):
                    gidx = int(g_before[k]) + g4
                    bank = ctx.psum[gidx % 2]
                    nin = min(4, npairs - g4 * 4)
                    ctx.wait(s, "tP", c0["tP"] + int(tp_before[k]) + g4 * 4 + nin)
                    if k >= 2 and g4 == 0:
                        ctx.wait(s, f"qT{bb}",
                                 c0[f"qT{bb}"] + 16 * st_slot_w13[bb][k - 1])
                        ctx.wait(s, f"qV{bb}",
                                 c0[f"qV{bb}"] + 16 * st_slot_w2[bb][k - 1])
                    ctx.inc(s.activation(out=st[:, g4 * 512:g4 * 512 + nin * 128],
                                         in_=bank[:, 0:nin * 128], func=AF.Copy), "sS", 1)

        @blk.gpsimd
        def _(g):
            for k, (key, rt, po, npairs, out) in enumerate(jobs):
                bb = k % 2
                st = T[f"p1st{bb}"]
                if out == "W2T":
                    continue
                ctx.wait(g, "sS", c0["sS"] + int(g_before[k + 1]))
                ctx.inc(g.dma_start(out=ctx.scr[out][rt],
                                    in_=st[:, 0:npairs * 128]), f"qT{bb}", 16)
            ctx.wait(g, "qT0", ctx.cnt["qT0"])
            ctx.wait(g, "qT1", ctx.cnt["qT1"])


    assert ctx.cnt["vV"] == c0["vV"] + 5 * njobs
    assert ctx.cnt["tP"] == c0["tP"] + int(tp_before[-1])
    assert ctx.cnt["sS"] == c0["sS"] + int(g_before[-1])
    assert (ctx.cnt["qT0"] - c0["qT0"]) + (ctx.cnt["qT1"] - c0["qT1"]) == 16 * sum(st_w13)
    assert (ctx.cnt["qV0"] - c0["qV0"]) + (ctx.cnt["qV1"] - c0["qV1"]) == 16 * sum(st_w2)


def rope_frac_vec(ctx, v, pos_col, pref):
    T = ctx.tiles
    u, ni, nf = T[pref + "_u"], T[pref + "_ni"], T[pref + "_nf"]

    def frac_turn(dst):
        ctx.vsync(v)
        v.tensor_copy(out=ni[:, 0:64], in_=u[:, 0:64])
        ctx.vsync(v)
        v.tensor_copy(out=nf[:, 0:64], in_=ni[:, 0:64])
        ctx.vsync(v)
        v.tensor_tensor(out=dst, in0=u[:, 0:32], in1=nf[:, 0:32], op=ALU.subtract)
        ctx.vsync(v)
        v.tensor_scalar(out=nf[:, 0:32], in0=dst, scalar1=0.5, scalar2=None,
                        op0=ALU.is_gt)
        ctx.inc(v.tensor_tensor(out=dst, in0=dst, in1=nf[:, 0:32], op=ALU.subtract),
                "vV", 1)

    v.tensor_scalar(out=u[:, 0:32], in0=T["invf2pi"][:], scalar1=pos_col,
                    scalar2=None, op0=ALU.mult)
    frac_turn(T[pref + "_fs"][:])
    v.tensor_scalar(out=u[:, 0:32], in0=u[:, 0:32], scalar1=0.25, scalar2=None,
                    op0=ALU.add)
    frac_turn(T[pref + "_fc"][:])


def rope_sin_scalar(ctx, s, pref):
    T = ctx.tiles
    TWO_PI = float(2.0 * np.pi)
    ctx.inc(s.activation(out=T[pref + "_sin"][:], in_=T[pref + "_fs"][:],
                         func=AF.Sin, scale=TWO_PI), "sS", 1)
    ctx.inc(s.activation(out=T[pref + "_cos"][:], in_=T[pref + "_fc"][:],
                         func=AF.Sin, scale=TWO_PI), "sS", 1)


def rope_apply_vec(ctx, v, src_ap, dst_ap, pref, nfreq):
    T = ctx.tiles
    x0 = src_ap.rearrange("p (i two) -> p i two", two=2)[:, :, 0:1]
    x1 = src_ap.rearrange("p (i two) -> p i two", two=2)[:, :, 1:2]
    dr = dst_ap.rearrange("p (i two) -> p i two", two=2)
    c3 = T[pref + "_cos"][:, 0:nfreq].unsqueeze(2)
    s3 = T[pref + "_sin"][:, 0:nfreq].unsqueeze(2)
    ta = T[pref + "_ta"][:, 0:nfreq].unsqueeze(2)
    tb = T[pref + "_tb"][:, 0:nfreq].unsqueeze(2)
    v.tensor_tensor(out=ta, in0=x0, in1=c3, op=ALU.mult)
    v.tensor_tensor(out=tb, in0=x1, in1=s3, op=ALU.mult)
    ctx.inc(v.tensor_tensor(out=dr[:, :, 0:1], in0=ta, in1=tb, op=ALU.subtract), "vV", 1)
    v.tensor_tensor(out=ta, in0=x1, in1=c3, op=ALU.mult)
    v.tensor_tensor(out=tb, in0=x0, in1=s3, op=ALU.mult)
    ctx.inc(v.tensor_tensor(out=dr[:, :, 1:2], in0=ta, in1=tb, op=ALU.add), "vV", 1)


def emit_p2a(ctx):
    nc = ctx.nc
    T = ctx.tiles
    c0 = dict(ctx.cnt)
    nD = D // 128

    with nc.Block("p2a") as blk:
        @blk.gpsimd
        def _(g):
            kvw = ctx.inp["kv_a_w"].rearrange("(c p) f -> c p f", p=128)
            ikw = ctx.inp["idx_wk_w"].rearrange("(c p) f -> c p f", p=128)
            for c in range(nD):
                ctx.inc(g.dma_start(out=T["p2a_kvw"][:, c * 512:(c + 1) * 512],
                                    in_=kvw[c][:, 0:512]), "qG", 16)
                ctx.inc(g.dma_start(out=T["p2a_pik"][:, c * 192:c * 192 + 64],
                                    in_=kvw[c][:, 512:576]), "qG", 16)
                ctx.inc(g.dma_start(out=T["p2a_pik"][:, c * 192 + 64:(c + 1) * 192],
                                    in_=ikw[c]), "qG", 16)
            ctx.wait(g, "qG", ctx.cnt["qG"])

        @blk.sync
        def _(sy):
            for i in range(TT):
                bb = i % 2
                if i >= 2:
                    ctx.wait(sy, "vV", c0["vV"] + 5 * (i - 1))
                    ctx.wait(sy, "sS", c0["sS"] + 11 * (i - 1))
                ctx.inc(sy.dma_start(out=T[f"p2a_xb{bb}"][:],
                                     in_=ctx.inp["xb"][i * 128:(i + 1) * 128, :]),
                        f"qL{bb}", 16)
                ctx.wait(sy, "vV", c0["vV"] + 5 * (i + 1))
                ctx.inc(sy.dma_start(out=ctx.scr["CKV"][i * 128:(i + 1) * 128, :],
                                     in_=T[f"p2a_ckv{bb}"][:]), f"qU{bb}", 16)
            for q in ("qL0", "qL1", "qU0", "qU1"):
                ctx.wait(sy, q, ctx.cnt[q])

        @blk.vector
        def _(v):
            v = VProxy(v, ctx)
            for i in range(TT):
                bb = i % 2
                xb = T[f"p2a_xb{bb}"]
                ctx.wait(v, f"qL{bb}", c0[f"qL{bb}"] + 16 * (i // 2 + 1))
                v.reduce_sum(out=T["st_sum"][:], in_=xb[:], axis=AXX)
                v.tensor_scalar(out=T["st_mean"][:], in0=T["st_sum"][:],
                                scalar1=1.0 / D, scalar2=None, op0=ALU.mult)
                ctx.wait(v, "sS", c0["sS"] + 11 * i + 1)
                v.tensor_scalar(out=T["st_ms"][:], in0=T["st_ssq"][:],
                                scalar1=1.0 / D, scalar2=None, op0=ALU.mult)
                v.tensor_tensor(out=T["st_m2"][:], in0=T["st_mean"][:],
                                in1=T["st_mean"][:], op=ALU.mult)
                ctx.inc(v.tensor_tensor(out=T["st_var"][:], in0=T["st_ms"][:],
                                        in1=T["st_m2"][:], op=ALU.subtract), "vV", 1)
                ctx.wait(v, "sS", c0["sS"] + 11 * i + 2)
                v.reciprocal(out=T["st_rstd"][:], in_=T["st_std"][:])
                ctx.inc(v.tensor_scalar(out=T[f"p2a_xn{bb}"][:], in0=xb[:],
                                        scalar1=T["st_mean"][:], scalar2=T["st_rstd"][:],
                                        op0=ALU.subtract, op1=ALU.mult), "vV", 1)
                ctx.wait(v, "tP", c0["tP"] + 34 * i + 33)
                if i >= 2:
                    ctx.wait(v, f"qU{bb}", c0[f"qU{bb}"] + 16 * (i // 2))
                ctx.inc(v.tensor_copy(out=T[f"p2a_ckv{bb}"][:, 0:512],
                                      in_=ctx.psum[0][:]), "vV", 1)
                ctx.inc(v.tensor_copy(out=T[f"p2a_ckv{bb}"][:, 512:576],
                                      in_=ctx.psum[1][:, 0:64]), "vV", 1)
                ctx.inc(v.tensor_copy(out=T["p2a_iktmp"][:],
                                      in_=ctx.psum[1][:, 64:192]), "vV", 1)

        @blk.scalar
        def _(s):
            for i in range(TT):
                bb = i % 2
                xb = T[f"p2a_xb{bb}"]
                ctx.wait(s, f"qL{bb}", c0[f"qL{bb}"] + 16 * (i // 2 + 1))
                ctx.inc(s.activation(out=T["st_xsq"][:], in_=xb[:], func=AF.Square,
                                     accum_out=T["st_ssq"][:]), "sS", 1)
                ctx.wait(s, "vV", c0["vV"] + 5 * i + 1)
                ctx.inc(s.activation(out=T["st_std"][:], in_=T["st_var"][:],
                                     func=AF.Sqrt, bias=T["eps"][:]), "sS", 1)
                for gq in range(8):
                    bank = ctx.psum[2 + (gq % 2)]
                    ctx.wait(s, "tP", c0["tP"] + 34 * i + 4 * (gq + 1))
                    ctx.inc(s.activation(out=T[f"p2a_xnt{bb}"][:, gq * 512:(gq + 1) * 512],
                                         in_=bank[:], func=AF.Copy), "sS", 1)
                ctx.wait(s, "tP", c0["tP"] + 34 * (i + 1))
                ctx.inc(s.activation(out=T["ikT"][:, i * 128:(i + 1) * 128],
                                     in_=ctx.psum[4][:, 0:128], func=AF.Copy), "sS", 1)

        @blk.tensor
        def _(t):
            ident = T["ident"]
            ctx.wait(t, "qG", c0["qG"] + 16 * 3 * nD)
            for i in range(TT):
                bb = i % 2
                ctx.wait(t, "vV", c0["vV"] + 5 * i + 2)
                for gq in range(8):
                    bank = ctx.psum[2 + (gq % 2)]
                    gidx = i * 8 + gq
                    if gidx >= 2:
                        prev = gidx - 2
                        ctx.wait(t, "sS", c0["sS"] + 11 * (prev // 8) + 2 + (prev % 8) + 1)
                    for u4 in range(4):
                        cch = gq * 4 + u4
                        ctx.inc(t.matmul(bank[:, u4 * 128:(u4 + 1) * 128],
                                         T[f"p2a_xn{bb}"][:, cch * 128:(cch + 1) * 128],
                                         ident[:], start=True, stop=True), "tP", 1)
                if i >= 1:
                    ctx.wait(t, "vV", c0["vV"] + 5 * i)
                for c in range(nD):
                    ctx.wait(t, "sS", c0["sS"] + 11 * i + 2 + c // 4 + 1)
                    lhs = T[f"p2a_xnt{bb}"][:, c * 128:(c + 1) * 128]
                    t.matmul(ctx.psum[0][:], lhs,
                             T["p2a_kvw"][:, c * 512:(c + 1) * 512],
                             start=(c == 0), stop=(c == nD - 1), skip_group_check=True)
                    mm = t.matmul(ctx.psum[1][:, 0:192], lhs,
                                  T["p2a_pik"][:, c * 192:(c + 1) * 192],
                                  start=(c == 0), stop=(c == nD - 1),
                                  skip_group_check=True)
                ctx.inc(mm, "tP", 1)
                ctx.wait(t, "vV", c0["vV"] + 5 * (i + 1))
                ctx.inc(t.matmul(ctx.psum[4][:, 0:128], T["p2a_iktmp"][:], ident[:],
                                 start=True, stop=True), "tP", 1)

    assert ctx.cnt["tP"] == c0["tP"] + 34 * TT
    assert ctx.cnt["vV"] == c0["vV"] + 5 * TT
    assert ctx.cnt["sS"] == c0["sS"] + 11 * TT


def emit_p2b(ctx):
    nc = ctx.nc
    T = ctx.tiles
    c0 = dict(ctx.cnt)
    NKV = 16
    TP_T = 53

    with nc.Block("p2b") as blk:
        @blk.gpsimd
        def _(g):
            kbw = ctx.inp["kv_b_w"].rearrange("(c p) f -> c p f", p=128)
            for c in range(4):
                ctx.inc(g.dma_start(out=T["p2b_kbw"][:, c * 8192:(c + 1) * 8192],
                                    in_=kbw[c]), "qG", 16)
            ctx.inc(g.dma_start(out=T["p2b_normw"][:],
                                in_=ctx.inp["kv_a_norm_w"].rearrange("(c p) -> p c", p=128),
                                allow_slow_non_contiguous=True), "qG", 16)
            ctx.wait(g, "qG", ctx.cnt["qG"])

        @blk.vector
        def _(v):
            v = VProxy(v, ctx)
            ctx.wait(v, "qG", c0["qG"] + 16 * 5)
            for c in range(4):
                ctx.inc(v.tensor_scalar(out=T["p2b_kbw"][:, c * 8192:(c + 1) * 8192],
                                        in0=T["p2b_kbw"][:, c * 8192:(c + 1) * 8192],
                                        scalar1=T["p2b_normw"][:, c:c + 1], scalar2=None,
                                        op0=ALU.mult), "vV", 1)
            P = c0["vV"] + 4
            for i in range(TT):
                bb = i % 2
                ckv = T[f"p2b_ckv{bb}"]
                ctx.wait(v, f"qL{bb}", c0[f"qL{bb}"] + 16 * (i // 2 + 1))
                ctx.wait(v, "sS", c0["sS"] + 30 * i + 1)
                ctx.inc(v.tensor_scalar(out=T["st_ms"][:], in0=T["st_ssq"][:],
                                        scalar1=1.0 / KVR, scalar2=None,
                                        op0=ALU.mult), "vV", 1)
                ctx.wait(v, "sS", c0["sS"] + 30 * i + 2)
                v.reciprocal(out=T["st_rstd"][:], in_=T["st_std"][:])
                ctx.vsync(v)
                if i >= 2:
                    ctx.wait(v, "tP", c0["tP"] + TP_T * (i - 2) + 4)
                ctx.inc(v.tensor_scalar(out=T["p2b_kvx"][:], in0=ckv[:, 0:512],
                                        scalar1=T["st_rstd"][:], scalar2=None,
                                        op0=ALU.mult), "vV", 1)
                rope_frac_vec(ctx, v, T["kpos"][:, i:i + 1], "rp")
                ctx.wait(v, "sS", c0["sS"] + 30 * i + 4)
                if i >= 1:
                    ctx.wait(v, "tP", c0["tP"] + TP_T * i)
                rope_apply_vec(ctx, v, ckv[:, 512:576], T["p2b_kr"][:], "rp", 32)

        @blk.scalar
        def _(s):
            for i in range(TT):
                bb = i % 2
                ckv = T[f"p2b_ckv{bb}"]
                P = c0["vV"] + 4
                ctx.wait(s, f"qL{bb}", c0[f"qL{bb}"] + 16 * (i // 2 + 1))
                ctx.inc(s.activation(out=T["st_xsq"][:, 0:KVR], in_=ckv[:, 0:512],
                                     func=AF.Square, accum_out=T["st_ssq"][:]), "sS", 1)
                ctx.wait(s, "vV", P + 6 * i + 1)
                ctx.inc(s.activation(out=T["st_std"][:], in_=T["st_ms"][:],
                                     func=AF.Sqrt, bias=T["eps"][:]), "sS", 1)
                ctx.wait(s, "vV", P + 6 * i + 4)
                rope_sin_scalar(ctx, s, "rp")
                ctx.wait(s, "tP", c0["tP"] + TP_T * i + 4)
                ctx.inc(s.activation(out=T["p2b_latT"][:], in_=ctx.psum[5][:],
                                     func=AF.Copy), "sS", 1)
                if i >= 2:
                    ctx.wait(s, f"qU{bb}", c0[f"qU{bb}"] + 16 * 65 * (i // 2))
                kvsb = T[f"p2b_kv{bb}"]
                for n in range(NKV):
                    bank = ctx.psum[n % 2]
                    ctx.wait(s, "tP", c0["tP"] + TP_T * i + 4 + (n + 1))
                    ctx.inc(s.activation(out=kvsb[:, n * 512:(n + 1) * 512],
                                         in_=bank[:], func=AF.Copy), "sS", 1)
                for gq in range(8):
                    bank = ctx.psum[2 + (gq % 2)]
                    ctx.wait(s, "tP", c0["tP"] + TP_T * i + 20 + 4 * (gq + 1))
                    ctx.inc(s.activation(out=T[f"p2b_knt{bb}"][:, gq * 512:(gq + 1) * 512],
                                         in_=bank[:], func=AF.Copy), "sS", 1)
                ctx.wait(s, "tP", c0["tP"] + TP_T * (i + 1))
                ctx.inc(s.activation(out=T[f"p2b_kpt{bb}"][0:64, 0:128],
                                     in_=ctx.psum[4][0:64, 0:128], func=AF.Copy), "sS", 1)

        @blk.tensor
        def _(t):
            ident = T["ident"]
            ctx.wait(t, "vV", c0["vV"] + 4)
            P = c0["vV"] + 4
            for i in range(TT):
                bb = i % 2
                ctx.wait(t, "vV", P + 6 * i + 2)
                if i >= 1:
                    ctx.wait(t, "sS", c0["sS"] + 30 * (i - 1) + 5)
                for u4 in range(4):
                    ctx.inc(t.matmul(ctx.psum[5][:, u4 * 128:(u4 + 1) * 128],
                                     T["p2b_kvx"][:, u4 * 128:(u4 + 1) * 128],
                                     ident[:], start=True, stop=True), "tP", 1)
                ctx.wait(t, "sS", c0["sS"] + 30 * i + 5)
                for n in range(NKV):
                    bank = ctx.psum[n % 2]
                    if n >= 2:
                        ctx.wait(t, "sS", c0["sS"] + 30 * i + 5 + (n - 1))
                    for c in range(4):
                        mm = t.matmul(bank[:],
                                      T["p2b_latT"][:, c * 128:(c + 1) * 128],
                                      T["p2b_kbw"][:, c * 8192 + n * 512:c * 8192 + (n + 1) * 512],
                                      start=(c == 0), stop=(c == 3))
                    ctx.inc(mm, "tP", 1)
                for gq in range(8):
                    bank = ctx.psum[2 + (gq % 2)]
                    gidx = i * 8 + gq
                    if gidx >= 2:
                        prev = gidx - 2
                        ctx.wait(t, "sS", c0["sS"] + 30 * (prev // 8) + 21 + (prev % 8) + 1)
                    for u4 in range(4):
                        hh = gq * 4 + u4
                        ctx.wait(t, "sS", c0["sS"] + 30 * i + 5 + hh // 2 + 1)
                        ctx.inc(t.matmul(bank[:, u4 * 128:(u4 + 1) * 128],
                                         T[f"p2b_kv{bb}"][:, 256 * hh:256 * hh + 128],
                                         ident[:], start=True, stop=True), "tP", 1)
                ctx.wait(t, "vV", P + 6 * (i + 1))
                if i >= 1:
                    ctx.wait(t, "sS", c0["sS"] + 30 * i)
                ctx.inc(t.matmul(ctx.psum[4][0:64, 0:128], T["p2b_kr"][:], ident[:],
                                 start=True, stop=True), "tP", 1)

        @blk.sync
        def _(sy):
            for i in range(TT):
                bb = i % 2
                if i >= 2:
                    ctx.wait(sy, "sS", c0["sS"] + 30 * (i - 1))
                    ctx.wait(sy, "vV", c0["vV"] + 4 + 6 * (i - 1))
                ctx.inc(sy.dma_start(out=T[f"p2b_ckv{bb}"][:],
                                     in_=ctx.scr["CKV"][i * 128:(i + 1) * 128, :]),
                        f"qL{bb}", 16)
                ctx.wait(sy, "sS", c0["sS"] + 30 * i + 21)
                for hh in range(H):
                    ctx.inc(sy.dma_start(
                        out=ctx.scr["V"][hh, i * 128:(i + 1) * 128, :],
                        in_=T[f"p2b_kv{bb}"][:, 256 * hh + 128:256 * (hh + 1)]),
                        f"qU{bb}", 16)
                ctx.wait(sy, "sS", c0["sS"] + 30 * i + 29)
                for hh in range(H):
                    ctx.inc(sy.dma_start(
                        out=ctx.scr["KnT"][hh, :, i * 128:(i + 1) * 128],
                        in_=T[f"p2b_knt{bb}"][:, hh * 128:(hh + 1) * 128]),
                        f"qU{bb}", 16)
                ctx.wait(sy, "sS", c0["sS"] + 30 * (i + 1))
                ctx.inc(sy.dma_start(
                    out=ctx.scr["KpT"][:, i * 128:(i + 1) * 128],
                    in_=T[f"p2b_kpt{bb}"][0:64, 0:128]), f"qU{bb}", 16)
            for q in ("qL0", "qL1", "qU0", "qU1"):
                ctx.wait(sy, q, ctx.cnt[q])

    assert ctx.cnt["tP"] == c0["tP"] + TP_T * TT
    assert ctx.cnt["vV"] == c0["vV"] + 4 + 6 * TT
    assert ctx.cnt["sS"] == c0["sS"] + 30 * TT


def rope_apply2(ctx, v, src_ap, dst_ap, cos_ap, sin_ap):
    T = ctx.tiles
    nf = 32
    x0 = src_ap.rearrange("p (i two) -> p i two", two=2)[:, :, 0:1]
    x1 = src_ap.rearrange("p (i two) -> p i two", two=2)[:, :, 1:2]
    dr = dst_ap.rearrange("p (i two) -> p i two", two=2)
    c3 = cos_ap.unsqueeze(2)
    s3 = sin_ap.unsqueeze(2)
    ta = T["rp_ta"][:, 0:nf].unsqueeze(2)
    tb = T["rp_tb"][:, 0:nf].unsqueeze(2)
    v.tensor_tensor(out=ta, in0=x0, in1=c3, op=ALU.mult)
    v.tensor_tensor(out=tb, in0=x1, in1=s3, op=ALU.mult)
    ctx.inc(v.tensor_tensor(out=dr[:, :, 0:1], in0=ta, in1=tb, op=ALU.subtract), "vV", 1)
    v.tensor_tensor(out=ta, in0=x1, in1=c3, op=ALU.mult)
    v.tensor_tensor(out=tb, in0=x0, in1=s3, op=ALU.mult)
    ctx.inc(v.tensor_tensor(out=dr[:, :, 1:2], in0=ta, in1=tb, op=ALU.add), "vV", 1)


def emit_p2c(ctx):
    nc = ctx.nc
    T = ctx.tiles
    ident = T["ident"]

    A = ctx.arena
    A.push()
    for bb in range(2):
        T[f"p2c_xq{bb}"] = A.alloc(D, F32)
    T["p2c_xn"] = A.alloc(D, BF16)
    c0 = dict(ctx.cnt)
    with nc.Block("p2c0") as blk:
        @blk.sync
        def _(sy):
            for j in range(QT):
                bb = j % 2
                if j >= 2:
                    ctx.wait(sy, "vV", c0["vV"] + 2 * (j - 1))
                    ctx.wait(sy, "sS", c0["sS"] + 10 * (j - 1))
                ctx.inc(sy.dma_start(out=T[f"p2c_xq{bb}"][:],
                                     in_=ctx.inp["xq"][j * 128:(j + 1) * 128, :]),
                        f"qL{bb}", 16)
                ctx.wait(sy, "sS", c0["sS"] + 10 * (j + 1))
                for cch in range(32):
                    ctx.inc(sy.dma_start(out=ctx.scr["XNQT"][cch, j],
                                         in_=T[f"xnt_st{bb}"][:, cch * 128:(cch + 1) * 128]),
                            f"qU{bb}", 16)
            for q in ("qL0", "qL1", "qU0", "qU1"):
                ctx.wait(sy, q, ctx.cnt[q])

        @blk.vector
        def _(v):
            v = VProxy(v, ctx)
            for j in range(QT):
                bb = j % 2
                xq = T[f"p2c_xq{bb}"]
                ctx.wait(v, f"qL{bb}", c0[f"qL{bb}"] + 16 * (j // 2 + 1))
                v.reduce_sum(out=T["st_sum"][:], in_=xq[:], axis=AXX)
                ctx.vsync(v)
                v.tensor_scalar(out=T["st_mean"][:], in0=T["st_sum"][:],
                                scalar1=1.0 / D, scalar2=None, op0=ALU.mult)
                ctx.wait(v, "sS", c0["sS"] + 10 * j + 1)
                v.tensor_scalar(out=T["st_ms"][:], in0=T["st_ssq"][:],
                                scalar1=1.0 / D, scalar2=None, op0=ALU.mult)
                v.tensor_tensor(out=T["st_m2"][:], in0=T["st_mean"][:],
                                in1=T["st_mean"][:], op=ALU.mult)
                ctx.inc(v.tensor_tensor(out=T["st_var"][:], in0=T["st_ms"][:],
                                        in1=T["st_m2"][:], op=ALU.subtract), "vV", 1)
                ctx.wait(v, "sS", c0["sS"] + 10 * j + 2)
                v.reciprocal(out=T["st_rstd"][:], in_=T["st_std"][:])
                if j >= 1:
                    ctx.wait(v, "tP", c0["tP"] + 32 * j)
                ctx.inc(v.tensor_scalar(out=T["p2c_xn"][:], in0=xq[:],
                                        scalar1=T["st_mean"][:], scalar2=T["st_rstd"][:],
                                        op0=ALU.subtract, op1=ALU.mult), "vV", 1)

        @blk.scalar
        def _(s):
            for j in range(QT):
                bb = j % 2
                xq = T[f"p2c_xq{bb}"]
                ctx.wait(s, f"qL{bb}", c0[f"qL{bb}"] + 16 * (j // 2 + 1))
                ctx.inc(s.activation(out=T["st_xsq"][:], in_=xq[:], func=AF.Square,
                                     accum_out=T["st_ssq"][:]), "sS", 1)
                ctx.wait(s, "vV", c0["vV"] + 2 * j + 1)
                ctx.inc(s.activation(out=T["st_std"][:], in_=T["st_var"][:],
                                     func=AF.Sqrt, bias=T["eps"][:]), "sS", 1)
                for gq in range(8):
                    bank = ctx.psum[6 + (gq % 2)]
                    ctx.wait(s, "tP", c0["tP"] + 32 * j + 4 * (gq + 1))
                    if j >= 2 and gq == 0:
                        ctx.wait(s, f"qU{j % 2}",
                                 c0[f"qU{j % 2}"] + 16 * 32 * (j // 2))
                    ctx.inc(s.activation(
                        out=T[f"xnt_st{j % 2}"][:, gq * 512:(gq + 1) * 512],
                        in_=bank[:], func=AF.Copy), "sS", 1)

        @blk.tensor
        def _(t):
            for j in range(QT):
                ctx.wait(t, "vV", c0["vV"] + 2 * j + 2)
                for gq in range(8):
                    bank = ctx.psum[6 + (gq % 2)]
                    gidx = j * 8 + gq
                    if gidx >= 2:
                        prev = gidx - 2
                        ctx.wait(t, "sS", c0["sS"] + 10 * (prev // 8) + 2 + (prev % 8) + 1)
                    for u4 in range(4):
                        cch = gq * 4 + u4
                        ctx.inc(t.matmul(bank[:, u4 * 128:(u4 + 1) * 128],
                                         T["p2c_xn"][:, cch * 128:(cch + 1) * 128],
                                         ident[:], start=True, stop=True), "tP", 1)

    assert ctx.cnt["tP"] == c0["tP"] + 32 * QT
    assert ctx.cnt["vV"] == c0["vV"] + 2 * QT
    assert ctx.cnt["sS"] == c0["sS"] + 10 * QT
    A.pop()

    def streamed_proj(tag, w_key, ncols, coltile, evac_cb):
        c1 = dict(ctx.cnt)
        ntiles = ncols // coltile
        nch = 32
        with nc.Block("p2c_" + tag) as blk:
            @blk.sync
            def _(sy):
                for n in range(ntiles):
                    for c in range(nch):
                        k = n * nch + c
                        bb = k % 2
                        if k >= 2:
                            ctx.wait(sy, "tP", c1["tP"] + (k - 1))
                        ctx.inc(sy.dma_start(
                            out=T[f"p2c_xc{bb}"][:].rearrange("p (j q) -> p j q", q=128),
                            in_=ctx.scr["XNQT"][c].rearrange("j p q -> p j q")),
                            f"qU{bb}", 16)
                for q in ("qU0", "qU1"):
                    ctx.wait(sy, q, ctx.cnt[q])

            @blk.gpsimd
            def _(g):
                for n in range(ntiles):
                    for c in range(nch):
                        k = n * nch + c
                        bb = k % 2
                        if k >= 2:
                            ctx.wait(g, "tP", c1["tP"] + (k - 1))
                        wv = ctx.inp[w_key].rearrange("(c p) f -> c p f", p=128)
                        ctx.inc(g.dma_start(
                            out=T[f"p2c_w{bb}"][:, 0:coltile],
                            in_=wv[c][:, n * coltile:(n + 1) * coltile]),
                            f"gL{bb}", 16)
                for q in ("gL0", "gL1"):
                    ctx.wait(g, q, ctx.cnt[q])

            @blk.tensor
            def _(t):
                for n in range(ntiles):
                    for c in range(nch):
                        k = n * nch + c
                        bb = k % 2
                        ctx.wait(t, f"gL{bb}", c1[f"gL{bb}"] + 16 * (k // 2 + 1))
                        ctx.wait(t, f"qU{bb}", c1[f"qU{bb}"] + 16 * (k // 2 + 1))
                        if c == 0 and n >= 1:
                            ctx.wait(t, "sS", c1["sS"] + 8 * n)
                        for j in range(QT):
                            mm = t.matmul(ctx.psum[j][:, 0:coltile],
                                          T[f"p2c_xc{bb}"][:, j * 128:(j + 1) * 128],
                                          T[f"p2c_w{bb}"][:, 0:coltile],
                                          start=(c == 0), stop=(c == nch - 1))
                        ctx.inc(mm, "tP", 1)

            @blk.scalar
            def _(s):
                for n in range(ntiles):
                    ctx.wait(s, "tP", c1["tP"] + nch * (n + 1))
                    for j in range(QT):
                        evac_cb(s, n, j, ctx.psum[j])

        assert ctx.cnt["tP"] == c1["tP"] + nch * ntiles
        assert ctx.cnt["sS"] == c1["sS"] + 8 * ntiles

    A.push()
    T["iq_all"] = A.alloc(QT * 2048, BF16)
    T["iww"] = A.alloc(32 * IH, BF16)

    def iq_evac(s, n, j, bank):
        ctx.inc(s.activation(
            out=T["iq_all"][:, j * 2048 + n * 512:j * 2048 + (n + 1) * 512],
            in_=bank[:], func=AF.Copy), "sS", 1)

    streamed_proj("iq", "idx_wq_w", IH * IDD, 512, iq_evac)

    c4 = dict(ctx.cnt)
    with nc.Block("p2c_iqT") as blk:
        @blk.tensor
        def _(t):
            for j in range(QT):
                for f in range(IH):
                    bank = ctx.psum[f % 2]
                    gidx = j * IH + f
                    if gidx >= 2:
                        prev = gidx - 2
                        ctx.wait(t, "sS", c4["sS"] + prev + 1)
                    ctx.inc(t.matmul(bank[:, 0:128],
                                     T["iq_all"][:, j * 2048 + f * 128:
                                                 j * 2048 + (f + 1) * 128],
                                     ident[:], start=True, stop=True), "tP", 1)

        @blk.scalar
        def _(s):
            for j in range(QT):
                for f in range(IH):
                    bank = ctx.psum[f % 2]
                    ctx.wait(s, "tP", c4["tP"] + j * IH + f + 1)
                    ctx.inc(s.activation(
                        out=T["iqT"][:, f * NQ + j * 128:f * NQ + (j + 1) * 128],
                        in_=bank[:, 0:128], func=AF.Copy), "sS", 1)

    assert ctx.cnt["tP"] == c4["tP"] + IH * QT
    assert ctx.cnt["sS"] == c4["sS"] + IH * QT

    c5 = dict(ctx.cnt)
    with nc.Block("p2c_iw") as blk:
        @blk.gpsimd
        def _(g):
            iww = ctx.inp["idx_weights_w"].rearrange("(c p) f -> c p f", p=128)
            for c in range(32):
                ctx.inc(g.dma_start(out=T["iww"][:, c * IH:(c + 1) * IH], in_=iww[c]),
                        "qG", 16)
            ctx.wait(g, "qG", ctx.cnt["qG"])

        @blk.sync
        def _(sy):
            for c in range(32):
                bb = c % 2
                if c >= 2:
                    ctx.wait(sy, "tP", c5["tP"] + (c - 1) * QT)
                ctx.inc(sy.dma_start(
                    out=T[f"p2c_xc{bb}"][:].rearrange("p (j q) -> p j q", q=128),
                    in_=ctx.scr["XNQT"][c].rearrange("j p q -> p j q")),
                    f"qU{bb}", 16)
            for q in ("qU0", "qU1"):
                ctx.wait(sy, q, ctx.cnt[q])

        @blk.tensor
        def _(t):
            ctx.wait(t, "qG", c5["qG"] + 16 * 32)
            for c in range(32):
                bb = c % 2
                ctx.wait(t, f"qU{bb}", c5[f"qU{bb}"] + 16 * (c // 2 + 1))
                for j in range(QT):
                    bank = ctx.psum[j]
                    if c == 0 and j >= 0:
                        pass
                    mm = t.matmul(bank[:, 0:IH],
                                  T[f"p2c_xc{bb}"][:, j * 128:(j + 1) * 128],
                                  T["iww"][:, c * IH:(c + 1) * IH],
                                  start=(c == 0), stop=(c == 31))
                    ctx.inc(mm, "tP", 1)

        @blk.scalar
        def _(s):
            ctx.wait(s, "tP", c5["tP"] + 32 * QT)
            for j in range(QT):
                bank = ctx.psum[j]
                ctx.inc(s.activation(out=T["iw_sb"][:, j * IH:(j + 1) * IH],
                                     in_=bank[:, 0:IH], func=AF.Copy,
                                     scale=IW_SCALE), "sS", 1)

    assert ctx.cnt["tP"] == c5["tP"] + 32 * QT
    assert ctx.cnt["sS"] == c5["sS"] + QT
    A.pop()

    A.push()
    T["qlat"] = A.alloc(QT * QR, BF16)
    T["qlat_n"] = A.alloc(QR, BF16)
    def qa_evac(s, n, j, bank):
        ctx.inc(s.activation(out=T["qlat"][:, j * QR + n * 512:j * QR + (n + 1) * 512],
                             in_=bank[:], func=AF.Copy), "sS", 1)

    streamed_proj("qa", "q_a_w", QR, 512, qa_evac)

    c2 = dict(ctx.cnt)
    with nc.Block("p2c1b") as blk:
        @blk.scalar
        def _(s):
            for j in range(QT):
                if j >= 1:
                    ctx.wait(s, "vV", c2["vV"] + 2 * j)
                ctx.inc(s.activation(out=T["st_xsq"][:, 0:QR],
                                     in_=T["qlat"][:, j * QR:(j + 1) * QR],
                                     func=AF.Square, accum_out=T["st_ssq"][:]), "sS", 1)
                ctx.wait(s, "vV", c2["vV"] + 2 * j + 1)
                ctx.inc(s.activation(out=T["st_std"][:], in_=T["st_ms"][:],
                                     func=AF.Sqrt, bias=T["eps"][:]), "sS", 1)
                for gq in range(3):
                    bank = ctx.psum[gq % 2]
                    ctx.wait(s, "tP", c2["tP"] + 12 * j + 4 * (gq + 1))
                    ctx.inc(s.activation(
                        out=T["qlatT"].rearrange("p (c q) -> p c q", q=NQ)[
                            :, gq * 4:(gq + 1) * 4, j * 128:(j + 1) * 128],
                        in_=bank[:].rearrange("p (c q) -> p c q", q=128),
                        func=AF.Copy), "sS", 1)

        @blk.vector
        def _(v):
            v = VProxy(v, ctx)
            for j in range(QT):
                ctx.wait(v, "sS", c2["sS"] + 5 * j + 1)
                ctx.inc(v.tensor_scalar(out=T["st_ms"][:], in0=T["st_ssq"][:],
                                        scalar1=1.0 / QR, scalar2=None,
                                        op0=ALU.mult), "vV", 1)
                ctx.wait(v, "sS", c2["sS"] + 5 * j + 2)
                v.reciprocal(out=T["st_rstd"][:], in_=T["st_std"][:])
                if j >= 1:
                    ctx.wait(v, "tP", c2["tP"] + 12 * j)
                ctx.inc(v.tensor_scalar(out=T["qlat_n"][:],
                                        in0=T["qlat"][:, j * QR:(j + 1) * QR],
                                        scalar1=T["st_rstd"][:], scalar2=None,
                                        op0=ALU.mult), "vV", 1)

        @blk.tensor
        def _(t):
            for j in range(QT):
                ctx.wait(t, "vV", c2["vV"] + 2 * j + 2)
                for gq in range(3):
                    bank = ctx.psum[gq % 2]
                    gidx = j * 3 + gq
                    if gidx >= 2:
                        prev = gidx - 2
                        ctx.wait(t, "sS", c2["sS"] + 5 * (prev // 3) + 2 + (prev % 3) + 1)
                    for u4 in range(4):
                        cch = gq * 4 + u4
                        ctx.inc(t.matmul(bank[:, u4 * 128:(u4 + 1) * 128],
                                         T["qlat_n"][:, cch * 128:(cch + 1) * 128],
                                         ident[:], start=True, stop=True), "tP", 1)

    assert ctx.cnt["tP"] == c2["tP"] + 12 * QT
    assert ctx.cnt["vV"] == c2["vV"] + 2 * QT
    assert ctx.cnt["sS"] == c2["sS"] + 5 * QT
    A.pop()

    c6 = dict(ctx.cnt)
    NB3, NCH3 = 16, 12
    V_PRO, S_PRO = 24, 16

    def tp_mm(k):
        return 44 * (k // NCH3) + (k % NCH3) + 1

    def tp_T(n, j):
        return 44 * n + 12 + 4 * (j + 1)

    def vv_fold(k):
        return V_PRO + 44 * (k // NCH3) + (k % NCH3) + 1

    def vv_rope(n, j):
        return V_PRO + 44 * n + 12 + 4 * (j + 1)

    def ss_qblk(n, j):
        return S_PRO + 24 * n + 3 * j + 1

    def ss_stage(n, j):
        return S_PRO + 24 * n + 3 * (j + 1)

    with nc.Block("p2c_qb") as blk:
        @blk.gpsimd
        def _(g):
            ctx.inc(g.dma_start(out=T["normq"][:],
                                in_=ctx.inp["q_a_norm_w"].rearrange("(c p) -> p c", p=128),
                                allow_slow_non_contiguous=True), "qG", 16)
            for n in range(NB3):
                for c in range(NCH3):
                    k = n * NCH3 + c
                    bb = k % 2
                    if k >= 2:
                        ctx.wait(g, "tP", c6["tP"] + tp_mm(k - 1))
                    wv = ctx.inp["q_b_w"].rearrange("(c p) f -> c p f", p=128)
                    ctx.inc(g.dma_start(
                        out=T[f"p2c_w{bb}"][:, 0:384],
                        in_=wv[c][:, n * 384:(n + 1) * 384]), f"gL{bb}", 16)
            ctx.wait(g, "qG", c6["qG"] + 16)
            for q in ("gL0", "gL1"):
                ctx.wait(g, q, ctx.cnt[q])

        @blk.sync
        def _(sy):
            for n in range(NB3):
                for j in range(QT):
                    ctx.wait(sy, "sS", c6["sS"] + ss_stage(n, j))
                    h0 = 2 * n
                    idx = n * QT + j
                    bb2 = idx % 2
                    st = T[f"qstage{bb2}"]
                    ctx.inc(sy.dma_start(out=ctx.scr["QnT"][h0, j],
                                         in_=st[:, 0:128]), f"qU{bb2}", 16)
                    ctx.inc(sy.dma_start(out=ctx.scr["QnT"][h0 + 1, j],
                                         in_=st[:, 128:256]), f"qU{bb2}", 16)
                    ctx.inc(sy.dma_start(out=ctx.scr["QpT"][h0, j],
                                         in_=st[0:64, 256:384]), f"qU{bb2}", 16)
                    ctx.inc(sy.dma_start(out=ctx.scr["QpT"][h0 + 1, j],
                                         in_=st[0:64, 384:512]), f"qU{bb2}", 16)
            for q in ("qL0", "qL1", "qU0", "qU1"):
                ctx.wait(sy, q, ctx.cnt[q])

        @blk.vector
        def _(v):
            v = VProxy(v, ctx)
            ctx.wait(v, "qG", c6["qG"] + 16)
            for j in range(QT):
                rope_frac_vec(ctx, v, T["qpos"][:, j:j + 1], "rp")
                ctx.wait(v, "sS", c6["sS"] + 2 * (j + 1))
                v.tensor_copy(out=T["qcos"][:, j * 32:(j + 1) * 32], in_=T["rp_cos"][:])
                ctx.inc(v.tensor_copy(out=T["qsin"][:, j * 32:(j + 1) * 32],
                                      in_=T["rp_sin"][:]), "vV", 1)
            for n in range(NB3):
                for c in range(NCH3):
                    k = n * NCH3 + c
                    bb = k % 2
                    ctx.wait(v, f"gL{bb}", c6[f"gL{bb}"] + 16 * (k // 2 + 1))
                    ctx.inc(v.tensor_scalar(out=T[f"p2c_w{bb}"][:, 0:384],
                                            in0=T[f"p2c_w{bb}"][:, 0:384],
                                            scalar1=T["normq"][:, c:c + 1],
                                            scalar2=None, op0=ALU.mult), "vV", 1)
                for j in range(QT):
                    ctx.wait(v, "sS", c6["sS"] + ss_qblk(n, j))
                    cosj = T["qcos"][:, j * 32:(j + 1) * 32]
                    sinj = T["qsin"][:, j * 32:(j + 1) * 32]
                    rope_apply2(ctx, v, T["qblk"][:, 128:192], T["qropea"][:],
                                cosj, sinj)
                    rope_apply2(ctx, v, T["qblk"][:, 320:384], T["qropeb"][:],
                                cosj, sinj)

        @blk.tensor
        def _(t):
            for n in range(NB3):
                for c in range(NCH3):
                    k = n * NCH3 + c
                    ctx.wait(t, "vV", c6["vV"] + vv_fold(k))
                    if c == 0 and n >= 1:
                        ctx.wait(t, "sS", c6["sS"] + S_PRO + 24 * n)
                    for j in range(QT):
                        mm = t.matmul(ctx.psum[j][:, 0:384],
                                      T["qlatT"].rearrange("p (c q) -> p c q", q=NQ)[
                                          :, c, j * 128:(j + 1) * 128],
                                      T[f"p2c_w{(k % 2)}"][:, 0:384],
                                      start=(c == 0), stop=(c == NCH3 - 1))
                    ctx.inc(mm, "tP", 1)
                for j in range(QT):
                    ctx.wait(t, "vV", c6["vV"] + vv_rope(n, j))
                    bank = ctx.psum[j]
                    ctx.inc(t.matmul(bank[:, 0:128], T["qblk"][:, 0:128], ident[:],
                                     start=True, stop=True), "tP", 1)
                    ctx.inc(t.matmul(bank[:, 128:256], T["qblk"][:, 192:320], ident[:],
                                     start=True, stop=True), "tP", 1)
                    ctx.inc(t.matmul(bank[0:64, 256:384], T["qropea"][:], ident[:],
                                     start=True, stop=True), "tP", 1)
                    ctx.inc(t.matmul(bank[0:64, 384:512], T["qropeb"][:], ident[:],
                                     start=True, stop=True), "tP", 1)

        @blk.scalar
        def _(s):
            for j in range(QT):
                ctx.wait(s, "vV", c6["vV"] + 3 * j + 2)
                if j >= 1:
                    ctx.wait(s, "vV", c6["vV"] + 3 * j)
                rope_sin_scalar(ctx, s, "rp")
            for n in range(NB3):
                for j in range(QT):
                    idx = n * QT + j
                    if idx >= 1:
                        pn, pj = (n, j - 1) if j >= 1 else (n - 1, QT - 1)
                        ctx.wait(s, "vV", c6["vV"] + vv_rope(pn, pj))
                        ctx.wait(s, "tP", c6["tP"] + tp_T(pn, pj))
                    ctx.wait(s, "tP", c6["tP"] + 44 * n + 12)
                    ctx.inc(s.activation(out=T["qblk"][:], in_=ctx.psum[j][:, 0:384],
                                         func=AF.Copy), "sS", 1)
                    ctx.wait(s, "tP", c6["tP"] + tp_T(n, j))
                    bb2 = idx % 2
                    if idx >= 2:
                        ctx.wait(s, f"qU{bb2}", c6[f"qU{bb2}"] + 16 * 4 * (idx // 2))
                    st = T[f"qstage{bb2}"]
                    s.activation(out=st[:, 0:256], in_=ctx.psum[j][:, 0:256],
                                 func=AF.Copy)
                    ctx.inc(s.activation(out=st[0:64, 256:512],
                                         in_=ctx.psum[j][0:64, 256:512],
                                         func=AF.Copy), "sS", 2)

    assert ctx.cnt["tP"] == c6["tP"] + 44 * NB3
    assert ctx.cnt["vV"] == c6["vV"] + V_PRO + 44 * NB3
    assert ctx.cnt["sS"] == c6["sS"] + S_PRO + 24 * NB3


def emit_p2d(ctx):
    nc = ctx.nc
    T = ctx.tiles
    c0 = dict(ctx.cnt)
    NIT = 28

    units = []
    for j in range(QT):
        for kb in range(j + 1):
            for f in range(IH):
                units.append((j, kb, f))

    vv_after_acc = {}
    vv = c0["vV"]
    u = 0
    for j in range(QT):
        if j < QT - 1:
            vv += 1
        for kb in range(j + 1):
            for f in range(IH):
                vv += 1
                vv_after_acc[u] = vv
                u += 1
            if kb == j:
                vv += 2
        vv += 1 + 1 + 1
        vv += 4 * NIT
        vv += j + 1
    VV_TOT = vv

    def moff(j):
        return sum(jj + 1 for jj in range(j)) * KBLK

    with nc.Block("p2d") as blk:
        @blk.tensor
        def _(t):
            for u, (j, kb, f) in enumerate(units):
                bank = ctx.psum[u % 2]
                if u >= 2:
                    ctx.wait(t, "sS", c0["sS"] + (u - 2) + 1)
                ctx.inc(t.matmul(bank[:], T["iqT"][:, f * NQ + j * 128:
                                                   f * NQ + (j + 1) * 128],
                                 T["ikT"][:, kb * KBLK:(kb + 1) * KBLK],
                                 start=True, stop=True), "tP", 1)

        @blk.scalar
        def _(s):
            for u, (j, kb, f) in enumerate(units):
                bank = ctx.psum[u % 2]
                ctx.wait(s, "tP", c0["tP"] + u + 1)
                if u >= 2:
                    ctx.wait(s, "vV", vv_after_acc[u - 2])
                ctx.inc(s.activation(out=T[f"p2d_r{u % 2}"][:], in_=bank[:],
                                     func=AF.Relu), "sS", 1)

        @blk.vector
        def _(v):
            v = VProxy(v, ctx)
            u = 0
            for j in range(QT):
                prefix = (j + 1) * KBLK
                if j < QT - 1:
                    ctx.inc(v.memset(T["idx_acc"][:, prefix:S], NEG_CAUSAL), "vV", 1)
                for kb in range(j + 1):
                    cols = T["idx_acc"][:, kb * KBLK:(kb + 1) * KBLK]
                    for f in range(IH):
                        ctx.wait(v, "sS", c0["sS"] + u + 1)
                        iwcol = T["iw_sb"][:, j * IH + f:j * IH + f + 1]
                        if f == 0:
                            ctx.inc(v.tensor_scalar(out=cols, in0=T[f"p2d_r{u % 2}"][:],
                                                    scalar1=iwcol, scalar2=None,
                                                    op0=ALU.mult), "vV", 1)
                        else:
                            ctx.inc(v.scalar_tensor_tensor(out=cols,
                                                           in0=T[f"p2d_r{u % 2}"][:],
                                                           scalar=iwcol, in1=cols,
                                                           op0=ALU.mult, op1=ALU.add),
                                    "vV", 1)
                        u += 1
                    if kb == j:
                        v.tensor_scalar(out=T["p2d_th"][:], in0=T["qpos"][:, j:j + 1],
                                        scalar1=float(512 * kb), scalar2=None,
                                        op0=ALU.subtract)
                        ctx.inc(v.tensor_scalar(out=T["p2d_t"][:], in0=T["kcol"][:],
                                                scalar1=T["p2d_th"][:],
                                                scalar2=NEG_CAUSAL,
                                                op0=ALU.is_gt, op1=ALU.mult), "vV", 1)
                        ctx.inc(v.tensor_tensor(out=cols, in0=cols, in1=T["p2d_t"][:],
                                                op=ALU.add), "vV", 1)
                ctx.inc(v.reduce_max(out=T["p2d_mx"][:], in_=T["idx_acc"][:],
                                     axis=AXX), "vV", 1)
                ctx.inc(v.memset(T["p2d_lo"][:], -1.0e8), "vV", 1)
                ctx.vsync(v)
                ctx.inc(v.tensor_scalar(out=T["p2d_hi"][:], in0=T["p2d_mx"][:],
                                        scalar1=1.0, scalar2=None, op0=ALU.add),
                        "vV", 1)
                for rnd in range(2):
                    if rnd == 1:
                        ctx.vsync(v)
                        v.tensor_scalar(out=T["p2d_hi"][:], in0=T["p2d_lo"][:],
                                        scalar1=0.4, scalar2=None, op0=ALU.add)
                    for it in range(NIT):
                        ctx.vsync(v)
                        v.tensor_scalar(out=T["p2d_mid"][:], in0=T["p2d_lo"][:],
                                        scalar1=T["p2d_hi"][:], scalar2=0.5,
                                        op0=ALU.add, op1=ALU.mult)
                        ctx.vsync(v)
                        ctx.inc(v.tensor_scalar(out=T["p2d_cmp"][:], in0=T["idx_acc"][:],
                                                scalar1=T["p2d_mid"][:], scalar2=0.0,
                                                op0=ALU.is_ge, op1=ALU.add,
                                                accum_out=T["p2d_cnt"][:]), "vV", 1)
                        ctx.vsync(v)
                        v.tensor_scalar(out=T["p2d_b"][:], in0=T["p2d_cnt"][:],
                                        scalar1=float(TOPK), scalar2=None,
                                        op0=ALU.is_ge)
                        v.tensor_scalar(out=T["p2d_d"][:], in0=T["p2d_mid"][:],
                                        scalar1=T["p2d_lo"][:], scalar2=None,
                                        op0=ALU.subtract)
                        ctx.vsync(v)
                        v.tensor_scalar(out=T["p2d_e"][:], in0=T["p2d_hi"][:],
                                        scalar1=T["p2d_mid"][:], scalar2=None,
                                        op0=ALU.subtract)
                        ctx.vsync(v)
                        v.scalar_tensor_tensor(out=T["p2d_lo"][:], in0=T["p2d_b"][:],
                                               scalar=T["p2d_d"][:], in1=T["p2d_lo"][:],
                                               op0=ALU.mult, op1=ALU.add)
                        ctx.inc(v.scalar_tensor_tensor(out=T["p2d_hi"][:],
                                                       in0=T["p2d_b"][:],
                                                       scalar=T["p2d_e"][:],
                                                       in1=T["p2d_mid"][:],
                                                       op0=ALU.mult, op1=ALU.add),
                                "vV", 1)
                ctx.vsync(v)
                for kb in range(j + 1):
                    ctx.inc(v.tensor_scalar(
                        out=T["mask"][:, moff(j) + kb * KBLK:moff(j) + (kb + 1) * KBLK],
                        in0=T["idx_acc"][:, kb * KBLK:(kb + 1) * KBLK],
                        scalar1=T["p2d_lo"][:], scalar2=NEG_BIG,
                        op0=ALU.is_lt, op1=ALU.mult), "vV", 1)

    assert ctx.cnt["vV"] == VV_TOT, (ctx.cnt["vV"], VV_TOT)
    assert ctx.cnt["tP"] == c0["tP"] + len(units)
    assert ctx.cnt["sS"] == c0["sS"] + len(units)


def emit_p3(ctx):
    nc = ctx.nc
    T = ctx.tiles
    c0 = dict(ctx.cnt)
    ISQ = float(QKD) ** -0.5

    units = []
    for h in range(H):
        for kb in range(NKB):
            for j in range(kb, QT):
                units.append((h, j, kb))
    NU = len(units)

    tpS, tpPT, tpAV, tpOn = {}, {}, {}, {}
    vvSm, vvStat, vvO, vvOn = {}, {}, {}, {}
    ssA, ssP, ssPT, ssOT = {}, {}, {}, {}
    tp = c0["tP"]; vv = c0["vV"]; ss = c0["sS"]
    for u, (h, j, kb) in enumerate(units):
        tp += 1; tpS[u] = tp
        tp += 1; tpPT[u] = tp
        tp += 1; tpAV[u] = tp
        vv += 1; vvSm[u] = vv
        vv += 1; vvStat[u] = vv
        vv += 1; vvO[u] = vv
        ss += 1; ssA[u] = ss
        ss += 1; ssP[u] = ss
        ss += 1; ssPT[u] = ss
        if kb == j:
            tp += 1; tpOn[(h, j)] = tp
            vv += 1; vvOn[(h, j)] = vv
            ss += 1; ssOT[(h, j)] = ss
    TP_TOT, VV_TOT, SS_TOT = tp, vv, ss
    last_av_of_head = {h: max(tpAV[u] for u, (hh, _, _) in enumerate(units) if hh == h)
                       for h in range(H)}
    ot_order = [(h, j) for h in range(H) for j in range(QT)]

    def moff(j):
        return sum(jj + 1 for jj in range(j)) * KBLK

    with nc.Block("p3") as blk:
        @blk.sync
        def _(sy):
            ctx.inc(sy.dma_start(out=T["xkp"][0:64, :], in_=ctx.scr["KpT"]),
                    "qL0", 16)
            for h in range(H):
                bb = h % 2
                if h >= 2:
                    ctx.wait(sy, "tP", last_av_of_head[h - 2])
                ctx.inc(sy.dma_start(out=T[f"xkn{bb}"][:], in_=ctx.scr["KnT"][h]),
                        f"qL{bb}", 16)
                for tt in range(TT):
                    ctx.inc(sy.dma_start(
                        out=T[f"xv{bb}"][:, tt * VD:(tt + 1) * VD],
                        in_=ctx.scr["V"][h, tt * 128:(tt + 1) * 128, :]),
                        f"qL{bb}", 16)
                for j in range(QT):
                    ctx.inc(sy.dma_start(out=T[f"xqn{bb}"][:, j * 128:(j + 1) * 128],
                                         in_=ctx.scr["QnT"][h, j]), f"qL{bb}", 16)
                    ctx.inc(sy.dma_start(out=T[f"xqp{bb}"][0:64, j * 128:(j + 1) * 128],
                                         in_=ctx.scr["QpT"][h, j]), f"qL{bb}", 16)
                if h >= 1:
                    for j in range(QT):
                        idx = (h - 1) * QT + j
                        ctx.wait(sy, "sS", ssOT[(h - 1, j)])
                        ctx.inc(sy.dma_start(out=ctx.scr["OT"][h - 1, j],
                                             in_=T[f"ot_sb{idx % 2}"][:]),
                                f"qU{idx % 2}", 16)
            for j in range(QT):
                idx = (H - 1) * QT + j
                ctx.wait(sy, "sS", ssOT[(H - 1, j)])
                ctx.inc(sy.dma_start(out=ctx.scr["OT"][H - 1, j],
                                     in_=T[f"ot_sb{idx % 2}"][:]), f"qU{idx % 2}", 16)
            for q in ("qL0", "qL1", "qU0", "qU1"):
                ctx.wait(sy, q, ctx.cnt[q])

        @blk.tensor
        def _(t):
            ident = T["ident"]
            for u, (h, j, kb) in enumerate(units):
                bb = h % 2
                bA = ctx.psum[u % 2]
                bB = ctx.psum[2 + u % 2]
                bC = ctx.psum[4 + u % 2]
                if kb == 0 and j == 0:
                    nload = 16 * (1 + TT + 2 * QT) * (h // 2 + 1) + \
                            (16 if bb == 0 else 0)
                    ctx.wait(t, f"qL{bb}", c0[f"qL{bb}"] + nload)
                if u >= 2:
                    ctx.wait(t, "vV", vvSm[u - 2])
                t.matmul(bA[:], T[f"xqn{bb}"][:, j * 128:(j + 1) * 128],
                         T[f"xkn{bb}"][:, kb * KBLK:(kb + 1) * KBLK],
                         start=True, stop=False)
                ctx.inc(t.matmul(bA[:], T[f"xqp{bb}"][0:64, j * 128:(j + 1) * 128],
                                 T["xkp"][0:64, kb * KBLK:(kb + 1) * KBLK],
                                 start=False, stop=True), "tP", 1)
                ctx.wait(t, "sS", ssP[u])
                if u >= 1:
                    ctx.wait(t, "sS", ssPT[u - 1])
                for i in range(4):
                    mm = t.matmul(bB[:, i * 128:(i + 1) * 128],
                                  T["p3_p"][:, i * 128:(i + 1) * 128], ident[:],
                                  start=True, stop=True)
                ctx.inc(mm, "tP", 1)
                ctx.wait(t, "sS", ssPT[u])
                if u >= 2:
                    ctx.wait(t, "vV", vvO[u - 2])
                for i in range(4):
                    mm = t.matmul(bC[:, 0:VD], T["p3_pt"][:, i * 128:(i + 1) * 128],
                                  T[f"xv{bb}"][:, (kb * 4 + i) * VD:(kb * 4 + i + 1) * VD],
                                  start=(i == 0), stop=(i == 3))
                ctx.inc(mm, "tP", 1)
                if kb == j:
                    ctx.wait(t, "vV", vvOn[(h, j)])
                    oidx = ot_order.index((h, j))
                    if oidx >= 1:
                        ctx.wait(t, "sS", ssOT[ot_order[oidx - 1]])
                    ctx.inc(t.matmul(ctx.psum[6][:, 0:128], T["p3_on"][:], ident[:],
                                     start=True, stop=True), "tP", 1)

        @blk.vector
        def _(v):
            v = VProxy(v, ctx)
            for u, (h, j, kb) in enumerate(units):
                if kb == 0:
                    v.memset(T[f"p3_o{j}"][:], 0.0)
                    v.memset(T[f"p3_l{j}"][:], 0.0)
                    v.memset(T[f"p3_m{j}"][:], -1.0e30)
                bA = ctx.psum[u % 2]
                bC = ctx.psum[4 + u % 2]
                ctx.wait(v, "tP", tpS[u])
                mt = T["mask"][:, moff(j) + kb * KBLK:moff(j) + (kb + 1) * KBLK]
                ctx.inc(v.scalar_tensor_tensor(out=T["p3_sm"][:], in0=bA[:],
                                               scalar=ISQ, in1=mt,
                                               op0=ALU.mult, op1=ALU.add), "vV", 1)
                v.reduce_max(out=T["p3_mkb"][:], in_=T["p3_sm"][:], axis=AXX)
                v.tensor_tensor(out=T["p3_mnew"][:], in0=T[f"p3_m{j}"][:],
                                in1=T["p3_mkb"][:], op=ALU.max)
                v.tensor_scalar(out=T["p3_d"][:], in0=T[f"p3_m{j}"][:],
                                scalar1=T["p3_mnew"][:], scalar2=None,
                                op0=ALU.subtract)
                v.tensor_copy(out=T[f"p3_m{j}"][:], in_=T["p3_mnew"][:])
                ctx.inc(v.tensor_scalar(out=T["p3_nm"][:], in0=T["p3_mnew"][:],
                                        scalar1=-1.0, scalar2=None, op0=ALU.mult),
                        "vV", 1)
                ctx.wait(v, "sS", ssP[u])
                v.scalar_tensor_tensor(out=T[f"p3_l{j}"][:], in0=T[f"p3_l{j}"][:],
                                       scalar=T["p3_al"][:], in1=T["p3_lkb"][:],
                                       op0=ALU.mult, op1=ALU.add)
                ctx.wait(v, "tP", tpAV[u])
                ctx.inc(v.scalar_tensor_tensor(out=T[f"p3_o{j}"][:], in0=T[f"p3_o{j}"][:],
                                               scalar=T["p3_al"][:], in1=bC[:, 0:VD],
                                               op0=ALU.mult, op1=ALU.add), "vV", 1)
                if kb == j:
                    v.reciprocal(out=T["p3_linv"][:], in_=T[f"p3_l{j}"][:])
                    ctx.inc(v.tensor_scalar(out=T["p3_on"][:], in0=T[f"p3_o{j}"][:],
                                            scalar1=T["p3_linv"][:], scalar2=None,
                                            op0=ALU.mult), "vV", 1)

        @blk.scalar
        def _(s):
            for u, (h, j, kb) in enumerate(units):
                bB = ctx.psum[2 + u % 2]
                ctx.wait(s, "vV", vvStat[u])
                ctx.inc(s.activation(out=T["p3_al"][:], in_=T["p3_d"][:],
                                     func=AF.Exp), "sS", 1)
                ctx.inc(s.activation(out=T["p3_p"][:], in_=T["p3_sm"][:], func=AF.Exp,
                                     bias=T["p3_nm"][:], accum_out=T["p3_lkb"][:]),
                        "sS", 1)
                ctx.wait(s, "tP", tpPT[u])
                ctx.inc(s.activation(out=T["p3_pt"][:], in_=bB[:], func=AF.Copy),
                        "sS", 1)
                if kb == j:
                    idx = h * QT + j
                    ctx.wait(s, "tP", tpOn[(h, j)])
                    if idx >= 2:
                        ctx.wait(s, f"qU{idx % 2}",
                                 c0[f"qU{idx % 2}"] + 16 * (idx // 2))
                    ctx.inc(s.activation(out=T[f"ot_sb{idx % 2}"][:],
                                         in_=ctx.psum[6][:, 0:128], func=AF.Copy),
                            "sS", 1)

    assert ctx.cnt["tP"] == TP_TOT, (ctx.cnt["tP"], TP_TOT)
    assert ctx.cnt["vV"] == VV_TOT, (ctx.cnt["vV"], VV_TOT)
    assert ctx.cnt["sS"] == SS_TOT, (ctx.cnt["sS"], SS_TOT)


def emit_p4a(ctx):
    nc = ctx.nc
    T = ctx.tiles
    c0 = dict(ctx.cnt)

    with nc.Block("p4a") as blk:
        @blk.gpsimd
        def _(g):
            for qtr in range(4):
                ow = ctx.inp["o_w"].rearrange("(c p) f -> c p f", p=128)
                if qtr >= 1:
                    ctx.wait(g, "tP", c0["tP"] + 16 * qtr)
                for c in range(32):
                    ctx.inc(g.dma_start(
                        out=T["p4_ow"][:, c * 1024:(c + 1) * 1024],
                        in_=ow[c][:, qtr * 1024:(qtr + 1) * 1024]), "gL0", 16)
            ctx.wait(g, "gL0", ctx.cnt["gL0"])

        @blk.sync
        def _(sy):
            for qtr in range(4):
                for j in range(QT):
                    k = qtr * QT + j
                    bb = k % 2
                    if k >= 2:
                        ctx.wait(sy, "tP", c0["tP"] + 2 * (k - 1))
                    for h in range(H):
                        ctx.inc(sy.dma_start(
                            out=T[f"p4_ot{bb}"][:, h * 128:(h + 1) * 128],
                            in_=ctx.scr["OT"][h, j]), f"qL{bb}", 16)
                    ctx.inc(sy.dma_start(
                        out=T[f"p4_xq{bb}"][:],
                        in_=ctx.inp["xq"][j * 128:(j + 1) * 128,
                                          qtr * 1024:(qtr + 1) * 1024]),
                        f"qL{bb}", 16)
                    if k >= 1:
                        pq, pj = (k - 1) // QT, (k - 1) % QT
                        ctx.wait(sy, "vV", c0["vV"] + 2 * k)
                        ctx.inc(sy.dma_start(
                            out=ctx.scr["XRES"][pj * 128:(pj + 1) * 128,
                                                pq * 1024:(pq + 1) * 1024],
                            in_=T[f"p4_x1{(k - 1) % 2}"][:]), f"qU{(k - 1) % 2}", 16)
            ctx.wait(sy, "vV", c0["vV"] + 2 * 32)
            ctx.inc(sy.dma_start(
                out=ctx.scr["XRES"][7 * 128:8 * 128, 3 * 1024:4 * 1024],
                in_=T["p4_x11"][:]), "qU1", 16)
            for q in ("qL0", "qL1", "qU0", "qU1"):
                ctx.wait(sy, q, ctx.cnt[q])

        @blk.tensor
        def _(t):
            for qtr in range(4):
                ctx.wait(t, "gL0", c0["gL0"] + 16 * 32 * (qtr + 1))
                for j in range(QT):
                    k = qtr * QT + j
                    bb = k % 2
                    ctx.wait(t, f"qL{bb}", c0[f"qL{bb}"] + 16 * 33 * (k // 2 + 1))
                    if k >= 2:
                        ctx.wait(t, "vV", c0["vV"] + 2 * (k - 1))
                    for n in range(2):
                        bank = ctx.psum[2 * (k % 2) + n]
                        for c in range(32):
                            mm = t.matmul(bank[:],
                                          T[f"p4_ot{bb}"][:, c * 128:(c + 1) * 128],
                                          T["p4_ow"][:, c * 1024 + n * 512:
                                                     c * 1024 + (n + 1) * 512],
                                          start=(c == 0), stop=(c == 31))
                        ctx.inc(mm, "tP", 1)

        @blk.vector
        def _(v):
            v = VProxy(v, ctx)
            for qtr in range(4):
                for j in range(QT):
                    k = qtr * QT + j
                    bb = k % 2
                    ctx.wait(v, "tP", c0["tP"] + 2 * (k + 1))
                    if k >= 2:
                        ctx.wait(v, f"qU{bb}", c0[f"qU{bb}"] + 16 * (k // 2))
                    for n in range(2):
                        bank = ctx.psum[2 * (k % 2) + n]
                        ctx.inc(v.tensor_tensor(
                            out=T[f"p4_x1{bb}"][:, n * 512:(n + 1) * 512],
                            in0=bank[:], in1=T[f"p4_xq{bb}"][:, n * 512:(n + 1) * 512],
                            op=ALU.add), "vV", 1)

    assert ctx.cnt["tP"] == c0["tP"] + 2 * 32
    assert ctx.cnt["vV"] == c0["vV"] + 2 * 32


def emit_lnpass(ctx, dst_name):
    nc = ctx.nc
    T = ctx.tiles
    c0 = dict(ctx.cnt)

    with nc.Block("ln_" + dst_name) as blk:
        @blk.sync
        def _(sy):
            for j in range(QT):
                bb = j % 2
                if j >= 2:
                    ctx.wait(sy, "vV", c0["vV"] + 2 * (j - 1))
                    ctx.wait(sy, "sS", c0["sS"] + 10 * (j - 1))
                ctx.inc(sy.dma_start(out=T[f"p2c_xq{bb}"][:],
                                     in_=ctx.scr["XRES"][j * 128:(j + 1) * 128, :]),
                        f"qL{bb}", 16)
            for q in ("qL0", "qL1"):
                ctx.wait(sy, q, ctx.cnt[q])

        @blk.vector
        def _(v):
            v = VProxy(v, ctx)
            for j in range(QT):
                bb = j % 2
                xq = T[f"p2c_xq{bb}"]
                ctx.wait(v, f"qL{bb}", c0[f"qL{bb}"] + 16 * (j // 2 + 1))
                v.reduce_sum(out=T["st_sum"][:], in_=xq[:], axis=AXX)
                ctx.vsync(v)
                v.tensor_scalar(out=T["st_mean"][:], in0=T["st_sum"][:],
                                scalar1=1.0 / D, scalar2=None, op0=ALU.mult)
                ctx.wait(v, "sS", c0["sS"] + 10 * j + 1)
                v.tensor_scalar(out=T["st_ms"][:], in0=T["st_ssq"][:],
                                scalar1=1.0 / D, scalar2=None, op0=ALU.mult)
                v.tensor_tensor(out=T["st_m2"][:], in0=T["st_mean"][:],
                                in1=T["st_mean"][:], op=ALU.mult)
                ctx.inc(v.tensor_tensor(out=T["st_var"][:], in0=T["st_ms"][:],
                                        in1=T["st_m2"][:], op=ALU.subtract), "vV", 1)
                ctx.wait(v, "sS", c0["sS"] + 10 * j + 2)
                v.reciprocal(out=T["st_rstd"][:], in_=T["st_std"][:])
                if j >= 1:
                    ctx.wait(v, "tP", c0["tP"] + 32 * j)
                ctx.inc(v.tensor_scalar(out=T["p2c_xn"][:], in0=xq[:],
                                        scalar1=T["st_mean"][:], scalar2=T["st_rstd"][:],
                                        op0=ALU.subtract, op1=ALU.mult), "vV", 1)

        @blk.scalar
        def _(s):
            for j in range(QT):
                bb = j % 2
                xq = T[f"p2c_xq{bb}"]
                ctx.wait(s, f"qL{bb}", c0[f"qL{bb}"] + 16 * (j // 2 + 1))
                ctx.inc(s.activation(out=T["st_xsq"][:], in_=xq[:], func=AF.Square,
                                     accum_out=T["st_ssq"][:]), "sS", 1)
                ctx.wait(s, "vV", c0["vV"] + 2 * j + 1)
                ctx.inc(s.activation(out=T["st_std"][:], in_=T["st_var"][:],
                                     func=AF.Sqrt, bias=T["eps"][:]), "sS", 1)
                for gq in range(8):
                    bank = ctx.psum[6 + (gq % 2)]
                    ctx.wait(s, "tP", c0["tP"] + 32 * j + 4 * (gq + 1))
                    ctx.inc(s.activation(
                        out=T[dst_name].rearrange("p (c q) -> p c q", q=NQ)[
                            :, gq * 4:(gq + 1) * 4, j * 128:(j + 1) * 128],
                        in_=bank[:].rearrange("p (c q) -> p c q", q=128),
                        func=AF.Copy), "sS", 1)

        @blk.tensor
        def _(t):
            ident = T["ident"]
            for j in range(QT):
                ctx.wait(t, "vV", c0["vV"] + 2 * j + 2)
                for gq in range(8):
                    bank = ctx.psum[6 + (gq % 2)]
                    gidx = j * 8 + gq
                    if gidx >= 2:
                        prev = gidx - 2
                        ctx.wait(t, "sS", c0["sS"] + 10 * (prev // 8) + 2 + (prev % 8) + 1)
                    for u4 in range(4):
                        cch = gq * 4 + u4
                        ctx.inc(t.matmul(bank[:, u4 * 128:(u4 + 1) * 128],
                                         T["p2c_xn"][:, cch * 128:(cch + 1) * 128],
                                         ident[:], start=True, stop=True), "tP", 1)

    assert ctx.cnt["tP"] == c0["tP"] + 32 * QT
    assert ctx.cnt["vV"] == c0["vV"] + 2 * QT
    assert ctx.cnt["sS"] == c0["sS"] + 10 * QT


def emit_p5(ctx, src_name, final):
    nc = ctx.nc
    T = ctx.tiles
    c0 = dict(ctx.cnt)

    with nc.Block("ffn1_" + src_name) as blk:
        @blk.sync
        def _(sy):
            for hb in range(HB1):
                bb = hb % 2
                if hb >= 2:
                    ctx.wait(sy, "tP", c0["tP"] + 4 * (hb - 1))
                ctx.inc(sy.dma_start(out=T[f"p5_w1{bb}"][:], in_=ctx.scr["W1T"][hb]),
                        f"qL{bb}", 16)
                ctx.inc(sy.dma_start(out=T[f"p5_w3{bb}"][:], in_=ctx.scr["W3T"][hb]),
                        f"qL{bb}", 16)
                if hb >= 1:
                    ctx.wait(sy, "vV", c0["vV"] + 2 * hb)
                    ctx.inc(sy.dma_start(out=ctx.scr["ACT"][hb - 1],
                                         in_=T[f"p5_act{(hb - 1) % 2}"][:]),
                            f"qU{(hb - 1) % 2}", 16)
            ctx.wait(sy, "vV", c0["vV"] + 2 * HB1)
            ctx.inc(sy.dma_start(out=ctx.scr["ACT"][HB1 - 1],
                                 in_=T[f"p5_act{(HB1 - 1) % 2}"][:]),
                    f"qU{(HB1 - 1) % 2}", 16)
            for q in ("qL0", "qL1", "qU0", "qU1"):
                ctx.wait(sy, q, ctx.cnt[q])

        @blk.tensor
        def _(t):
            for hb in range(HB1):
                bb = hb % 2
                ctx.wait(t, f"qL{bb}", c0[f"qL{bb}"] + 32 * (hb // 2 + 1))
                if hb >= 2:
                    ctx.wait(t, "vV", c0["vV"] + 2 * (hb - 1))
                for ab in range(2):
                    w = T[f"p5_w1{bb}"] if ab == 0 else T[f"p5_w3{bb}"]
                    for c in range(32):
                        for th in range(2):
                            bank = ctx.psum[(hb % 2) * 4 + th * 2 + ab]
                            mm = t.matmul(bank[:],
                                          w[:, c * 128:(c + 1) * 128],
                                          T[src_name][:, c * NQ + th * 512:
                                                      c * NQ + (th + 1) * 512],
                                          start=(c == 0), stop=(c == 31),
                                          skip_group_check=True)
                    if ab == 0:
                        ctx.inc(mm, "tP", 2)
                    else:
                        ctx.inc(mm, "tP", 2)

        @blk.scalar
        def _(s):
            for hb in range(HB1):
                for th in range(2):
                    idx = 2 * hb + th
                    bank_a = ctx.psum[(hb % 2) * 4 + th * 2]
                    ctx.wait(s, "tP", c0["tP"] + 4 * hb + 2)
                    if idx >= 1:
                        ctx.wait(s, "vV", c0["vV"] + idx)
                    ctx.inc(s.activation(out=T["p5_sa"][:], in_=bank_a[:],
                                         func=AF.Silu), "sS", 1)

        @blk.vector
        def _(v):
            v = VProxy(v, ctx)
            for hb in range(HB1):
                bb = hb % 2
                for th in range(2):
                    bank_b = ctx.psum[(hb % 2) * 4 + th * 2 + 1]
                    ctx.wait(v, "sS", c0["sS"] + 2 * hb + th + 1)
                    ctx.wait(v, "tP", c0["tP"] + 4 * hb + 4)
                    if hb >= 2 and th == 0:
                        ctx.wait(v, f"qU{bb}", c0[f"qU{bb}"] + 16 * (hb // 2))
                    ctx.inc(v.tensor_tensor(
                        out=T[f"p5_act{bb}"][:, th * 512:(th + 1) * 512],
                        in0=T["p5_sa"][:], in1=bank_b[:], op=ALU.mult), "vV", 1)

    assert ctx.cnt["tP"] == c0["tP"] + 4 * HB1
    assert ctx.cnt["vV"] == c0["vV"] + 2 * HB1
    assert ctx.cnt["sS"] == c0["sS"] + 2 * HB1

    c1 = dict(ctx.cnt)
    with nc.Block("ffn2_" + src_name) as blk:
        @blk.sync
        def _(sy):
            for dwin in range(8):
                for hb in range(HB1):
                    k = dwin * HB1 + hb
                    bb = k % 2
                    if k >= 2:
                        ctx.wait(sy, "tP", c1["tP"] + (k - 1))
                    ctx.inc(sy.dma_start(out=T[f"p5_a{bb}"][:], in_=ctx.scr["ACT"][hb]),
                            f"qL{bb}", 16)
                    ctx.inc(sy.dma_start(
                        out=T[f"p5_w2{bb}"][:],
                        in_=ctx.scr["W2T"][dwin, hb * 128:(hb + 1) * 128, :]),
                        f"qL{bb}", 16)
                for tsub in range(QT):
                    k2 = dwin * QT + tsub
                    bb2 = k2 % 2
                    if k2 >= 2:
                        ctx.wait(sy, "vV", c1["vV"] + k2 - 1)
                    ctx.inc(sy.dma_start(
                        out=T[f"p5_xr{bb2}"][:],
                        in_=ctx.scr["XRES"][tsub * 128:(tsub + 1) * 128,
                                            dwin * 512:(dwin + 1) * 512]),
                        f"qV{bb2}", 16)
                    ctx.wait(sy, "vV", c1["vV"] + k2 + 1)
                    dst = (ctx.out_y if final else ctx.scr["XRES"])
                    ctx.inc(sy.dma_start(
                        out=dst[tsub * 128:(tsub + 1) * 128,
                                dwin * 512:(dwin + 1) * 512],
                        in_=T[f"p5_x2{bb2}"][:]), f"qU{bb2}", 16)
            for q in ("qL0", "qL1", "qU0", "qU1", "qV0", "qV1"):
                ctx.wait(sy, q, ctx.cnt[q])

        @blk.tensor
        def _(t):
            for dwin in range(8):
                for hb in range(HB1):
                    k = dwin * HB1 + hb
                    bb = k % 2
                    ctx.wait(t, f"qL{bb}", c1[f"qL{bb}"] + 32 * (k // 2 + 1))
                    if hb == 0 and dwin >= 1:
                        ctx.wait(t, "vV", c1["vV"] + QT * dwin)
                    for tsub in range(QT):
                        mm = t.matmul(ctx.psum[tsub][:],
                                      T[f"p5_a{bb}"][:, tsub * 128:(tsub + 1) * 128],
                                      T[f"p5_w2{bb}"][:],
                                      start=(hb == 0), stop=(hb == HB1 - 1))
                    ctx.inc(mm, "tP", 1)

        @blk.vector
        def _(v):
            v = VProxy(v, ctx)
            for dwin in range(8):
                ctx.wait(v, "tP", c1["tP"] + HB1 * (dwin + 1))
                for tsub in range(QT):
                    k2 = dwin * QT + tsub
                    bb2 = k2 % 2
                    ctx.wait(v, f"qV{bb2}", c1[f"qV{bb2}"] + 16 * (k2 // 2 + 1))
                    if k2 >= 2:
                        ctx.wait(v, f"qU{bb2}", c1[f"qU{bb2}"] + 16 * (k2 // 2))
                    ctx.inc(v.tensor_tensor(out=T[f"p5_x2{bb2}"][:],
                                            in0=ctx.psum[tsub][:],
                                            in1=T[f"p5_xr{bb2}"][:],
                                            op=ALU.add), "vV", 1)

    assert ctx.cnt["tP"] == c1["tP"] + 8 * HB1
    assert ctx.cnt["vV"] == c1["vV"] + 8 * QT


def emit_p6(ctx, src_name):
    nc = ctx.nc
    T = ctx.tiles
    ident = T["ident"]

    c0 = dict(ctx.cnt)
    with nc.Block("p6a") as blk:
        @blk.sync
        def _(sy):
            for mt in range(32):
                bb = 0
                if mt >= 1:
                    ctx.wait(sy, "vV", c0["vV"] + 3 * mt - 1)
                ctx.inc(sy.dma_start(out=T["p6_mk0"][:],
                                     in_=ctx.inp["ab_mask"][mt * 128:(mt + 1) * 128, :]),
                        "qL0", 16)
            ctx.wait(sy, "vV", c0["vV"] + 3 * 32 + 1)
            ctx.inc(sy.dma_start(out=ctx.scr["BMT"], in_=T["p6_bmT"][:]), "qU0", 16)
            for q in ("qL0", "qU0"):
                ctx.wait(sy, q, ctx.cnt[q])

        @blk.vector
        def _(v):
            v = VProxy(v, ctx)
            for mt in range(32):
                ctx.wait(v, "qL0", c0["qL0"] + 16 * (mt + 1))
                ctx.inc(v.tensor_reduce(
                    out=T["p6_cs"][:],
                    in_=T["p6_mk0"][:].rearrange("p (g c) -> p g c", c=32),
                    axis=AXX, op=ALU.add), "vV", 1)
                ctx.vsync(v)
                ctx.inc(v.tensor_copy(out=T["p6_csb"][:], in_=T["p6_cs"][:]), "vV", 1)
                ctx.wait(v, "tP", c0["tP"] + mt + 1)
                ctx.inc(v.tensor_reduce(
                    out=T["p6_bmT"][:, mt * 4:(mt + 1) * 4],
                    in_=ctx.psum[6][:, 0:128].rearrange("p (f c) -> p f c", c=32),
                    axis=AXX, op=ALU.add), "vV", 1)
            ctx.vsync(v)
            ctx.inc(v.tensor_scalar(out=T["p6_bmT"][:], in0=T["p6_bmT"][:],
                                    scalar1=0.0, scalar2=None, op0=ALU.is_gt), "vV", 1)

        @blk.tensor
        def _(t):
            for mt in range(32):
                ctx.wait(t, "vV", c0["vV"] + 3 * mt + 2)
                if mt >= 1:
                    ctx.wait(t, "vV", c0["vV"] + 3 * mt)
                ctx.inc(t.matmul(ctx.psum[6][:, 0:128], T["p6_csb"][:], ident[:],
                                 start=True, stop=True), "tP", 1)

    assert ctx.cnt["vV"] == c0["vV"] + 3 * 32 + 1
    assert ctx.cnt["tP"] == c0["tP"] + 32

    c1 = dict(ctx.cnt)
    with nc.Block("p6b") as blk:
        @blk.gpsimd
        def _(g):
            for ksub in range(32):
                src_bm = ctx.scr["BMT"][ksub * 4:(ksub + 1) * 4, :]
                src_bm = src_bm.unsqueeze(1).to_broadcast([4, 32, 128])
                ctx.inc(g.dma_start(
                    out=T["p6_bme"][:, ksub * 128:(ksub + 1) * 128],
                    in_=src_bm), "qG", 16)
            for nt in range(32):
                bb = nt % 2
                if nt >= 2:
                    ctx.wait(g, "tP", c1["tP"] + 32 * (nt - 1))
                ctx.inc(g.dma_start(out=T[f"p6_w{bb}"][:],
                                    in_=ctx.inp["ab_w"][nt * 128:(nt + 1) * 128, :]),
                        f"gL{bb}", 16)
            ctx.wait(g, "qG", ctx.cnt["qG"])
            for q in ("gL0", "gL1"):
                ctx.wait(g, q, ctx.cnt[q])

        @blk.tensor
        def _(t):
            for nt in range(32):
                bb = nt % 2
                ctx.wait(t, f"gL{bb}", c1[f"gL{bb}"] + 16 * (nt // 2 + 1))
                for gq in range(8):
                    bank = ctx.psum[gq % 2]
                    gidx = nt * 8 + gq
                    if gidx >= 2:
                        prev = gidx - 2
                        ctx.wait(t, "vV", c1["vV"] + 4 * prev + 4)
                    for u4 in range(4):
                        ksub = gq * 4 + u4
                        ctx.inc(t.matmul(bank[:, u4 * 128:(u4 + 1) * 128],
                                         T[f"p6_w{bb}"][:, ksub * 128:(ksub + 1) * 128],
                                         ident[:], start=True, stop=True), "tP", 1)

        @blk.vector
        def _(v):
            v = VProxy(v, ctx)
            ctx.wait(v, "qG", c1["qG"] + 16 * 32)
            for nt in range(32):
                for gq in range(8):
                    bank = ctx.psum[gq % 2]
                    gidx = nt * 8 + gq
                    ctx.wait(v, "tP", c1["tP"] + 4 * gidx + 4)
                    if gidx >= 2:
                        ctx.wait(v, f"qU{gidx % 2}",
                                 c1[f"qU{gidx % 2}"] + 16 * 4 * (gidx // 2))
                    for u4 in range(4):
                        ksub = gq * 4 + u4
                        bm = T["p6_bme"][:, ksub * 128 + nt * 4:ksub * 128 + (nt + 1) * 4]
                        bm = bm.unsqueeze(2).to_broadcast([128, 4, 32])
                        ctx.inc(v.tensor_tensor(
                            out=T[f"p6_st{gidx % 2}"][:, u4 * 128:(u4 + 1) * 128]
                                .rearrange("p (a b) -> p a b", b=32),
                            in0=bank[:, u4 * 128:(u4 + 1) * 128]
                                .rearrange("p (a b) -> p a b", b=32),
                            in1=bm, op=ALU.mult), "vV", 1)

        @blk.sync
        def _(sy):
            for nt in range(32):
                for gq in range(8):
                    gidx = nt * 8 + gq
                    ctx.wait(sy, "vV", c1["vV"] + 4 * gidx + 4)
                    for u4 in range(4):
                        ksub = gq * 4 + u4
                        ctx.inc(sy.dma_start(
                            out=ctx.scr["WMT"][ksub, nt],
                            in_=T[f"p6_st{gidx % 2}"][:, u4 * 128:(u4 + 1) * 128]),
                            f"qU{gidx % 2}", 16)
            for q in ("qU0", "qU1"):
                ctx.wait(sy, q, ctx.cnt[q])

    assert ctx.cnt["tP"] == c1["tP"] + 32 * 32
    assert ctx.cnt["vV"] == c1["vV"] + 32 * 32

    c2 = dict(ctx.cnt)
    with nc.Block("p6c") as blk:
        @blk.sync
        def _(sy):
            for nw in range(8):
                for kc in range(32):
                    k = nw * 32 + kc
                    bb = k % 2
                    if k >= 2:
                        ctx.wait(sy, "tP", c2["tP"] + (k - 1))
                    ctx.inc(sy.dma_start(
                        out=T[f"p6_r{bb}"][:],
                        in_=ctx.scr["WMT"][kc, nw * 4:(nw + 1) * 4].rearrange(
                            "n p q -> p n q")), f"qL{bb}", 16)
                for tsub in range(QT):
                    k2 = nw * QT + tsub
                    bb2 = k2 % 2
                    if k2 >= 2:
                        ctx.wait(sy, "vV", c2["vV"] + k2 - 1)
                    ctx.inc(sy.dma_start(
                        out=T[f"p5_xr{bb2}"][:],
                        in_=ctx.scr["XRES"][tsub * 128:(tsub + 1) * 128,
                                            nw * 512:(nw + 1) * 512]),
                        f"qV{bb2}", 16)
                    ctx.wait(sy, "vV", c2["vV"] + k2 + 1)
                    ctx.inc(sy.dma_start(
                        out=ctx.scr["XRES"][tsub * 128:(tsub + 1) * 128,
                                            nw * 512:(nw + 1) * 512],
                        in_=T[f"p5_x2{bb2}"][:]), f"qU{bb2}", 16)
            for q in ("qL0", "qL1", "qU0", "qU1", "qV0", "qV1"):
                ctx.wait(sy, q, ctx.cnt[q])

        @blk.tensor
        def _(t):
            for nw in range(8):
                for kc in range(32):
                    k = nw * 32 + kc
                    bb = k % 2
                    ctx.wait(t, f"qL{bb}", c2[f"qL{bb}"] + 16 * (k // 2 + 1))
                    if kc == 0 and nw >= 1:
                        ctx.wait(t, "vV", c2["vV"] + QT * nw)
                    for tsub in range(QT):
                        mm = t.matmul(ctx.psum[tsub][:],
                                      T[src_name][:, kc * NQ + tsub * 128:
                                                  kc * NQ + (tsub + 1) * 128],
                                      T[f"p6_r{bb}"][:],
                                      start=(kc == 0), stop=(kc == 31))
                    ctx.inc(mm, "tP", 1)

        @blk.vector
        def _(v):
            v = VProxy(v, ctx)
            for nw in range(8):
                ctx.wait(v, "tP", c2["tP"] + 32 * (nw + 1))
                for tsub in range(QT):
                    k2 = nw * QT + tsub
                    bb2 = k2 % 2
                    ctx.wait(v, f"qV{bb2}", c2[f"qV{bb2}"] + 16 * (k2 // 2 + 1))
                    if k2 >= 2:
                        ctx.wait(v, f"qU{bb2}", c2[f"qU{bb2}"] + 16 * (k2 // 2))
                    ctx.inc(v.tensor_tensor(out=T[f"p5_x2{bb2}"][:],
                                            in0=ctx.psum[tsub][:],
                                            in1=T[f"p5_xr{bb2}"][:],
                                            op=ALU.add), "vV", 1)

    assert ctx.cnt["tP"] == c2["tP"] + 8 * 32
    assert ctx.cnt["vV"] == c2["vV"] + 8 * QT


def build_program(enabled=("p1",), taps=()):
    nc = bass.Bass("TRN2", target_bir_lowering=False, detect_race_conditions=False)
    ctx = Ctx(nc)

    def ext(name, shape, dt=F32):
        ctx.inp[name] = nc.dram_tensor(name, shape, dt, kind="ExternalInput").ap()

    ext("xb", [S, D]); ext("xq", [NQ, D])
    ext("q_a_w", [D, QR]); ext("q_a_norm_w", [QR]); ext("q_b_w", [QR, H * QKD])
    ext("kv_a_w", [D, KVR + ROPE]); ext("kv_a_norm_w", [KVR])
    ext("kv_b_w", [KVR, H * (NOPE + VD)])
    ext("o_w", [H * VD, D])
    ext("idx_wq_w", [D, IH * IDD]); ext("idx_wk_w", [D, IDD]); ext("idx_weights_w", [D, IH])
    ext("ab_w", [D, D]); ext("ab_mask", [D, D], I32)
    ext("w1", [HID * D // GRP // 2, GRP], I32); ext("s1", [HID * D // GRP])
    ext("w2", [HID * D // GRP // 2, GRP], I32); ext("s2", [HID * D // GRP])
    ext("w3", [HID * D // GRP // 2, GRP], I32); ext("s3", [HID * D // GRP])
    ext("ident_c", [128, 128], BF16)
    ext("qpos_c", [NQ]); ext("kpos_c", [S]); ext("invf2pi_c", [128, ROPE // 2])

    out_y = nc.dram_tensor("y", [NQ, D], F32, kind="ExternalOutput").ap()
    ctx.out_y = out_y

    def scr(name, shape, dt=BF16):
        ctx.scr[name] = nc.dram_tensor("scr_" + name, shape, dt, kind="Internal").ap()

    scr("W1T", [HB1, 128, 32 * 128]); scr("W3T", [HB1, 128, 32 * 128])
    scr("W2T", [8, HID, 512])
    scr("CKV", [S, KVR + ROPE], F32)
    scr("XNQT", [32, QT, 128, 128])
    scr("KnT", [H, NOPE, S]); scr("KpT", [ROPE, S]); scr("V", [H, S, VD])
    scr("QnT", [H, QT, NOPE, 128]); scr("QpT", [H, QT, ROPE, 128])
    scr("OT", [H, QT, VD, 128])
    scr("ACT", [HB1, 128, NQ])
    scr("XRES", [NQ, D], F32)
    scr("BMT", [128, 128], F32)
    scr("WMT", [32, 32, 128, 128])
    scr("MASKDUMP", [128, 36 * KBLK])

    def snap(tag):
        if "snaps" not in enabled:
            return
        out_t = nc.dram_tensor("snap_" + tag, [NQ, D], F32, kind="ExternalOutput").ap()
        with nc.Block("snap_" + tag) as blk:
            @blk.sync
            def _(sy):
                ctx.inc(sy.dma_start(out=out_t, in_=ctx.scr["XRES"]), "qS", 16)
                ctx.wait(sy, "qS", ctx.cnt["qS"])
    ctx.snap = snap

    for n in ("qS", "qG", "qC", "tP", "vV", "sS", "gP", "vB",
              "qL0", "qL1", "qT0", "qT1", "qU0", "qU1", "qV0", "qV1",
              "gL0", "gL1"):
        ctx.sem(n)

    ARENA_B = 200 * 1024
    arena_t = nc.alloc_sbuf_tensor("arena", [128, ARENA_B], U8)
    ctx.arena = Arena(arena_t, ARENA_B)
    A = ctx.arena
    T = ctx.tiles

    T["ident"] = A.alloc(128, BF16)
    T["qpos"] = A.alloc(QT, F32)
    T["kpos"] = A.alloc(TT, F32)
    T["invf2pi"] = A.alloc(ROPE // 2, F32)
    T["kcol"] = A.alloc(KBLK, F32)
    for nm in ("st_sum", "st_mean", "st_ms", "st_m2", "st_var", "st_std",
               "st_rstd", "st_ssq"):
        T[nm] = A.alloc(1, F32)
    T["st_xsq"] = A.alloc(D, F32)
    T["kcol_i"] = T["st_xsq"][:, 0:KBLK].bitcast(I32)
    T["eps"] = A.alloc(1, F32)
    for nm in ("rp_u", "rp_ni", "rp_nf"):
        T[nm] = A.alloc(2 * ROPE, I32 if nm == "rp_ni" else F32)
    for nm in ("rp_fs", "rp_fc", "rp_cos", "rp_sin", "rp_ta", "rp_tb"):
        T[nm] = A.alloc(ROPE // 2, F32)
    T["ikT"] = A.alloc(S, BF16)

    ctx.psum = [nc.alloc_psum_tensor(f"pb{i}", [128, 512], F32) for i in range(8)]

    with nc.Block("p0") as blk:
        @blk.sync
        def _(sy):
            ctx.inc(sy.dma_start(out=T["ident"][:], in_=ctx.inp["ident_c"]), "qS", 16)
            ctx.inc(sy.dma_start(out=T["qpos"][:],
                                 in_=ctx.inp["qpos_c"].rearrange("(j p) -> p j", p=128),
                                 allow_slow_non_contiguous=True), "qS", 16)
            ctx.inc(sy.dma_start(out=T["kpos"][:],
                                 in_=ctx.inp["kpos_c"].rearrange("(t p) -> p t", p=128),
                                 allow_slow_non_contiguous=True), "qS", 16)
            ctx.inc(sy.dma_start(out=T["invf2pi"][:], in_=ctx.inp["invf2pi_c"]), "qS", 16)
            ctx.wait(sy, "qS", ctx.cnt["qS"])

        @blk.gpsimd
        def _(g):
            ctx.inc(g.iota(T["kcol_i"][:], pattern=[[1, KBLK]], base=0,
                           channel_multiplier=0), "gP", 1)

        @blk.vector
        def _(v):
            v = VProxy(v, ctx)
            ctx.wait(v, "gP", 1)
            v.memset(T["eps"][:], EPS)
            v.memset(T["rp_u"][:, 32:64], 0.0)
            ctx.inc(v.tensor_copy(out=T["kcol"][:], in_=T["kcol_i"][:]), "vV", 1)

    A.push()
    NPB = 43
    msb_sh = A.alloc(NPB * 64, I32)
    lsb_sh = A.alloc(NPB * 64, I32)
    for bb in range(2):
        T[f"p1pk{bb}"] = A.alloc(NPB * 64, I32)
        T[f"p1sc{bb}"] = A.alloc(2 * NPB, F32)
        T[f"p1s8{bb}"] = A.alloc(2 * NPB, F32)
        T[f"p1msb{bb}"] = msb_sh
        T[f"p1lsb{bb}"] = lsb_sh
        T[f"p1wt{bb}"] = A.alloc(NPB * 128, BF16)
        T[f"p1st{bb}"] = A.alloc(NPB * 128, BF16)

    if "p1" in enabled:
        emit_p1(ctx)
    A.pop()

    if "p2a" in enabled:
        A.push()
        for bb in range(2):
            T[f"p2a_xb{bb}"] = A.alloc(D, F32)
            T[f"p2a_xn{bb}"] = A.alloc(D, BF16)
            T[f"p2a_xnt{bb}"] = A.alloc(D, BF16)
            T[f"p2a_ckv{bb}"] = A.alloc(KVR + ROPE, F32)
        T["p2a_kvw"] = A.alloc(32 * KVR, BF16)
        T["p2a_pik"] = A.alloc(32 * (ROPE + IDD), BF16)
        T["p2a_iktmp"] = A.alloc(128, BF16)
        emit_p2a(ctx)
        A.pop()

    if "p2b" in enabled:
        A.push()
        for bb in range(2):
            T[f"p2b_ckv{bb}"] = A.alloc(KVR + ROPE, F32)
            T[f"p2b_kv{bb}"] = A.alloc(H * (NOPE + VD), BF16)
            T[f"p2b_knt{bb}"] = A.alloc(D, BF16)
            T[f"p2b_kpt{bb}"] = A.alloc(128, BF16)
        T["p2b_kbw"] = A.alloc(4 * H * (NOPE + VD), BF16)
        T["p2b_normw"] = A.alloc(4, F32)
        T["p2b_kvx"] = A.alloc(KVR, BF16)
        T["p2b_latT"] = A.alloc(KVR, BF16)
        T["p2b_kr"] = A.alloc(ROPE, BF16)
        emit_p2b(ctx)
        A.pop()

    if "p2c" in enabled or "p2d" in enabled or "p3" in enabled:
        A.push()
        T["iqT"] = A.alloc(IH * NQ, BF16)
        T["iw_sb"] = A.alloc(QT * IH, F32)
        SUM_PREF = sum(min(j + 1, NKB) for j in range(QT)) * KBLK
        T["mask"] = A.alloc(SUM_PREF, BF16)

    if "p2c" in enabled:
        A.push()
        T["qlatT"] = A.alloc(12 * NQ, BF16)
        for bb in range(2):
            T[f"p2c_w{bb}"] = A.alloc(512, BF16)
            T[f"p2c_xc{bb}"] = A.alloc(NQ, BF16)
            T[f"qstage{bb}"] = A.alloc(512, BF16)
            T[f"xnt_st{bb}"] = A.alloc(D, BF16)
        T["normq"] = A.alloc(12, F32)
        T["qcos"] = A.alloc(QT * 32, F32)
        T["qsin"] = A.alloc(QT * 32, F32)
        T["qblk"] = A.alloc(384, BF16)
        T["qropea"] = A.alloc(64, BF16)
        T["qropeb"] = A.alloc(64, BF16)
        emit_p2c(ctx)
        A.pop()

    if "p2d" in enabled:
        A.push()
        T["idx_acc"] = A.alloc(S, F32)
        T["p2d_cmp"] = A.alloc(S, BF16)
        for bb in range(2):
            T[f"p2d_r{bb}"] = A.alloc(KBLK, F32)
        for nm in ("p2d_th", "p2d_t2", "p2d_mx", "p2d_lo", "p2d_hi", "p2d_mid",
                   "p2d_cnt", "p2d_b", "p2d_d", "p2d_e"):
            T[nm] = A.alloc(1, F32)
        T["p2d_t"] = A.alloc(KBLK, F32)
        emit_p2d(ctx)
        A.pop()
        if "dump_mask" in enabled:
            cdm = dict(ctx.cnt)
            with nc.Block("dumpmask") as blk:
                @blk.sync
                def _(sy):
                    ctx.inc(sy.dma_start(out=ctx.scr["MASKDUMP"],
                                         in_=T["mask"][:]), "qS", 16)
                    ctx.wait(sy, "qS", ctx.cnt["qS"])

    if "p3" in enabled:
        A.push()
        for bb in range(2):
            T[f"xkn{bb}"] = A.alloc(S, BF16)
            T[f"xv{bb}"] = A.alloc(S, BF16)
            T[f"xqn{bb}"] = A.alloc(QT * 128, BF16)
            T[f"xqp{bb}"] = A.alloc(QT * 128, BF16)
            T[f"ot_sb{bb}"] = A.alloc(128, BF16)
        T["xkp"] = A.alloc(S, BF16)
        T["p3_sm"] = A.alloc(KBLK, F32)
        T["p3_p"] = A.alloc(KBLK, BF16)
        T["p3_pt"] = A.alloc(KBLK, BF16)
        for j in range(QT):
            T[f"p3_o{j}"] = A.alloc(VD, F32)
            T[f"p3_l{j}"] = A.alloc(1, F32)
            T[f"p3_m{j}"] = A.alloc(1, F32)
        T["p3_on"] = A.alloc(VD, BF16)
        for nm in ("p3_mkb", "p3_mnew", "p3_d", "p3_nm",
                   "p3_al", "p3_lkb", "p3_linv"):
            T[nm] = A.alloc(1, F32)
        emit_p3(ctx)
        A.pop()

    if "p2c" in enabled or "p2d" in enabled or "p3" in enabled:
        A.pop()

    if "p4" in enabled:
        A.push()
        T["p4_ow"] = A.alloc(32 * 1024, BF16)
        for bb in range(2):
            T[f"p4_ot{bb}"] = A.alloc(H * 128, BF16)
            T[f"p4_xq{bb}"] = A.alloc(1024, F32)
            T[f"p4_x1{bb}"] = A.alloc(1024, F32)
        emit_p4a(ctx)
        A.pop()
        ctx.snap("x1")

    def alloc_ln():
        for bb in range(2):
            T[f"p2c_xq{bb}"] = A.alloc(D, F32)
        T["p2c_xn"] = A.alloc(D, BF16)

    def alloc_p5():
        for bb in range(2):
            T[f"p5_w1{bb}"] = A.alloc(32 * 128, BF16)
            T[f"p5_w3{bb}"] = A.alloc(32 * 128, BF16)
            T[f"p5_act{bb}"] = A.alloc(NQ, BF16)
            T[f"p5_a{bb}"] = A.alloc(NQ, BF16)
            T[f"p5_w2{bb}"] = A.alloc(512, BF16)
            T[f"p5_xr{bb}"] = A.alloc(512, F32)
            T[f"p5_x2{bb}"] = A.alloc(512, F32)
        T["p5_sa"] = A.alloc(512, F32)

    if "p5" in enabled:
        A.push()
        T["xn2T"] = A.alloc(32 * NQ, BF16)
        alloc_ln()
        alloc_p5()
        emit_lnpass(ctx, "xn2T")
        emit_p5(ctx, "xn2T", final=False)
        A.pop()
        ctx.snap("x2")

    if "p6" in enabled:
        A.push()
        T["xn3T"] = A.alloc(32 * NQ, BF16)
        alloc_ln()
        T["p6_mk0"] = A.alloc(D, I32)
        for bb in range(2):
            T[f"p6_w{bb}"] = A.alloc(D, BF16)
            T[f"p6_st{bb}"] = A.alloc(512, BF16)
            T[f"p6_r{bb}"] = A.alloc(512, BF16)
            T[f"p5_xr{bb}"] = A.alloc(512, F32)
            T[f"p5_x2{bb}"] = A.alloc(512, F32)
        T["p6_cs"] = A.alloc(128, F32)
        T["p6_csb"] = A.alloc(128, BF16)
        T["p6_bmT"] = A.alloc(128, F32)
        T["p6_bme"] = A.alloc(32 * 128, BF16)
        emit_lnpass(ctx, "xn3T")
        emit_p6(ctx, "xn3T")
        A.pop()
        ctx.snap("x3")

    if "p7" in enabled:
        A.push()
        T["xn4T"] = A.alloc(32 * NQ, BF16)
        alloc_ln()
        alloc_p5()
        emit_lnpass(ctx, "xn4T")
        emit_p5(ctx, "xn4T", final=True)
        A.pop()

    tap_outs = {}
    for (tname, tshape, tdt, tsrc) in taps:
        tap_outs[tname] = nc.dram_tensor("tap_" + tname, tshape, tdt,
                                         kind="ExternalOutput").ap()
    with nc.Block("fin") as blk:
        if "p7" not in enabled:
            zt = A.alloc(D, F32)

            @blk.vector
            def _(v):
                ctx.inc(v.memset(zt[:], 0.0), "vV", 1)

        @blk.sync
        def _(sy):
            n = 0
            for (tname, tshape, tdt, tsrc) in taps:
                sy.dma_start(out=tap_outs[tname], in_=tsrc(ctx)).then_inc(
                    ctx.sems["qS"], 16)
                n += 1
            if "p7" in enabled:
                pass
            else:
                ctx.wait(sy, "vV", ctx.cnt["vV"])
                for i in range(QT):
                    sy.dma_start(out=out_y[i * 128:(i + 1) * 128, :],
                                 in_=zt[:]).then_inc(ctx.sems["qS"], 16)
                    n += 1
            sy.wait_ge(ctx.sems["qS"], ctx.cnt["qS"] + 16 * n)

    return nc, tap_outs


def make_in_maps(inputs):
    x = np.ascontiguousarray(inputs["L0_attn_a_hidden_states"])
    ident = np.eye(128, dtype=ml_dtypes.bfloat16)
    kpos = np.arange(S, dtype=np.float32)
    invf = (1.0 / THETA ** (np.arange(0, ROPE, 2, dtype=np.float32) / ROPE)) / (2 * np.pi)
    invf2pi = np.tile(invf[None, :], (128, 1)).astype(np.float32)
    ims = []
    for core in range(8):
        b, c = core // 4, core % 4
        im = {
            "xb": x[b],
            "xq": x[b, c::4, :],
            "q_a_w": inputs["L0_attn_a_q_a_w"],
            "q_a_norm_w": inputs["L0_attn_a_q_a_norm_w"],
            "q_b_w": inputs["L0_attn_a_q_b_w"],
            "kv_a_w": inputs["L0_attn_a_kv_a_w"],
            "kv_a_norm_w": inputs["L0_attn_a_kv_a_norm_w"],
            "kv_b_w": inputs["L0_attn_a_kv_b_w"],
            "o_w": inputs["L0_attn_a_o_w"],
            "idx_wq_w": inputs["L0_attn_a_idx_wq_w"],
            "idx_wk_w": inputs["L0_attn_a_idx_wk_w"],
            "idx_weights_w": inputs["L0_attn_a_idx_weights_w"],
            "ab_w": inputs["L0_attn_b_weight"],
            "ab_mask": inputs["L0_attn_b_mask"],
            "w1": inputs["L0_ffn_w1"], "s1": inputs["L0_ffn_s1"],
            "w2": inputs["L0_ffn_w2"], "s2": inputs["L0_ffn_s2"],
            "w3": inputs["L0_ffn_w3"], "s3": inputs["L0_ffn_s3"],
            "ident_c": ident,
            "qpos_c": np.arange(c, S, 4, dtype=np.float32),
            "kpos_c": kpos,
            "invf2pi_c": invf2pi,
        }
        ims.append({k: np.ascontiguousarray(v) for k, v in im.items()})
    return ims


ALL_PHASES = ("p1", "p2a", "p2b", "p2c", "p2d", "p3", "p4", "p5", "p6", "p7")


def kernel(**inputs):
    ims = make_in_maps(inputs)
    nc, _ = build_program(enabled=ALL_PHASES)
    res = run_bass_kernel_spmd(nc, ims, core_ids=list(range(8)))
    out = np.empty((B, S, D), np.float32)
    for core in range(8):
        b, c = core // 4, core % 4
        out[b, c::4, :] = np.asarray(res.results[core]["y"], np.float32)
    return out


# version 05
# speedup vs baseline: 1.3231x; 1.3231x over previous
import numpy as np
import ml_dtypes
import concourse.bass as bass
import concourse.mybir as mybir
from concourse.bass_utils import run_bass_kernel_spmd

F32 = mybir.dt.float32
BF16 = mybir.dt.bfloat16
I32 = mybir.dt.int32
U8 = mybir.dt.uint8
AF = mybir.ActivationFunctionType
ALU = mybir.AluOpType
AXX = mybir.AxisListType.X

B, S, D = 2, 4096, 4096
NQ = 1024
H, NOPE, ROPE, VD = 32, 128, 64, 128
QKD = NOPE + ROPE
QR, KVR = 1536, 512
IH, IDD, TOPK = 16, 128, 1024
HID = 11008
GRP = 64
EPS = 1e-5
THETA = 1e4
NEG_BIG = -1.0e30
NEG_CAUSAL = -1.0e38
ISQ_QKD = float(QKD) ** -0.5
IW_SCALE = float(IH) ** -0.5 * float(IDD) ** -0.5

QT = NQ // 128
TT = S // 128
KBLK = 512
NKB = S // KBLK
HB1 = HID // 128
DB2 = D // 128
NP1 = D // 128
NP2 = HID // 128


VOPS = {"tensor_copy", "tensor_tensor", "tensor_scalar", "tensor_add",
        "tensor_sub", "tensor_mul", "tensor_max", "tensor_relu",
        "tensor_single_scalar", "tensor_reduce", "reduce_sum", "reduce_max",
        "scalar_tensor_tensor", "tensor_tensor_scan", "memset", "memzero",
        "select", "copy_predicated", "reciprocal", "transpose",
        "stream_shuffle", "max", "max_index", "bn_stats", "bn_aggr"}


class VProxy:

    def __init__(self, eng, ctx):
        self._e = eng
        self._c = ctx

    def __getattr__(self, name):
        f = getattr(self._e, name)
        if not callable(f) or name not in VOPS:
            return f

        def wrap(*a, **kw):
            self._c.vsync(self._e)
            r = f(*a, **kw)
            self._c.last_v = r
            self._c.last_v_inc = None
            return r
        return wrap


class Arena:
    def __init__(self, tensor, nbytes):
        self.t = tensor
        self.nbytes = nbytes
        self.off = 0
        self.marks = []

    def alloc(self, cols, dtype):
        size = {F32: 4, I32: 4, BF16: 2, U8: 1}[dtype]
        nb = cols * size
        self.off = (self.off + 63) // 64 * 64
        assert self.off + nb <= self.nbytes, \
            f"SBUF arena overflow: {self.off}+{nb} > {self.nbytes}"
        v = self.t[:, self.off:self.off + nb].bitcast(dtype)
        self.off += nb
        return v

    def push(self):
        self.marks.append(self.off)

    def pop(self):
        self.off = self.marks.pop()


class Ctx:
    def __init__(self, nc):
        self.nc = nc
        self.sems = {}
        self.cnt = {}
        self.tiles = {}
        self.scr = {}
        self.inp = {}
        self.psum = []
        self.arena = None
        self.last_v = None
        self.last_v_inc = None

    def sem(self, name):
        s = self.nc.alloc_semaphore(name)
        self.sems[name] = s
        self.cnt[name] = 0
        return s

    def inc(self, instr, sem, n=1):
        instr.then_inc(self.sems[sem], n)
        self.cnt[sem] += n
        if instr is self.last_v:
            self.last_v_inc = (sem, self.cnt[sem])
        return instr

    def wait(self, eng, sem, val):
        eng.wait_ge(self.sems[sem], val)

    def vsync(self, v):
        if self.last_v is None:
            return
        if self.last_v_inc is not None:
            sem, val = self.last_v_inc
            self.last_v = None
            self.last_v_inc = None
            v.wait_ge(self.sems[sem], val)
            return
        self.cnt["vB"] += 1
        self.last_v.then_inc(self.sems["vB"], 1)
        self.last_v = None
        self.last_v_inc = None
        v.wait_ge(self.sems["vB"], self.cnt["vB"])


def emit_p1(ctx):
    nc = ctx.nc
    T = ctx.tiles
    jobs = []
    for rt in range(HB1):
        jobs.append(("w1", rt, 0, NP1, "W1T"))
    for rt in range(HB1):
        jobs.append(("w3", rt, 0, NP1, "W3T"))
    for rt in range(DB2):
        jobs.append(("w2", rt, 0, 43, "W2T"))
        jobs.append(("w2", rt, 43, 43, "W2T"))
    njobs = len(jobs)

    ngroups = [(j[3] + 3) // 4 for j in jobs]
    g_before = np.concatenate([[0], np.cumsum(ngroups)]).astype(int)
    tp_before = np.concatenate([[0], np.cumsum([j[3] for j in jobs])]).astype(int)
    st_w13 = [1 if j[4] != "W2T" else 0 for j in jobs]
    st_w2 = [0 if j[4] != "W2T" else 43 for j in jobs]
    st_slot_w13 = [[0], [0]]
    st_slot_w2 = [[0], [0]]
    for k in range(len(jobs)):
        for bb in range(2):
            st_slot_w13[bb].append(st_slot_w13[bb][-1] + (st_w13[k] if k % 2 == bb else 0))
            st_slot_w2[bb].append(st_slot_w2[bb][-1] + (st_w2[k] if k % 2 == bb else 0))

    c0 = dict(ctx.cnt)

    with nc.Block("p1") as blk:
        @blk.sync
        def _(sy):
            def w2_store(k):
                key, rt, po, npairs, out = jobs[k]
                if out != "W2T":
                    return
                bb = k % 2
                st = T[f"p1st{bb}"]
                ctx.wait(sy, "sS", c0["sS"] + int(g_before[k + 1]))
                w2t = ctx.scr["W2T"]
                for hc in range(npairs):
                    hg = po + hc
                    dwin, dsub = rt // 4, rt % 4
                    ctx.inc(sy.dma_start(
                        out=w2t[dwin, hg * 128:(hg + 1) * 128,
                                dsub * 128:(dsub + 1) * 128],
                        in_=st[:, hc * 128:(hc + 1) * 128]), f"qV{bb}", 16)

            for k, (key, rt, po, npairs, out) in enumerate(jobs):
                bb = k % 2
                if k >= 2:
                    ctx.wait(sy, "vV", c0["vV"] + 5 * (k - 1))
                ntot = 32 if out != "W2T" else 86
                pv = ctx.inp[key].rearrange("(r n) c -> r (n c)", n=ntot)
                sv = ctx.inp[key.replace("w", "s")].rearrange(
                    "(r g) -> r g", g=2 * ntot)
                ctx.inc(sy.dma_start(out=T[f"p1pk{bb}"][:, 0:npairs * 64],
                                     in_=pv[rt * 128:(rt + 1) * 128,
                                            po * 64:(po + npairs) * 64]),
                        f"qL{bb}", 16)
                ctx.inc(sy.dma_start(out=T[f"p1sc{bb}"][:, 0:2 * npairs],
                                     in_=sv[rt * 128:(rt + 1) * 128,
                                            2 * po:2 * (po + npairs)]),
                        f"qL{bb}", 16)
                if k >= 1:
                    w2_store(k - 1)
            w2_store(len(jobs) - 1)
            for q in ("qL0", "qL1", "qV0", "qV1"):
                ctx.wait(sy, q, ctx.cnt[q])

        @blk.vector
        def _(v):
            v = VProxy(v, ctx)
            for k, (key, rt, po, npairs, out) in enumerate(jobs):
                bb = k % 2
                pk, sc = T[f"p1pk{bb}"], T[f"p1sc{bb}"]
                s8, msb, lsb, wt = T[f"p1s8{bb}"], T[f"p1msb{bb}"], T[f"p1lsb{bb}"], T[f"p1wt{bb}"]
                nco = npairs * 64
                ctx.wait(v, f"qL{bb}", c0[f"qL{bb}"] + 32 * (k // 2 + 1))
                if k >= 2:
                    ctx.wait(v, "tP", c0["tP"] + int(tp_before[k - 1]))
                v.tensor_scalar(out=msb[:, 0:nco], in0=pk[:, 0:nco], scalar1=4,
                                scalar2=None, op0=ALU.arith_shift_right)
                v.tensor_scalar(out=lsb[:, 0:nco], in0=pk[:, 0:nco], scalar1=15,
                                scalar2=8, op0=ALU.bitwise_and, op1=ALU.bitwise_xor)
                ctx.inc(v.tensor_scalar(out=lsb[:, 0:nco], in0=lsb[:, 0:nco],
                                        scalar1=8, scalar2=None, op0=ALU.subtract),
                        "vV", 3)
                wt4 = wt[:, 0:2 * nco].rearrange("p (j two ci) -> p j two ci", two=2, ci=64)
                m4 = msb[:, 0:nco].rearrange("p (j one ci) -> p j one ci", one=1, ci=64)
                l4 = lsb[:, 0:nco].rearrange("p (j one ci) -> p j one ci", one=1, ci=64)
                sc3 = sc[:, 0:2 * npairs].rearrange("p (j two) -> p j two", two=2)
                se = sc3[:, :, 0:1].unsqueeze(3).to_broadcast([128, npairs, 1, 64])
                so = sc3[:, :, 1:2].unsqueeze(3).to_broadcast([128, npairs, 1, 64])
                ctx.inc(v.tensor_tensor(out=wt4[:, :, 0:1, :], in0=m4, in1=se,
                                        op=ALU.mult), "vV", 1)
                ctx.inc(v.tensor_tensor(out=wt4[:, :, 1:2, :], in0=l4, in1=so,
                                        op=ALU.mult), "vV", 1)

        @blk.tensor
        def _(t):
            ident = T["ident"]
            for k, (key, rt, po, npairs, out) in enumerate(jobs):
                bb = k % 2
                wt = T[f"p1wt{bb}"]
                ctx.wait(t, "vV", c0["vV"] + 5 * (k + 1))
                for g4 in range(ngroups[k]):
                    gidx = int(g_before[k]) + g4
                    bank = ctx.psum[gidx % 2]
                    if gidx >= 2:
                        ctx.wait(t, "sS", c0["sS"] + gidx - 1)
                    nin = min(4, npairs - g4 * 4)
                    for i in range(nin):
                        j = g4 * 4 + i
                        ctx.inc(t.matmul(bank[:, i * 128:(i + 1) * 128],
                                         wt[:, j * 128:(j + 1) * 128],
                                         ident[:], start=True, stop=True), "tP", 1)

        @blk.scalar
        def _(s):
            for k, (key, rt, po, npairs, out) in enumerate(jobs):
                bb = k % 2
                st = T[f"p1st{bb}"]
                for g4 in range(ngroups[k]):
                    gidx = int(g_before[k]) + g4
                    bank = ctx.psum[gidx % 2]
                    nin = min(4, npairs - g4 * 4)
                    ctx.wait(s, "tP", c0["tP"] + int(tp_before[k]) + g4 * 4 + nin)
                    if k >= 2 and g4 == 0:
                        ctx.wait(s, f"qT{bb}",
                                 c0[f"qT{bb}"] + 16 * st_slot_w13[bb][k - 1])
                        ctx.wait(s, f"qV{bb}",
                                 c0[f"qV{bb}"] + 16 * st_slot_w2[bb][k - 1])
                    ctx.inc(s.activation(out=st[:, g4 * 512:g4 * 512 + nin * 128],
                                         in_=bank[:, 0:nin * 128], func=AF.Copy), "sS", 1)

        @blk.gpsimd
        def _(g):
            for k, (key, rt, po, npairs, out) in enumerate(jobs):
                bb = k % 2
                st = T[f"p1st{bb}"]
                if out == "W2T":
                    continue
                ctx.wait(g, "sS", c0["sS"] + int(g_before[k + 1]))
                ctx.inc(g.dma_start(out=ctx.scr[out][rt],
                                    in_=st[:, 0:npairs * 128]), f"qT{bb}", 16)
            ctx.wait(g, "qT0", ctx.cnt["qT0"])
            ctx.wait(g, "qT1", ctx.cnt["qT1"])


    assert ctx.cnt["vV"] == c0["vV"] + 5 * njobs
    assert ctx.cnt["tP"] == c0["tP"] + int(tp_before[-1])
    assert ctx.cnt["sS"] == c0["sS"] + int(g_before[-1])
    assert (ctx.cnt["qT0"] - c0["qT0"]) + (ctx.cnt["qT1"] - c0["qT1"]) == 16 * sum(st_w13)
    assert (ctx.cnt["qV0"] - c0["qV0"]) + (ctx.cnt["qV1"] - c0["qV1"]) == 16 * sum(st_w2)


def rope_frac_vec(ctx, v, pos_col, pref):
    T = ctx.tiles
    u, ni, nf = T[pref + "_u"], T[pref + "_ni"], T[pref + "_nf"]

    def frac_turn(dst):
        ctx.vsync(v)
        v.tensor_copy(out=ni[:, 0:64], in_=u[:, 0:64])
        ctx.vsync(v)
        v.tensor_copy(out=nf[:, 0:64], in_=ni[:, 0:64])
        ctx.vsync(v)
        v.tensor_tensor(out=dst, in0=u[:, 0:32], in1=nf[:, 0:32], op=ALU.subtract)
        ctx.vsync(v)
        v.tensor_scalar(out=nf[:, 0:32], in0=dst, scalar1=0.5, scalar2=None,
                        op0=ALU.is_gt)
        ctx.inc(v.tensor_tensor(out=dst, in0=dst, in1=nf[:, 0:32], op=ALU.subtract),
                "vV", 1)

    v.tensor_scalar(out=u[:, 0:32], in0=T["invf2pi"][:], scalar1=pos_col,
                    scalar2=None, op0=ALU.mult)
    frac_turn(T[pref + "_fs"][:])
    v.tensor_scalar(out=u[:, 0:32], in0=u[:, 0:32], scalar1=0.25, scalar2=None,
                    op0=ALU.add)
    frac_turn(T[pref + "_fc"][:])


def rope_sin_scalar(ctx, s, pref):
    T = ctx.tiles
    TWO_PI = float(2.0 * np.pi)
    ctx.inc(s.activation(out=T[pref + "_sin"][:], in_=T[pref + "_fs"][:],
                         func=AF.Sin, scale=TWO_PI), "sS", 1)
    ctx.inc(s.activation(out=T[pref + "_cos"][:], in_=T[pref + "_fc"][:],
                         func=AF.Sin, scale=TWO_PI), "sS", 1)


def rope_apply_vec(ctx, v, src_ap, dst_ap, pref, nfreq):
    T = ctx.tiles
    x0 = src_ap.rearrange("p (i two) -> p i two", two=2)[:, :, 0:1]
    x1 = src_ap.rearrange("p (i two) -> p i two", two=2)[:, :, 1:2]
    dr = dst_ap.rearrange("p (i two) -> p i two", two=2)
    c3 = T[pref + "_cos"][:, 0:nfreq].unsqueeze(2)
    s3 = T[pref + "_sin"][:, 0:nfreq].unsqueeze(2)
    ta = T[pref + "_ta"][:, 0:nfreq].unsqueeze(2)
    tb = T[pref + "_tb"][:, 0:nfreq].unsqueeze(2)
    v.tensor_tensor(out=ta, in0=x0, in1=c3, op=ALU.mult)
    v.tensor_tensor(out=tb, in0=x1, in1=s3, op=ALU.mult)
    ctx.inc(v.tensor_tensor(out=dr[:, :, 0:1], in0=ta, in1=tb, op=ALU.subtract), "vV", 1)
    v.tensor_tensor(out=ta, in0=x1, in1=c3, op=ALU.mult)
    v.tensor_tensor(out=tb, in0=x0, in1=s3, op=ALU.mult)
    ctx.inc(v.tensor_tensor(out=dr[:, :, 1:2], in0=ta, in1=tb, op=ALU.add), "vV", 1)


def emit_p2a(ctx):
    nc = ctx.nc
    T = ctx.tiles
    c0 = dict(ctx.cnt)
    nD = D // 128

    with nc.Block("p2a") as blk:
        @blk.gpsimd
        def _(g):
            kvw = ctx.inp["kv_a_w"].rearrange("(c p) f -> c p f", p=128)
            ikw = ctx.inp["idx_wk_w"].rearrange("(c p) f -> c p f", p=128)
            for c in range(nD):
                ctx.inc(g.dma_start(out=T["p2a_kvw"][:, c * 512:(c + 1) * 512],
                                    in_=kvw[c][:, 0:512]), "qG", 16)
                ctx.inc(g.dma_start(out=T["p2a_pik"][:, c * 192:c * 192 + 64],
                                    in_=kvw[c][:, 512:576]), "qG", 16)
                ctx.inc(g.dma_start(out=T["p2a_pik"][:, c * 192 + 64:(c + 1) * 192],
                                    in_=ikw[c]), "qG", 16)
            ctx.wait(g, "qG", ctx.cnt["qG"])

        @blk.sync
        def _(sy):
            for i in range(TT):
                bb = i % 2
                if i >= 2:
                    ctx.wait(sy, "vV", c0["vV"] + 5 * (i - 1))
                    ctx.wait(sy, "sS", c0["sS"] + 11 * (i - 1))
                ctx.inc(sy.dma_start(out=T[f"p2a_xb{bb}"][:],
                                     in_=ctx.inp["xb"][i * 128:(i + 1) * 128, :]),
                        f"qL{bb}", 16)
                ctx.wait(sy, "vV", c0["vV"] + 5 * (i + 1))
                ctx.inc(sy.dma_start(out=ctx.scr["CKV"][i * 128:(i + 1) * 128, :],
                                     in_=T[f"p2a_ckv{bb}"][:]), f"qU{bb}", 16)
            for q in ("qL0", "qL1", "qU0", "qU1"):
                ctx.wait(sy, q, ctx.cnt[q])

        @blk.vector
        def _(v):
            v = VProxy(v, ctx)
            for i in range(TT):
                bb = i % 2
                xb = T[f"p2a_xb{bb}"]
                ctx.wait(v, f"qL{bb}", c0[f"qL{bb}"] + 16 * (i // 2 + 1))
                v.reduce_sum(out=T["st_sum"][:], in_=xb[:], axis=AXX)
                v.tensor_scalar(out=T["st_mean"][:], in0=T["st_sum"][:],
                                scalar1=1.0 / D, scalar2=None, op0=ALU.mult)
                ctx.wait(v, "sS", c0["sS"] + 11 * i + 1)
                v.tensor_scalar(out=T["st_ms"][:], in0=T["st_ssq"][:],
                                scalar1=1.0 / D, scalar2=None, op0=ALU.mult)
                v.tensor_tensor(out=T["st_m2"][:], in0=T["st_mean"][:],
                                in1=T["st_mean"][:], op=ALU.mult)
                ctx.inc(v.tensor_tensor(out=T["st_var"][:], in0=T["st_ms"][:],
                                        in1=T["st_m2"][:], op=ALU.subtract), "vV", 1)
                ctx.wait(v, "sS", c0["sS"] + 11 * i + 2)
                v.reciprocal(out=T["st_rstd"][:], in_=T["st_std"][:])
                ctx.inc(v.tensor_scalar(out=T[f"p2a_xn{bb}"][:], in0=xb[:],
                                        scalar1=T["st_mean"][:], scalar2=T["st_rstd"][:],
                                        op0=ALU.subtract, op1=ALU.mult), "vV", 1)
                ctx.wait(v, "tP", c0["tP"] + 34 * i + 33)
                if i >= 2:
                    ctx.wait(v, f"qU{bb}", c0[f"qU{bb}"] + 16 * (i // 2))
                ctx.inc(v.tensor_copy(out=T[f"p2a_ckv{bb}"][:, 0:512],
                                      in_=ctx.psum[0][:]), "vV", 1)
                ctx.inc(v.tensor_copy(out=T[f"p2a_ckv{bb}"][:, 512:576],
                                      in_=ctx.psum[1][:, 0:64]), "vV", 1)
                ctx.inc(v.tensor_copy(out=T["p2a_iktmp"][:],
                                      in_=ctx.psum[1][:, 64:192]), "vV", 1)

        @blk.scalar
        def _(s):
            for i in range(TT):
                bb = i % 2
                xb = T[f"p2a_xb{bb}"]
                ctx.wait(s, f"qL{bb}", c0[f"qL{bb}"] + 16 * (i // 2 + 1))
                ctx.inc(s.activation(out=T["st_xsq"][:], in_=xb[:], func=AF.Square,
                                     accum_out=T["st_ssq"][:]), "sS", 1)
                ctx.wait(s, "vV", c0["vV"] + 5 * i + 1)
                ctx.inc(s.activation(out=T["st_std"][:], in_=T["st_var"][:],
                                     func=AF.Sqrt, bias=T["eps"][:]), "sS", 1)
                for gq in range(8):
                    bank = ctx.psum[2 + (gq % 2)]
                    ctx.wait(s, "tP", c0["tP"] + 34 * i + 4 * (gq + 1))
                    ctx.inc(s.activation(out=T[f"p2a_xnt{bb}"][:, gq * 512:(gq + 1) * 512],
                                         in_=bank[:], func=AF.Copy), "sS", 1)
                ctx.wait(s, "tP", c0["tP"] + 34 * (i + 1))
                ctx.inc(s.activation(out=T["ikT"][:, i * 128:(i + 1) * 128],
                                     in_=ctx.psum[4][:, 0:128], func=AF.Copy), "sS", 1)

        @blk.tensor
        def _(t):
            ident = T["ident"]
            ctx.wait(t, "qG", c0["qG"] + 16 * 3 * nD)
            for i in range(TT):
                bb = i % 2
                ctx.wait(t, "vV", c0["vV"] + 5 * i + 2)
                for gq in range(8):
                    bank = ctx.psum[2 + (gq % 2)]
                    gidx = i * 8 + gq
                    if gidx >= 2:
                        prev = gidx - 2
                        ctx.wait(t, "sS", c0["sS"] + 11 * (prev // 8) + 2 + (prev % 8) + 1)
                    for u4 in range(4):
                        cch = gq * 4 + u4
                        ctx.inc(t.matmul(bank[:, u4 * 128:(u4 + 1) * 128],
                                         T[f"p2a_xn{bb}"][:, cch * 128:(cch + 1) * 128],
                                         ident[:], start=True, stop=True), "tP", 1)
                if i >= 1:
                    ctx.wait(t, "vV", c0["vV"] + 5 * i)
                for c in range(nD):
                    ctx.wait(t, "sS", c0["sS"] + 11 * i + 2 + c // 4 + 1)
                    lhs = T[f"p2a_xnt{bb}"][:, c * 128:(c + 1) * 128]
                    t.matmul(ctx.psum[0][:], lhs,
                             T["p2a_kvw"][:, c * 512:(c + 1) * 512],
                             start=(c == 0), stop=(c == nD - 1), skip_group_check=True)
                    mm = t.matmul(ctx.psum[1][:, 0:192], lhs,
                                  T["p2a_pik"][:, c * 192:(c + 1) * 192],
                                  start=(c == 0), stop=(c == nD - 1),
                                  skip_group_check=True)
                ctx.inc(mm, "tP", 1)
                ctx.wait(t, "vV", c0["vV"] + 5 * (i + 1))
                ctx.inc(t.matmul(ctx.psum[4][:, 0:128], T["p2a_iktmp"][:], ident[:],
                                 start=True, stop=True), "tP", 1)

    assert ctx.cnt["tP"] == c0["tP"] + 34 * TT
    assert ctx.cnt["vV"] == c0["vV"] + 5 * TT
    assert ctx.cnt["sS"] == c0["sS"] + 11 * TT


def emit_p2b(ctx):
    nc = ctx.nc
    T = ctx.tiles
    c0 = dict(ctx.cnt)
    NKV = 16
    TP_T = 53

    with nc.Block("p2b") as blk:
        @blk.gpsimd
        def _(g):
            kbw = ctx.inp["kv_b_w"].rearrange("(c p) f -> c p f", p=128)
            for c in range(4):
                ctx.inc(g.dma_start(out=T["p2b_kbw"][:, c * 8192:(c + 1) * 8192],
                                    in_=kbw[c]), "qG", 16)
            ctx.inc(g.dma_start(out=T["p2b_normw"][:],
                                in_=ctx.inp["kv_a_norm_w"].rearrange("(c p) -> p c", p=128),
                                allow_slow_non_contiguous=True), "qG", 16)
            ctx.wait(g, "qG", ctx.cnt["qG"])

        @blk.vector
        def _(v):
            v = VProxy(v, ctx)
            ctx.wait(v, "qG", c0["qG"] + 16 * 5)
            for c in range(4):
                ctx.inc(v.tensor_scalar(out=T["p2b_kbw"][:, c * 8192:(c + 1) * 8192],
                                        in0=T["p2b_kbw"][:, c * 8192:(c + 1) * 8192],
                                        scalar1=T["p2b_normw"][:, c:c + 1], scalar2=None,
                                        op0=ALU.mult), "vV", 1)
            P = c0["vV"] + 4
            for i in range(TT):
                bb = i % 2
                ckv = T[f"p2b_ckv{bb}"]
                ctx.wait(v, f"qL{bb}", c0[f"qL{bb}"] + 16 * (i // 2 + 1))
                ctx.wait(v, "sS", c0["sS"] + 30 * i + 1)
                ctx.inc(v.tensor_scalar(out=T["st_ms"][:], in0=T["st_ssq"][:],
                                        scalar1=1.0 / KVR, scalar2=None,
                                        op0=ALU.mult), "vV", 1)
                ctx.wait(v, "sS", c0["sS"] + 30 * i + 2)
                v.reciprocal(out=T["st_rstd"][:], in_=T["st_std"][:])
                ctx.vsync(v)
                if i >= 2:
                    ctx.wait(v, "tP", c0["tP"] + TP_T * (i - 2) + 4)
                ctx.inc(v.tensor_scalar(out=T["p2b_kvx"][:], in0=ckv[:, 0:512],
                                        scalar1=T["st_rstd"][:], scalar2=None,
                                        op0=ALU.mult), "vV", 1)
                rope_frac_vec(ctx, v, T["kpos"][:, i:i + 1], "rp")
                ctx.wait(v, "sS", c0["sS"] + 30 * i + 4)
                if i >= 1:
                    ctx.wait(v, "tP", c0["tP"] + TP_T * i)
                rope_apply_vec(ctx, v, ckv[:, 512:576], T["p2b_kr"][:], "rp", 32)

        @blk.scalar
        def _(s):
            for i in range(TT):
                bb = i % 2
                ckv = T[f"p2b_ckv{bb}"]
                P = c0["vV"] + 4
                ctx.wait(s, f"qL{bb}", c0[f"qL{bb}"] + 16 * (i // 2 + 1))
                ctx.inc(s.activation(out=T["st_xsq"][:, 0:KVR], in_=ckv[:, 0:512],
                                     func=AF.Square, accum_out=T["st_ssq"][:]), "sS", 1)
                ctx.wait(s, "vV", P + 6 * i + 1)
                ctx.inc(s.activation(out=T["st_std"][:], in_=T["st_ms"][:],
                                     func=AF.Sqrt, bias=T["eps"][:]), "sS", 1)
                ctx.wait(s, "vV", P + 6 * i + 4)
                rope_sin_scalar(ctx, s, "rp")
                ctx.wait(s, "tP", c0["tP"] + TP_T * i + 4)
                ctx.inc(s.activation(out=T["p2b_latT"][:], in_=ctx.psum[5][:],
                                     func=AF.Copy), "sS", 1)
                if i >= 2:
                    ctx.wait(s, f"qU{bb}", c0[f"qU{bb}"] + 16 * 65 * (i // 2))
                kvsb = T[f"p2b_kv{bb}"]
                for n in range(NKV):
                    bank = ctx.psum[n % 2]
                    ctx.wait(s, "tP", c0["tP"] + TP_T * i + 4 + (n + 1))
                    ctx.inc(s.activation(out=kvsb[:, n * 512:(n + 1) * 512],
                                         in_=bank[:], func=AF.Copy), "sS", 1)
                for gq in range(8):
                    bank = ctx.psum[2 + (gq % 2)]
                    ctx.wait(s, "tP", c0["tP"] + TP_T * i + 20 + 4 * (gq + 1))
                    ctx.inc(s.activation(out=T[f"p2b_knt{bb}"][:, gq * 512:(gq + 1) * 512],
                                         in_=bank[:], func=AF.Copy), "sS", 1)
                ctx.wait(s, "tP", c0["tP"] + TP_T * (i + 1))
                ctx.inc(s.activation(out=T[f"p2b_kpt{bb}"][0:64, 0:128],
                                     in_=ctx.psum[4][0:64, 0:128], func=AF.Copy), "sS", 1)

        @blk.tensor
        def _(t):
            ident = T["ident"]
            ctx.wait(t, "vV", c0["vV"] + 4)
            P = c0["vV"] + 4
            for i in range(TT):
                bb = i % 2
                ctx.wait(t, "vV", P + 6 * i + 2)
                if i >= 1:
                    ctx.wait(t, "sS", c0["sS"] + 30 * (i - 1) + 5)
                for u4 in range(4):
                    ctx.inc(t.matmul(ctx.psum[5][:, u4 * 128:(u4 + 1) * 128],
                                     T["p2b_kvx"][:, u4 * 128:(u4 + 1) * 128],
                                     ident[:], start=True, stop=True), "tP", 1)
                ctx.wait(t, "sS", c0["sS"] + 30 * i + 5)
                for n in range(NKV):
                    bank = ctx.psum[n % 2]
                    if n >= 2:
                        ctx.wait(t, "sS", c0["sS"] + 30 * i + 5 + (n - 1))
                    for c in range(4):
                        mm = t.matmul(bank[:],
                                      T["p2b_latT"][:, c * 128:(c + 1) * 128],
                                      T["p2b_kbw"][:, c * 8192 + n * 512:c * 8192 + (n + 1) * 512],
                                      start=(c == 0), stop=(c == 3))
                    ctx.inc(mm, "tP", 1)
                for gq in range(8):
                    bank = ctx.psum[2 + (gq % 2)]
                    gidx = i * 8 + gq
                    if gidx >= 2:
                        prev = gidx - 2
                        ctx.wait(t, "sS", c0["sS"] + 30 * (prev // 8) + 21 + (prev % 8) + 1)
                    for u4 in range(4):
                        hh = gq * 4 + u4
                        ctx.wait(t, "sS", c0["sS"] + 30 * i + 5 + hh // 2 + 1)
                        ctx.inc(t.matmul(bank[:, u4 * 128:(u4 + 1) * 128],
                                         T[f"p2b_kv{bb}"][:, 256 * hh:256 * hh + 128],
                                         ident[:], start=True, stop=True), "tP", 1)
                ctx.wait(t, "vV", P + 6 * (i + 1))
                if i >= 1:
                    ctx.wait(t, "sS", c0["sS"] + 30 * i)
                ctx.inc(t.matmul(ctx.psum[4][0:64, 0:128], T["p2b_kr"][:], ident[:],
                                 start=True, stop=True), "tP", 1)

        @blk.sync
        def _(sy):
            for i in range(TT):
                bb = i % 2
                if i >= 2:
                    ctx.wait(sy, "sS", c0["sS"] + 30 * (i - 1))
                    ctx.wait(sy, "vV", c0["vV"] + 4 + 6 * (i - 1))
                ctx.inc(sy.dma_start(out=T[f"p2b_ckv{bb}"][:],
                                     in_=ctx.scr["CKV"][i * 128:(i + 1) * 128, :]),
                        f"qL{bb}", 16)
                ctx.wait(sy, "sS", c0["sS"] + 30 * i + 21)
                for hh in range(H):
                    ctx.inc(sy.dma_start(
                        out=ctx.scr["V"][hh, i * 128:(i + 1) * 128, :],
                        in_=T[f"p2b_kv{bb}"][:, 256 * hh + 128:256 * (hh + 1)]),
                        f"qU{bb}", 16)
                ctx.wait(sy, "sS", c0["sS"] + 30 * i + 29)
                for hh in range(H):
                    ctx.inc(sy.dma_start(
                        out=ctx.scr["KnT"][hh, :, i * 128:(i + 1) * 128],
                        in_=T[f"p2b_knt{bb}"][:, hh * 128:(hh + 1) * 128]),
                        f"qU{bb}", 16)
                ctx.wait(sy, "sS", c0["sS"] + 30 * (i + 1))
                ctx.inc(sy.dma_start(
                    out=ctx.scr["KpT"][:, i * 128:(i + 1) * 128],
                    in_=T[f"p2b_kpt{bb}"][0:64, 0:128]), f"qU{bb}", 16)
            for q in ("qL0", "qL1", "qU0", "qU1"):
                ctx.wait(sy, q, ctx.cnt[q])

    assert ctx.cnt["tP"] == c0["tP"] + TP_T * TT
    assert ctx.cnt["vV"] == c0["vV"] + 4 + 6 * TT
    assert ctx.cnt["sS"] == c0["sS"] + 30 * TT


def rope_apply2(ctx, v, src_ap, dst_ap, cos_ap, sin_ap):
    T = ctx.tiles
    nf = 32
    x0 = src_ap.rearrange("p (i two) -> p i two", two=2)[:, :, 0:1]
    x1 = src_ap.rearrange("p (i two) -> p i two", two=2)[:, :, 1:2]
    dr = dst_ap.rearrange("p (i two) -> p i two", two=2)
    c3 = cos_ap.unsqueeze(2)
    s3 = sin_ap.unsqueeze(2)
    ta = T["rp_ta"][:, 0:nf].unsqueeze(2)
    tb = T["rp_tb"][:, 0:nf].unsqueeze(2)
    v.tensor_tensor(out=ta, in0=x0, in1=c3, op=ALU.mult)
    v.tensor_tensor(out=tb, in0=x1, in1=s3, op=ALU.mult)
    ctx.inc(v.tensor_tensor(out=dr[:, :, 0:1], in0=ta, in1=tb, op=ALU.subtract), "vV", 1)
    v.tensor_tensor(out=ta, in0=x1, in1=c3, op=ALU.mult)
    v.tensor_tensor(out=tb, in0=x0, in1=s3, op=ALU.mult)
    ctx.inc(v.tensor_tensor(out=dr[:, :, 1:2], in0=ta, in1=tb, op=ALU.add), "vV", 1)


def emit_p2c(ctx):
    nc = ctx.nc
    T = ctx.tiles
    ident = T["ident"]

    A = ctx.arena
    A.push()
    for bb in range(2):
        T[f"p2c_xq{bb}"] = A.alloc(D, F32)
    T["p2c_xn"] = A.alloc(D, BF16)
    c0 = dict(ctx.cnt)
    with nc.Block("p2c0") as blk:
        @blk.sync
        def _(sy):
            for j in range(QT):
                bb = j % 2
                if j >= 2:
                    ctx.wait(sy, "vV", c0["vV"] + 2 * (j - 1))
                    ctx.wait(sy, "sS", c0["sS"] + 10 * (j - 1))
                ctx.inc(sy.dma_start(out=T[f"p2c_xq{bb}"][:],
                                     in_=ctx.inp["xq"][j * 128:(j + 1) * 128, :]),
                        f"qL{bb}", 16)
                ctx.wait(sy, "sS", c0["sS"] + 10 * (j + 1))
                for cch in range(32):
                    ctx.inc(sy.dma_start(out=ctx.scr["XNQT"][cch, j],
                                         in_=T[f"xnt_st{bb}"][:, cch * 128:(cch + 1) * 128]),
                            f"qU{bb}", 16)
            for q in ("qL0", "qL1", "qU0", "qU1"):
                ctx.wait(sy, q, ctx.cnt[q])

        @blk.vector
        def _(v):
            v = VProxy(v, ctx)
            for j in range(QT):
                bb = j % 2
                xq = T[f"p2c_xq{bb}"]
                ctx.wait(v, f"qL{bb}", c0[f"qL{bb}"] + 16 * (j // 2 + 1))
                v.reduce_sum(out=T["st_sum"][:], in_=xq[:], axis=AXX)
                ctx.vsync(v)
                v.tensor_scalar(out=T["st_mean"][:], in0=T["st_sum"][:],
                                scalar1=1.0 / D, scalar2=None, op0=ALU.mult)
                ctx.wait(v, "sS", c0["sS"] + 10 * j + 1)
                v.tensor_scalar(out=T["st_ms"][:], in0=T["st_ssq"][:],
                                scalar1=1.0 / D, scalar2=None, op0=ALU.mult)
                v.tensor_tensor(out=T["st_m2"][:], in0=T["st_mean"][:],
                                in1=T["st_mean"][:], op=ALU.mult)
                ctx.inc(v.tensor_tensor(out=T["st_var"][:], in0=T["st_ms"][:],
                                        in1=T["st_m2"][:], op=ALU.subtract), "vV", 1)
                ctx.wait(v, "sS", c0["sS"] + 10 * j + 2)
                v.reciprocal(out=T["st_rstd"][:], in_=T["st_std"][:])
                if j >= 1:
                    ctx.wait(v, "tP", c0["tP"] + 32 * j)
                ctx.inc(v.tensor_scalar(out=T["p2c_xn"][:], in0=xq[:],
                                        scalar1=T["st_mean"][:], scalar2=T["st_rstd"][:],
                                        op0=ALU.subtract, op1=ALU.mult), "vV", 1)

        @blk.scalar
        def _(s):
            for j in range(QT):
                bb = j % 2
                xq = T[f"p2c_xq{bb}"]
                ctx.wait(s, f"qL{bb}", c0[f"qL{bb}"] + 16 * (j // 2 + 1))
                ctx.inc(s.activation(out=T["st_xsq"][:], in_=xq[:], func=AF.Square,
                                     accum_out=T["st_ssq"][:]), "sS", 1)
                ctx.wait(s, "vV", c0["vV"] + 2 * j + 1)
                ctx.inc(s.activation(out=T["st_std"][:], in_=T["st_var"][:],
                                     func=AF.Sqrt, bias=T["eps"][:]), "sS", 1)
                for gq in range(8):
                    bank = ctx.psum[6 + (gq % 2)]
                    ctx.wait(s, "tP", c0["tP"] + 32 * j + 4 * (gq + 1))
                    if j >= 2 and gq == 0:
                        ctx.wait(s, f"qU{j % 2}",
                                 c0[f"qU{j % 2}"] + 16 * 32 * (j // 2))
                    ctx.inc(s.activation(
                        out=T[f"xnt_st{j % 2}"][:, gq * 512:(gq + 1) * 512],
                        in_=bank[:], func=AF.Copy), "sS", 1)

        @blk.tensor
        def _(t):
            for j in range(QT):
                ctx.wait(t, "vV", c0["vV"] + 2 * j + 2)
                for gq in range(8):
                    bank = ctx.psum[6 + (gq % 2)]
                    gidx = j * 8 + gq
                    if gidx >= 2:
                        prev = gidx - 2
                        ctx.wait(t, "sS", c0["sS"] + 10 * (prev // 8) + 2 + (prev % 8) + 1)
                    for u4 in range(4):
                        cch = gq * 4 + u4
                        ctx.inc(t.matmul(bank[:, u4 * 128:(u4 + 1) * 128],
                                         T["p2c_xn"][:, cch * 128:(cch + 1) * 128],
                                         ident[:], start=True, stop=True), "tP", 1)

    assert ctx.cnt["tP"] == c0["tP"] + 32 * QT
    assert ctx.cnt["vV"] == c0["vV"] + 2 * QT
    assert ctx.cnt["sS"] == c0["sS"] + 10 * QT
    A.pop()

    def streamed_proj(tag, w_key, ncols, coltile, evac_cb):
        c1 = dict(ctx.cnt)
        ntiles = ncols // coltile
        nch = 32
        with nc.Block("p2c_" + tag) as blk:
            @blk.sync
            def _(sy):
                for n in range(ntiles):
                    for c in range(nch):
                        k = n * nch + c
                        bb = k % 2
                        if k >= 2:
                            ctx.wait(sy, "tP", c1["tP"] + (k - 1))
                        ctx.inc(sy.dma_start(
                            out=T[f"p2c_xc{bb}"][:].rearrange("p (j q) -> p j q", q=128),
                            in_=ctx.scr["XNQT"][c].rearrange("j p q -> p j q")),
                            f"qU{bb}", 16)
                for q in ("qU0", "qU1"):
                    ctx.wait(sy, q, ctx.cnt[q])

            @blk.gpsimd
            def _(g):
                for n in range(ntiles):
                    for c in range(nch):
                        k = n * nch + c
                        bb = k % 2
                        if k >= 2:
                            ctx.wait(g, "tP", c1["tP"] + (k - 1))
                        wv = ctx.inp[w_key].rearrange("(c p) f -> c p f", p=128)
                        ctx.inc(g.dma_start(
                            out=T[f"p2c_w{bb}"][:, 0:coltile],
                            in_=wv[c][:, n * coltile:(n + 1) * coltile]),
                            f"gL{bb}", 16)
                for q in ("gL0", "gL1"):
                    ctx.wait(g, q, ctx.cnt[q])

            @blk.tensor
            def _(t):
                for n in range(ntiles):
                    for c in range(nch):
                        k = n * nch + c
                        bb = k % 2
                        ctx.wait(t, f"gL{bb}", c1[f"gL{bb}"] + 16 * (k // 2 + 1))
                        ctx.wait(t, f"qU{bb}", c1[f"qU{bb}"] + 16 * (k // 2 + 1))
                        if c == 0 and n >= 1:
                            ctx.wait(t, "sS", c1["sS"] + 8 * n)
                        for j in range(QT):
                            mm = t.matmul(ctx.psum[j][:, 0:coltile],
                                          T[f"p2c_xc{bb}"][:, j * 128:(j + 1) * 128],
                                          T[f"p2c_w{bb}"][:, 0:coltile],
                                          start=(c == 0), stop=(c == nch - 1))
                        ctx.inc(mm, "tP", 1)

            @blk.scalar
            def _(s):
                for n in range(ntiles):
                    ctx.wait(s, "tP", c1["tP"] + nch * (n + 1))
                    for j in range(QT):
                        evac_cb(s, n, j, ctx.psum[j])

        assert ctx.cnt["tP"] == c1["tP"] + nch * ntiles
        assert ctx.cnt["sS"] == c1["sS"] + 8 * ntiles

    A.push()
    T["iq_all"] = A.alloc(QT * 2048, BF16)
    T["iww"] = A.alloc(32 * IH, BF16)

    def iq_evac(s, n, j, bank):
        ctx.inc(s.activation(
            out=T["iq_all"][:, j * 2048 + n * 512:j * 2048 + (n + 1) * 512],
            in_=bank[:], func=AF.Copy), "sS", 1)

    streamed_proj("iq", "idx_wq_w", IH * IDD, 512, iq_evac)

    c4 = dict(ctx.cnt)
    with nc.Block("p2c_iqT") as blk:
        @blk.tensor
        def _(t):
            for j in range(QT):
                for f in range(IH):
                    bank = ctx.psum[f % 2]
                    gidx = j * IH + f
                    if gidx >= 2:
                        prev = gidx - 2
                        ctx.wait(t, "sS", c4["sS"] + prev + 1)
                    ctx.inc(t.matmul(bank[:, 0:128],
                                     T["iq_all"][:, j * 2048 + f * 128:
                                                 j * 2048 + (f + 1) * 128],
                                     ident[:], start=True, stop=True), "tP", 1)

        @blk.scalar
        def _(s):
            for j in range(QT):
                for f in range(IH):
                    bank = ctx.psum[f % 2]
                    ctx.wait(s, "tP", c4["tP"] + j * IH + f + 1)
                    ctx.inc(s.activation(
                        out=T["iqT"][:, f * NQ + j * 128:f * NQ + (j + 1) * 128],
                        in_=bank[:, 0:128], func=AF.Copy), "sS", 1)

    assert ctx.cnt["tP"] == c4["tP"] + IH * QT
    assert ctx.cnt["sS"] == c4["sS"] + IH * QT

    c5 = dict(ctx.cnt)
    with nc.Block("p2c_iw") as blk:
        @blk.gpsimd
        def _(g):
            iww = ctx.inp["idx_weights_w"].rearrange("(c p) f -> c p f", p=128)
            for c in range(32):
                ctx.inc(g.dma_start(out=T["iww"][:, c * IH:(c + 1) * IH], in_=iww[c]),
                        "qG", 16)
            ctx.wait(g, "qG", ctx.cnt["qG"])

        @blk.sync
        def _(sy):
            for c in range(32):
                bb = c % 2
                if c >= 2:
                    ctx.wait(sy, "tP", c5["tP"] + (c - 1) * QT)
                ctx.inc(sy.dma_start(
                    out=T[f"p2c_xc{bb}"][:].rearrange("p (j q) -> p j q", q=128),
                    in_=ctx.scr["XNQT"][c].rearrange("j p q -> p j q")),
                    f"qU{bb}", 16)
            for q in ("qU0", "qU1"):
                ctx.wait(sy, q, ctx.cnt[q])

        @blk.tensor
        def _(t):
            ctx.wait(t, "qG", c5["qG"] + 16 * 32)
            for c in range(32):
                bb = c % 2
                ctx.wait(t, f"qU{bb}", c5[f"qU{bb}"] + 16 * (c // 2 + 1))
                for j in range(QT):
                    bank = ctx.psum[j]
                    if c == 0 and j >= 0:
                        pass
                    mm = t.matmul(bank[:, 0:IH],
                                  T[f"p2c_xc{bb}"][:, j * 128:(j + 1) * 128],
                                  T["iww"][:, c * IH:(c + 1) * IH],
                                  start=(c == 0), stop=(c == 31))
                    ctx.inc(mm, "tP", 1)

        @blk.scalar
        def _(s):
            ctx.wait(s, "tP", c5["tP"] + 32 * QT)
            for j in range(QT):
                bank = ctx.psum[j]
                ctx.inc(s.activation(out=T["iw_sb"][:, j * IH:(j + 1) * IH],
                                     in_=bank[:, 0:IH], func=AF.Copy,
                                     scale=IW_SCALE), "sS", 1)

    assert ctx.cnt["tP"] == c5["tP"] + 32 * QT
    assert ctx.cnt["sS"] == c5["sS"] + QT
    A.pop()

    A.push()
    T["qlat"] = A.alloc(QT * QR, BF16)
    T["qlat_n"] = A.alloc(QR, BF16)
    def qa_evac(s, n, j, bank):
        ctx.inc(s.activation(out=T["qlat"][:, j * QR + n * 512:j * QR + (n + 1) * 512],
                             in_=bank[:], func=AF.Copy), "sS", 1)

    streamed_proj("qa", "q_a_w", QR, 512, qa_evac)

    c2 = dict(ctx.cnt)
    with nc.Block("p2c1b") as blk:
        @blk.scalar
        def _(s):
            for j in range(QT):
                if j >= 1:
                    ctx.wait(s, "vV", c2["vV"] + 2 * j)
                ctx.inc(s.activation(out=T["st_xsq"][:, 0:QR],
                                     in_=T["qlat"][:, j * QR:(j + 1) * QR],
                                     func=AF.Square, accum_out=T["st_ssq"][:]), "sS", 1)
                ctx.wait(s, "vV", c2["vV"] + 2 * j + 1)
                ctx.inc(s.activation(out=T["st_std"][:], in_=T["st_ms"][:],
                                     func=AF.Sqrt, bias=T["eps"][:]), "sS", 1)
                for gq in range(3):
                    bank = ctx.psum[gq % 2]
                    ctx.wait(s, "tP", c2["tP"] + 12 * j + 4 * (gq + 1))
                    ctx.inc(s.activation(
                        out=T["qlatT"].rearrange("p (c q) -> p c q", q=NQ)[
                            :, gq * 4:(gq + 1) * 4, j * 128:(j + 1) * 128],
                        in_=bank[:].rearrange("p (c q) -> p c q", q=128),
                        func=AF.Copy), "sS", 1)

        @blk.vector
        def _(v):
            v = VProxy(v, ctx)
            for j in range(QT):
                ctx.wait(v, "sS", c2["sS"] + 5 * j + 1)
                ctx.inc(v.tensor_scalar(out=T["st_ms"][:], in0=T["st_ssq"][:],
                                        scalar1=1.0 / QR, scalar2=None,
                                        op0=ALU.mult), "vV", 1)
                ctx.wait(v, "sS", c2["sS"] + 5 * j + 2)
                v.reciprocal(out=T["st_rstd"][:], in_=T["st_std"][:])
                if j >= 1:
                    ctx.wait(v, "tP", c2["tP"] + 12 * j)
                ctx.inc(v.tensor_scalar(out=T["qlat_n"][:],
                                        in0=T["qlat"][:, j * QR:(j + 1) * QR],
                                        scalar1=T["st_rstd"][:], scalar2=None,
                                        op0=ALU.mult), "vV", 1)

        @blk.tensor
        def _(t):
            for j in range(QT):
                ctx.wait(t, "vV", c2["vV"] + 2 * j + 2)
                for gq in range(3):
                    bank = ctx.psum[gq % 2]
                    gidx = j * 3 + gq
                    if gidx >= 2:
                        prev = gidx - 2
                        ctx.wait(t, "sS", c2["sS"] + 5 * (prev // 3) + 2 + (prev % 3) + 1)
                    for u4 in range(4):
                        cch = gq * 4 + u4
                        ctx.inc(t.matmul(bank[:, u4 * 128:(u4 + 1) * 128],
                                         T["qlat_n"][:, cch * 128:(cch + 1) * 128],
                                         ident[:], start=True, stop=True), "tP", 1)

    assert ctx.cnt["tP"] == c2["tP"] + 12 * QT
    assert ctx.cnt["vV"] == c2["vV"] + 2 * QT
    assert ctx.cnt["sS"] == c2["sS"] + 5 * QT
    A.pop()

    c6 = dict(ctx.cnt)
    NB3, NCH3 = 16, 12
    V_PRO, S_PRO = 24, 16

    def tp_mm(k):
        return 44 * (k // NCH3) + (k % NCH3) + 1

    def tp_T(n, j):
        return 44 * n + 12 + 4 * (j + 1)

    def vv_fold(k):
        return V_PRO + 44 * (k // NCH3) + (k % NCH3) + 1

    def vv_rope(n, j):
        return V_PRO + 44 * n + 12 + 4 * (j + 1)

    def ss_qblk(n, j):
        return S_PRO + 24 * n + 3 * j + 1

    def ss_stage(n, j):
        return S_PRO + 24 * n + 3 * (j + 1)

    with nc.Block("p2c_qb") as blk:
        @blk.gpsimd
        def _(g):
            ctx.inc(g.dma_start(out=T["normq"][:],
                                in_=ctx.inp["q_a_norm_w"].rearrange("(c p) -> p c", p=128),
                                allow_slow_non_contiguous=True), "qG", 16)
            for n in range(NB3):
                for c in range(NCH3):
                    k = n * NCH3 + c
                    bb = k % 2
                    if k >= 2:
                        ctx.wait(g, "tP", c6["tP"] + tp_mm(k - 1))
                    wv = ctx.inp["q_b_w"].rearrange("(c p) f -> c p f", p=128)
                    ctx.inc(g.dma_start(
                        out=T[f"p2c_w{bb}"][:, 0:384],
                        in_=wv[c][:, n * 384:(n + 1) * 384]), f"gL{bb}", 16)
            ctx.wait(g, "qG", c6["qG"] + 16)
            for q in ("gL0", "gL1"):
                ctx.wait(g, q, ctx.cnt[q])

        @blk.sync
        def _(sy):
            for n in range(NB3):
                for j in range(QT):
                    ctx.wait(sy, "sS", c6["sS"] + ss_stage(n, j))
                    h0 = 2 * n
                    idx = n * QT + j
                    bb2 = idx % 2
                    st = T[f"qstage{bb2}"]
                    ctx.inc(sy.dma_start(out=ctx.scr["QnT"][h0, j],
                                         in_=st[:, 0:128]), f"qU{bb2}", 16)
                    ctx.inc(sy.dma_start(out=ctx.scr["QnT"][h0 + 1, j],
                                         in_=st[:, 128:256]), f"qU{bb2}", 16)
                    ctx.inc(sy.dma_start(out=ctx.scr["QpT"][h0, j],
                                         in_=st[0:64, 256:384]), f"qU{bb2}", 16)
                    ctx.inc(sy.dma_start(out=ctx.scr["QpT"][h0 + 1, j],
                                         in_=st[0:64, 384:512]), f"qU{bb2}", 16)
            for q in ("qL0", "qL1", "qU0", "qU1"):
                ctx.wait(sy, q, ctx.cnt[q])

        @blk.vector
        def _(v):
            v = VProxy(v, ctx)
            ctx.wait(v, "qG", c6["qG"] + 16)
            for j in range(QT):
                rope_frac_vec(ctx, v, T["qpos"][:, j:j + 1], "rp")
                ctx.wait(v, "sS", c6["sS"] + 2 * (j + 1))
                v.tensor_copy(out=T["qcos"][:, j * 32:(j + 1) * 32], in_=T["rp_cos"][:])
                ctx.inc(v.tensor_copy(out=T["qsin"][:, j * 32:(j + 1) * 32],
                                      in_=T["rp_sin"][:]), "vV", 1)
            for n in range(NB3):
                for c in range(NCH3):
                    k = n * NCH3 + c
                    bb = k % 2
                    ctx.wait(v, f"gL{bb}", c6[f"gL{bb}"] + 16 * (k // 2 + 1))
                    ctx.inc(v.tensor_scalar(out=T[f"p2c_w{bb}"][:, 0:384],
                                            in0=T[f"p2c_w{bb}"][:, 0:384],
                                            scalar1=T["normq"][:, c:c + 1],
                                            scalar2=None, op0=ALU.mult), "vV", 1)
                for j in range(QT):
                    ctx.wait(v, "sS", c6["sS"] + ss_qblk(n, j))
                    cosj = T["qcos"][:, j * 32:(j + 1) * 32]
                    sinj = T["qsin"][:, j * 32:(j + 1) * 32]
                    rope_apply2(ctx, v, T["qblk"][:, 128:192], T["qropea"][:],
                                cosj, sinj)
                    rope_apply2(ctx, v, T["qblk"][:, 320:384], T["qropeb"][:],
                                cosj, sinj)

        @blk.tensor
        def _(t):
            for n in range(NB3):
                for c in range(NCH3):
                    k = n * NCH3 + c
                    ctx.wait(t, "vV", c6["vV"] + vv_fold(k))
                    if c == 0 and n >= 1:
                        ctx.wait(t, "sS", c6["sS"] + S_PRO + 24 * n)
                    for j in range(QT):
                        mm = t.matmul(ctx.psum[j][:, 0:384],
                                      T["qlatT"].rearrange("p (c q) -> p c q", q=NQ)[
                                          :, c, j * 128:(j + 1) * 128],
                                      T[f"p2c_w{(k % 2)}"][:, 0:384],
                                      start=(c == 0), stop=(c == NCH3 - 1))
                    ctx.inc(mm, "tP", 1)
                for j in range(QT):
                    ctx.wait(t, "vV", c6["vV"] + vv_rope(n, j))
                    bank = ctx.psum[j]
                    ctx.inc(t.matmul(bank[:, 0:128], T["qblk"][:, 0:128], ident[:],
                                     start=True, stop=True), "tP", 1)
                    ctx.inc(t.matmul(bank[:, 128:256], T["qblk"][:, 192:320], ident[:],
                                     start=True, stop=True), "tP", 1)
                    ctx.inc(t.matmul(bank[0:64, 256:384], T["qropea"][:], ident[:],
                                     start=True, stop=True), "tP", 1)
                    ctx.inc(t.matmul(bank[0:64, 384:512], T["qropeb"][:], ident[:],
                                     start=True, stop=True), "tP", 1)

        @blk.scalar
        def _(s):
            for j in range(QT):
                ctx.wait(s, "vV", c6["vV"] + 3 * j + 2)
                if j >= 1:
                    ctx.wait(s, "vV", c6["vV"] + 3 * j)
                rope_sin_scalar(ctx, s, "rp")
            for n in range(NB3):
                for j in range(QT):
                    idx = n * QT + j
                    if idx >= 1:
                        pn, pj = (n, j - 1) if j >= 1 else (n - 1, QT - 1)
                        ctx.wait(s, "vV", c6["vV"] + vv_rope(pn, pj))
                        ctx.wait(s, "tP", c6["tP"] + tp_T(pn, pj))
                    ctx.wait(s, "tP", c6["tP"] + 44 * n + 12)
                    ctx.inc(s.activation(out=T["qblk"][:], in_=ctx.psum[j][:, 0:384],
                                         func=AF.Copy), "sS", 1)
                    ctx.wait(s, "tP", c6["tP"] + tp_T(n, j))
                    bb2 = idx % 2
                    if idx >= 2:
                        ctx.wait(s, f"qU{bb2}", c6[f"qU{bb2}"] + 16 * 4 * (idx // 2))
                    st = T[f"qstage{bb2}"]
                    s.activation(out=st[:, 0:256], in_=ctx.psum[j][:, 0:256],
                                 func=AF.Copy)
                    ctx.inc(s.activation(out=st[0:64, 256:512],
                                         in_=ctx.psum[j][0:64, 256:512],
                                         func=AF.Copy), "sS", 2)

    assert ctx.cnt["tP"] == c6["tP"] + 44 * NB3
    assert ctx.cnt["vV"] == c6["vV"] + V_PRO + 44 * NB3
    assert ctx.cnt["sS"] == c6["sS"] + S_PRO + 24 * NB3


def emit_p2d(ctx):
    nc = ctx.nc
    T = ctx.tiles
    c0 = dict(ctx.cnt)
    NIT_A, NIT_B = 16, 28

    units = []
    for j in range(QT):
        for kb in range(j + 1):
            for f in range(IH):
                units.append((j, kb, f))

    vv_after_acc = {}
    vv = c0["vV"]
    u = 0
    for j in range(QT):
        for kb in range(j + 1):
            for f in range(IH):
                vv += 1
                vv_after_acc[u] = vv
                u += 1
            if kb == j:
                vv += 2
        vv += 7
        vv += 2 * (NIT_A + NIT_B)
        vv += j + 1
    VV_TOT = vv

    def moff(j):
        return sum(jj + 1 for jj in range(j)) * KBLK

    with nc.Block("p2d") as blk:
        @blk.tensor
        def _(t):
            for u, (j, kb, f) in enumerate(units):
                bank = ctx.psum[u % 2]
                if u >= 2:
                    ctx.wait(t, "sS", c0["sS"] + (u - 2) + 1)
                ctx.inc(t.matmul(bank[:], T["iqT"][:, f * NQ + j * 128:
                                                   f * NQ + (j + 1) * 128],
                                 T["ikT"][:, kb * KBLK:(kb + 1) * KBLK],
                                 start=True, stop=True), "tP", 1)

        @blk.scalar
        def _(s):
            for u, (j, kb, f) in enumerate(units):
                bank = ctx.psum[u % 2]
                ctx.wait(s, "tP", c0["tP"] + u + 1)
                if u >= 2:
                    ctx.wait(s, "vV", vv_after_acc[u - 2])
                ctx.inc(s.activation(out=T[f"p2d_r{u % 2}"][:], in_=bank[:],
                                     func=AF.Relu), "sS", 1)

        @blk.vector
        def _(v):
            v = VProxy(v, ctx)
            u = 0
            for j in range(QT):
                prefix = (j + 1) * KBLK
                for kb in range(j + 1):
                    cols = T["idx_acc"][:, kb * KBLK:(kb + 1) * KBLK]
                    for f in range(IH):
                        ctx.wait(v, "sS", c0["sS"] + u + 1)
                        iwcol = T["iw_sb"][:, j * IH + f:j * IH + f + 1]
                        if f == 0:
                            ctx.inc(v.tensor_scalar(out=cols, in0=T[f"p2d_r{u % 2}"][:],
                                                    scalar1=iwcol, scalar2=None,
                                                    op0=ALU.mult), "vV", 1)
                        else:
                            ctx.inc(v.scalar_tensor_tensor(out=cols,
                                                           in0=T[f"p2d_r{u % 2}"][:],
                                                           scalar=iwcol, in1=cols,
                                                           op0=ALU.mult, op1=ALU.add),
                                    "vV", 1)
                        u += 1
                    if kb == j:
                        v.tensor_scalar(out=T["p2d_th"][:], in0=T["qpos"][:, j:j + 1],
                                        scalar1=float(512 * kb), scalar2=None,
                                        op0=ALU.subtract)
                        ctx.inc(v.tensor_scalar(out=T["p2d_t"][:], in0=T["kcol"][:],
                                                scalar1=T["p2d_th"][:],
                                                scalar2=NEG_CAUSAL,
                                                op0=ALU.is_gt, op1=ALU.mult), "vV", 1)
                        ctx.inc(v.tensor_tensor(out=cols, in0=cols, in1=T["p2d_t"][:],
                                                op=ALU.add), "vV", 1)
                ctx.inc(v.reduce_max(out=T["p2d_mx"][:], in_=T["idx_acc"][:, 0:prefix],
                                     axis=AXX), "vV", 1)
                ctx.inc(v.memset(T["p2d_th"][:], -1.0e30), "vV", 1)
                ctx.inc(v.tensor_scalar(out=T["p2d_cmp"][:, 0:prefix],
                                        in0=T["idx_acc"][:, 0:prefix],
                                        scalar1=-1.0, scalar2=None,
                                        op0=ALU.mult), "vV", 1)
                ctx.vsync(v)
                ctx.inc(v.scalar_tensor_tensor(out=T["p2d_cmp"][:, 0:prefix],
                                               in0=T["idx_acc"][:, 0:prefix],
                                               scalar=T["p2d_th"][:],
                                               in1=T["p2d_cmp"][:, 0:prefix],
                                               op0=ALU.is_gt, op1=ALU.mult), "vV", 1)
                ctx.vsync(v)
                ctx.inc(v.reduce_max(out=T["p2d_cnt"][:], in_=T["p2d_cmp"][:, 0:prefix],
                                     axis=AXX), "vV", 1)
                ctx.vsync(v)
                ctx.inc(v.tensor_scalar(out=T["p2d_lo"][:], in0=T["p2d_cnt"][:],
                                        scalar1=-1.0, scalar2=1.0,
                                        op0=ALU.mult, op1=ALU.subtract), "vV", 1)
                ctx.inc(v.tensor_scalar(out=T["p2d_hi"][:], in0=T["p2d_mx"][:],
                                        scalar1=1.0, scalar2=None, op0=ALU.add),
                        "vV", 1)
                for rnd, nit in ((0, NIT_A), (1, NIT_B)):
                    if rnd == 1:
                        ctx.vsync(v)
                        v.tensor_scalar(out=T["p2d_hi"][:], in0=T["p2d_lo"][:],
                                        scalar1=0.4, scalar2=None, op0=ALU.add)
                    for it in range(nit):
                        ctx.vsync(v)
                        v.tensor_scalar(out=T["p2d_mid"][:], in0=T["p2d_lo"][:],
                                        scalar1=T["p2d_hi"][:], scalar2=0.5,
                                        op0=ALU.add, op1=ALU.mult)
                        ctx.vsync(v)
                        ctx.inc(v.tensor_scalar(out=T["p2d_cmp"][:, 0:prefix],
                                                in0=T["idx_acc"][:, 0:prefix],
                                                scalar1=T["p2d_mid"][:], scalar2=0.0,
                                                op0=ALU.is_ge, op1=ALU.add,
                                                accum_out=T["p2d_cnt"][:]), "vV", 1)
                        ctx.vsync(v)
                        v.tensor_scalar(out=T["p2d_b"][:], in0=T["p2d_cnt"][:],
                                        scalar1=float(TOPK), scalar2=None,
                                        op0=ALU.is_ge)
                        v.tensor_scalar(out=T["p2d_d"][:], in0=T["p2d_mid"][:],
                                        scalar1=T["p2d_lo"][:], scalar2=None,
                                        op0=ALU.subtract)
                        ctx.vsync(v)
                        v.tensor_scalar(out=T["p2d_e"][:], in0=T["p2d_hi"][:],
                                        scalar1=T["p2d_mid"][:], scalar2=None,
                                        op0=ALU.subtract)
                        ctx.vsync(v)
                        v.scalar_tensor_tensor(out=T["p2d_lo"][:], in0=T["p2d_b"][:],
                                               scalar=T["p2d_d"][:], in1=T["p2d_lo"][:],
                                               op0=ALU.mult, op1=ALU.add)
                        ctx.inc(v.scalar_tensor_tensor(out=T["p2d_hi"][:],
                                                       in0=T["p2d_b"][:],
                                                       scalar=T["p2d_e"][:],
                                                       in1=T["p2d_mid"][:],
                                                       op0=ALU.mult, op1=ALU.add),
                                "vV", 1)
                ctx.vsync(v)
                for kb in range(j + 1):
                    ctx.inc(v.tensor_scalar(
                        out=T["mask"][:, moff(j) + kb * KBLK:moff(j) + (kb + 1) * KBLK],
                        in0=T["idx_acc"][:, kb * KBLK:(kb + 1) * KBLK],
                        scalar1=T["p2d_lo"][:], scalar2=NEG_BIG,
                        op0=ALU.is_lt, op1=ALU.mult), "vV", 1)

    assert ctx.cnt["vV"] == VV_TOT, (ctx.cnt["vV"], VV_TOT)
    assert ctx.cnt["tP"] == c0["tP"] + len(units)
    assert ctx.cnt["sS"] == c0["sS"] + len(units)


def emit_p3(ctx):
    nc = ctx.nc
    T = ctx.tiles
    c0 = dict(ctx.cnt)
    ISQ = float(QKD) ** -0.5

    units = []
    for h in range(H):
        for kb in range(NKB):
            for j in range(kb, QT):
                units.append((h, j, kb))
    NU = len(units)

    tpS, tpPT, tpAV, tpOn = {}, {}, {}, {}
    vvSm, vvStat, vvO, vvOn = {}, {}, {}, {}
    ssA, ssP, ssPT, ssOT = {}, {}, {}, {}
    NU0 = len(units)
    tp = c0["tP"]

    def _tpS(u):
        global_ = None
    tp += 1; tpS[0] = tp
    if NU0 > 1:
        tp += 1; tpS[1] = tp
    for u, (h, j, kb) in enumerate(units):
        tp += 1; tpPT[u] = tp
        tp += 1; tpAV[u] = tp
        if kb == j:
            tp += 1; tpOn[(h, j)] = tp
        if u + 2 < NU0:
            tp += 1; tpS[u + 2] = tp
    vv = c0["vV"]
    vv += 2; vvSm[0] = vv - 1; vvStat[0] = vv
    for u, (h, j, kb) in enumerate(units):
        if u + 1 < NU0:
            vv += 2; vvSm[u + 1] = vv - 1; vvStat[u + 1] = vv
        vv += 1; vvO[u] = vv
        if kb == j:
            vv += 1; vvOn[(h, j)] = vv
    ss = c0["sS"]
    for u, (h, j, kb) in enumerate(units):
        ss += 1; ssA[u] = ss
        ss += 1; ssP[u] = ss
        ss += 1; ssPT[u] = ss
        if kb == j:
            ss += 1; ssOT[(h, j)] = ss
    TP_TOT, VV_TOT, SS_TOT = tp, vv, ss
    last_av_of_head = {h: max(tpAV[u] for u, (hh, _, _) in enumerate(units) if hh == h)
                       for h in range(H)}
    ot_order = [(h, j) for h in range(H) for j in range(QT)]

    def moff(j):
        return sum(jj + 1 for jj in range(j)) * KBLK

    with nc.Block("p3") as blk:
        @blk.sync
        def _(sy):
            ctx.inc(sy.dma_start(out=T["xkp"][0:64, :], in_=ctx.scr["KpT"]),
                    "qL0", 16)
            for h in range(H):
                bb = h % 2
                if h >= 2:
                    ctx.wait(sy, "tP", last_av_of_head[h - 2])
                ctx.inc(sy.dma_start(out=T[f"xkn{bb}"][:], in_=ctx.scr["KnT"][h]),
                        f"qL{bb}", 16)
                for tt in range(TT):
                    ctx.inc(sy.dma_start(
                        out=T[f"xv{bb}"][:, tt * VD:(tt + 1) * VD],
                        in_=ctx.scr["V"][h, tt * 128:(tt + 1) * 128, :]),
                        f"qL{bb}", 16)
                for j in range(QT):
                    ctx.inc(sy.dma_start(out=T[f"xqn{bb}"][:, j * 128:(j + 1) * 128],
                                         in_=ctx.scr["QnT"][h, j]), f"qL{bb}", 16)
                    ctx.inc(sy.dma_start(out=T[f"xqp{bb}"][0:64, j * 128:(j + 1) * 128],
                                         in_=ctx.scr["QpT"][h, j]), f"qL{bb}", 16)
                if h >= 1:
                    for j in range(QT):
                        idx = (h - 1) * QT + j
                        ctx.wait(sy, "sS", ssOT[(h - 1, j)])
                        ctx.inc(sy.dma_start(out=ctx.scr["OT"][h - 1, j],
                                             in_=T[f"ot_sb{idx % 2}"][:]),
                                f"qU{idx % 2}", 16)
            for j in range(QT):
                idx = (H - 1) * QT + j
                ctx.wait(sy, "sS", ssOT[(H - 1, j)])
                ctx.inc(sy.dma_start(out=ctx.scr["OT"][H - 1, j],
                                     in_=T[f"ot_sb{idx % 2}"][:]), f"qU{idx % 2}", 16)
            for q in ("qL0", "qL1", "qU0", "qU1"):
                ctx.wait(sy, q, ctx.cnt[q])

        @blk.tensor
        def _(t):
            ident = T["ident"]

            def emit_S(u):
                h, j, kb = units[u]
                bb = h % 2
                bA = ctx.psum[u % 2]
                if kb == 0 and j == 0:
                    nload = 16 * (1 + TT + 2 * QT) * (h // 2 + 1) + \
                            (16 if bb == 0 else 0)
                    ctx.wait(t, f"qL{bb}", c0[f"qL{bb}"] + nload)
                if u >= 2:
                    ctx.wait(t, "vV", vvSm[u - 2])
                t.matmul(bA[:], T[f"xqn{bb}"][:, j * 128:(j + 1) * 128],
                         T[f"xkn{bb}"][:, kb * KBLK:(kb + 1) * KBLK],
                         start=True, stop=False)
                ctx.inc(t.matmul(bA[:], T[f"xqp{bb}"][0:64, j * 128:(j + 1) * 128],
                                 T["xkp"][0:64, kb * KBLK:(kb + 1) * KBLK],
                                 start=False, stop=True), "tP", 1)

            def emit_PT_AV(u):
                h, j, kb = units[u]
                bb = h % 2
                bB = ctx.psum[2 + u % 2]
                bC = ctx.psum[4 + u % 2]
                ctx.wait(t, "sS", ssP[u])
                if u >= 1:
                    ctx.wait(t, "sS", ssPT[u - 1])
                for i in range(4):
                    mm = t.matmul(bB[:, i * 128:(i + 1) * 128],
                                  T[f"p3_p{u % 2}"][:, i * 128:(i + 1) * 128],
                                  ident[:], start=True, stop=True)
                ctx.inc(mm, "tP", 1)
                ctx.wait(t, "sS", ssPT[u])
                if u >= 2:
                    ctx.wait(t, "vV", vvO[u - 2])
                for i in range(4):
                    mm = t.matmul(bC[:, 0:VD], T["p3_pt"][:, i * 128:(i + 1) * 128],
                                  T[f"xv{bb}"][:, (kb * 4 + i) * VD:(kb * 4 + i + 1) * VD],
                                  start=(i == 0), stop=(i == 3))
                ctx.inc(mm, "tP", 1)
                if kb == j:
                    ctx.wait(t, "vV", vvOn[(h, j)])
                    oidx = ot_order.index((h, j))
                    if oidx >= 1:
                        ctx.wait(t, "sS", ssOT[ot_order[oidx - 1]])
                    ctx.inc(t.matmul(ctx.psum[6][:, 0:128], T["p3_on"][:], ident[:],
                                     start=True, stop=True), "tP", 1)

            emit_S(0)
            if NU > 1:
                emit_S(1)
            for u in range(NU):
                emit_PT_AV(u)
                if u + 2 < NU:
                    emit_S(u + 2)

        @blk.vector
        def _(v):
            v = VProxy(v, ctx)

            def emit_Sm_stats(u):
                h, j, kb = units[u]
                if kb == 0:
                    v.memset(T[f"p3_o{j}"][:], 0.0)
                    v.memset(T[f"p3_l{j}"][:], 0.0)
                    v.memset(T[f"p3_m{j}"][:], -1.0e30)
                bA = ctx.psum[u % 2]
                ctx.wait(v, "tP", tpS[u])
                mt = T["mask"][:, moff(j) + kb * KBLK:moff(j) + (kb + 1) * KBLK]
                ctx.inc(v.scalar_tensor_tensor(out=T[f"p3_sm{u % 2}"][:], in0=bA[:],
                                               scalar=ISQ, in1=mt,
                                               op0=ALU.mult, op1=ALU.add), "vV", 1)
                v.reduce_max(out=T["p3_mkb"][:], in_=T[f"p3_sm{u % 2}"][:], axis=AXX)
                v.tensor_tensor(out=T["p3_mnew"][:], in0=T[f"p3_m{j}"][:],
                                in1=T["p3_mkb"][:], op=ALU.max)
                v.tensor_scalar(out=T[f"p3_d{u % 2}"][:], in0=T[f"p3_m{j}"][:],
                                scalar1=T["p3_mnew"][:], scalar2=None,
                                op0=ALU.subtract)
                v.tensor_copy(out=T[f"p3_m{j}"][:], in_=T["p3_mnew"][:])
                ctx.inc(v.tensor_scalar(out=T[f"p3_nm{u % 2}"][:], in0=T["p3_mnew"][:],
                                        scalar1=-1.0, scalar2=None, op0=ALU.mult),
                        "vV", 1)

            def emit_lO(u):
                h, j, kb = units[u]
                bC = ctx.psum[4 + u % 2]
                ctx.wait(v, "sS", ssP[u])
                v.scalar_tensor_tensor(out=T[f"p3_l{j}"][:], in0=T[f"p3_l{j}"][:],
                                       scalar=T[f"p3_al{u % 2}"][:],
                                       in1=T[f"p3_lkb{u % 2}"][:],
                                       op0=ALU.mult, op1=ALU.add)
                ctx.wait(v, "tP", tpAV[u])
                ctx.inc(v.scalar_tensor_tensor(out=T[f"p3_o{j}"][:],
                                               in0=T[f"p3_o{j}"][:],
                                               scalar=T[f"p3_al{u % 2}"][:],
                                               in1=bC[:, 0:VD],
                                               op0=ALU.mult, op1=ALU.add), "vV", 1)
                if kb == j:
                    v.reciprocal(out=T["p3_linv"][:], in_=T[f"p3_l{j}"][:])
                    ctx.inc(v.tensor_scalar(out=T["p3_on"][:], in0=T[f"p3_o{j}"][:],
                                            scalar1=T["p3_linv"][:], scalar2=None,
                                            op0=ALU.mult), "vV", 1)

            emit_Sm_stats(0)
            for u in range(NU):
                if u + 1 < NU:
                    emit_Sm_stats(u + 1)
                emit_lO(u)

        @blk.scalar
        def _(s):
            for u, (h, j, kb) in enumerate(units):
                bB = ctx.psum[2 + u % 2]
                ctx.wait(s, "vV", vvStat[u])
                ctx.inc(s.activation(out=T[f"p3_al{u % 2}"][:], in_=T[f"p3_d{u % 2}"][:],
                                     func=AF.Exp), "sS", 1)
                ctx.inc(s.activation(out=T[f"p3_p{u % 2}"][:], in_=T[f"p3_sm{u % 2}"][:],
                                     func=AF.Exp, bias=T[f"p3_nm{u % 2}"][:],
                                     accum_out=T[f"p3_lkb{u % 2}"][:]), "sS", 1)
                ctx.wait(s, "tP", tpPT[u])
                ctx.inc(s.activation(out=T["p3_pt"][:], in_=bB[:], func=AF.Copy),
                        "sS", 1)
                if kb == j:
                    idx = h * QT + j
                    ctx.wait(s, "tP", tpOn[(h, j)])
                    if idx >= 2:
                        ctx.wait(s, f"qU{idx % 2}",
                                 c0[f"qU{idx % 2}"] + 16 * (idx // 2))
                    ctx.inc(s.activation(out=T[f"ot_sb{idx % 2}"][:],
                                         in_=ctx.psum[6][:, 0:128], func=AF.Copy),
                            "sS", 1)

    assert ctx.cnt["tP"] == TP_TOT, (ctx.cnt["tP"], TP_TOT)
    assert ctx.cnt["vV"] == VV_TOT, (ctx.cnt["vV"], VV_TOT)
    assert ctx.cnt["sS"] == SS_TOT, (ctx.cnt["sS"], SS_TOT)


def emit_p4a(ctx):
    nc = ctx.nc
    T = ctx.tiles
    c0 = dict(ctx.cnt)

    with nc.Block("p4a") as blk:
        @blk.gpsimd
        def _(g):
            for qtr in range(4):
                ow = ctx.inp["o_w"].rearrange("(c p) f -> c p f", p=128)
                if qtr >= 1:
                    ctx.wait(g, "tP", c0["tP"] + 16 * qtr)
                for c in range(32):
                    ctx.inc(g.dma_start(
                        out=T["p4_ow"][:, c * 1024:(c + 1) * 1024],
                        in_=ow[c][:, qtr * 1024:(qtr + 1) * 1024]), "gL0", 16)
            ctx.wait(g, "gL0", ctx.cnt["gL0"])

        @blk.sync
        def _(sy):
            for qtr in range(4):
                for j in range(QT):
                    k = qtr * QT + j
                    bb = k % 2
                    if k >= 2:
                        ctx.wait(sy, "tP", c0["tP"] + 2 * (k - 1))
                    for h in range(H):
                        ctx.inc(sy.dma_start(
                            out=T[f"p4_ot{bb}"][:, h * 128:(h + 1) * 128],
                            in_=ctx.scr["OT"][h, j]), f"qL{bb}", 16)
                    ctx.inc(sy.dma_start(
                        out=T[f"p4_xq{bb}"][:],
                        in_=ctx.inp["xq"][j * 128:(j + 1) * 128,
                                          qtr * 1024:(qtr + 1) * 1024]),
                        f"qL{bb}", 16)
                    if k >= 1:
                        pq, pj = (k - 1) // QT, (k - 1) % QT
                        ctx.wait(sy, "vV", c0["vV"] + 2 * k)
                        ctx.inc(sy.dma_start(
                            out=ctx.scr["XRES"][pj * 128:(pj + 1) * 128,
                                                pq * 1024:(pq + 1) * 1024],
                            in_=T[f"p4_x1{(k - 1) % 2}"][:]), f"qU{(k - 1) % 2}", 16)
            ctx.wait(sy, "vV", c0["vV"] + 2 * 32)
            ctx.inc(sy.dma_start(
                out=ctx.scr["XRES"][7 * 128:8 * 128, 3 * 1024:4 * 1024],
                in_=T["p4_x11"][:]), "qU1", 16)
            for q in ("qL0", "qL1", "qU0", "qU1"):
                ctx.wait(sy, q, ctx.cnt[q])

        @blk.tensor
        def _(t):
            for qtr in range(4):
                ctx.wait(t, "gL0", c0["gL0"] + 16 * 32 * (qtr + 1))
                for j in range(QT):
                    k = qtr * QT + j
                    bb = k % 2
                    ctx.wait(t, f"qL{bb}", c0[f"qL{bb}"] + 16 * 33 * (k // 2 + 1))
                    if k >= 2:
                        ctx.wait(t, "vV", c0["vV"] + 2 * (k - 1))
                    for n in range(2):
                        bank = ctx.psum[2 * (k % 2) + n]
                        for c in range(32):
                            mm = t.matmul(bank[:],
                                          T[f"p4_ot{bb}"][:, c * 128:(c + 1) * 128],
                                          T["p4_ow"][:, c * 1024 + n * 512:
                                                     c * 1024 + (n + 1) * 512],
                                          start=(c == 0), stop=(c == 31))
                        ctx.inc(mm, "tP", 1)

        @blk.vector
        def _(v):
            v = VProxy(v, ctx)
            for qtr in range(4):
                for j in range(QT):
                    k = qtr * QT + j
                    bb = k % 2
                    ctx.wait(v, "tP", c0["tP"] + 2 * (k + 1))
                    if k >= 2:
                        ctx.wait(v, f"qU{bb}", c0[f"qU{bb}"] + 16 * (k // 2))
                    for n in range(2):
                        bank = ctx.psum[2 * (k % 2) + n]
                        ctx.inc(v.tensor_tensor(
                            out=T[f"p4_x1{bb}"][:, n * 512:(n + 1) * 512],
                            in0=bank[:], in1=T[f"p4_xq{bb}"][:, n * 512:(n + 1) * 512],
                            op=ALU.add), "vV", 1)

    assert ctx.cnt["tP"] == c0["tP"] + 2 * 32
    assert ctx.cnt["vV"] == c0["vV"] + 2 * 32


def emit_lnpass(ctx, dst_name):
    nc = ctx.nc
    T = ctx.tiles
    c0 = dict(ctx.cnt)

    with nc.Block("ln_" + dst_name) as blk:
        @blk.sync
        def _(sy):
            for j in range(QT):
                bb = j % 2
                if j >= 2:
                    ctx.wait(sy, "vV", c0["vV"] + 2 * (j - 1))
                    ctx.wait(sy, "sS", c0["sS"] + 10 * (j - 1))
                ctx.inc(sy.dma_start(out=T[f"p2c_xq{bb}"][:],
                                     in_=ctx.scr["XRES"][j * 128:(j + 1) * 128, :]),
                        f"qL{bb}", 16)
            for q in ("qL0", "qL1"):
                ctx.wait(sy, q, ctx.cnt[q])

        @blk.vector
        def _(v):
            v = VProxy(v, ctx)
            for j in range(QT):
                bb = j % 2
                xq = T[f"p2c_xq{bb}"]
                ctx.wait(v, f"qL{bb}", c0[f"qL{bb}"] + 16 * (j // 2 + 1))
                v.reduce_sum(out=T["st_sum"][:], in_=xq[:], axis=AXX)
                ctx.vsync(v)
                v.tensor_scalar(out=T["st_mean"][:], in0=T["st_sum"][:],
                                scalar1=1.0 / D, scalar2=None, op0=ALU.mult)
                ctx.wait(v, "sS", c0["sS"] + 10 * j + 1)
                v.tensor_scalar(out=T["st_ms"][:], in0=T["st_ssq"][:],
                                scalar1=1.0 / D, scalar2=None, op0=ALU.mult)
                v.tensor_tensor(out=T["st_m2"][:], in0=T["st_mean"][:],
                                in1=T["st_mean"][:], op=ALU.mult)
                ctx.inc(v.tensor_tensor(out=T["st_var"][:], in0=T["st_ms"][:],
                                        in1=T["st_m2"][:], op=ALU.subtract), "vV", 1)
                ctx.wait(v, "sS", c0["sS"] + 10 * j + 2)
                v.reciprocal(out=T["st_rstd"][:], in_=T["st_std"][:])
                if j >= 1:
                    ctx.wait(v, "tP", c0["tP"] + 32 * j)
                ctx.inc(v.tensor_scalar(out=T["p2c_xn"][:], in0=xq[:],
                                        scalar1=T["st_mean"][:], scalar2=T["st_rstd"][:],
                                        op0=ALU.subtract, op1=ALU.mult), "vV", 1)

        @blk.scalar
        def _(s):
            for j in range(QT):
                bb = j % 2
                xq = T[f"p2c_xq{bb}"]
                ctx.wait(s, f"qL{bb}", c0[f"qL{bb}"] + 16 * (j // 2 + 1))
                ctx.inc(s.activation(out=T["st_xsq"][:], in_=xq[:], func=AF.Square,
                                     accum_out=T["st_ssq"][:]), "sS", 1)
                ctx.wait(s, "vV", c0["vV"] + 2 * j + 1)
                ctx.inc(s.activation(out=T["st_std"][:], in_=T["st_var"][:],
                                     func=AF.Sqrt, bias=T["eps"][:]), "sS", 1)
                for gq in range(8):
                    bank = ctx.psum[6 + (gq % 2)]
                    ctx.wait(s, "tP", c0["tP"] + 32 * j + 4 * (gq + 1))
                    ctx.inc(s.activation(
                        out=T[dst_name].rearrange("p (c q) -> p c q", q=NQ)[
                            :, gq * 4:(gq + 1) * 4, j * 128:(j + 1) * 128],
                        in_=bank[:].rearrange("p (c q) -> p c q", q=128),
                        func=AF.Copy), "sS", 1)

        @blk.tensor
        def _(t):
            ident = T["ident"]
            for j in range(QT):
                ctx.wait(t, "vV", c0["vV"] + 2 * j + 2)
                for gq in range(8):
                    bank = ctx.psum[6 + (gq % 2)]
                    gidx = j * 8 + gq
                    if gidx >= 2:
                        prev = gidx - 2
                        ctx.wait(t, "sS", c0["sS"] + 10 * (prev // 8) + 2 + (prev % 8) + 1)
                    for u4 in range(4):
                        cch = gq * 4 + u4
                        ctx.inc(t.matmul(bank[:, u4 * 128:(u4 + 1) * 128],
                                         T["p2c_xn"][:, cch * 128:(cch + 1) * 128],
                                         ident[:], start=True, stop=True), "tP", 1)

    assert ctx.cnt["tP"] == c0["tP"] + 32 * QT
    assert ctx.cnt["vV"] == c0["vV"] + 2 * QT
    assert ctx.cnt["sS"] == c0["sS"] + 10 * QT


def emit_p5(ctx, src_name, final):
    nc = ctx.nc
    T = ctx.tiles
    c0 = dict(ctx.cnt)

    with nc.Block("ffn1_" + src_name) as blk:
        @blk.sync
        def _(sy):
            for hb in range(HB1):
                bb = hb % 2
                if hb >= 2:
                    ctx.wait(sy, "tP", c0["tP"] + 4 * (hb - 1))
                ctx.inc(sy.dma_start(out=T[f"p5_w1{bb}"][:], in_=ctx.scr["W1T"][hb]),
                        f"qL{bb}", 16)
                ctx.inc(sy.dma_start(out=T[f"p5_w3{bb}"][:], in_=ctx.scr["W3T"][hb]),
                        f"qL{bb}", 16)
                if hb >= 1:
                    ctx.wait(sy, "vV", c0["vV"] + 2 * hb)
                    ctx.inc(sy.dma_start(out=ctx.scr["ACT"][hb - 1],
                                         in_=T[f"p5_act{(hb - 1) % 2}"][:]),
                            f"qU{(hb - 1) % 2}", 16)
            ctx.wait(sy, "vV", c0["vV"] + 2 * HB1)
            ctx.inc(sy.dma_start(out=ctx.scr["ACT"][HB1 - 1],
                                 in_=T[f"p5_act{(HB1 - 1) % 2}"][:]),
                    f"qU{(HB1 - 1) % 2}", 16)
            for q in ("qL0", "qL1", "qU0", "qU1"):
                ctx.wait(sy, q, ctx.cnt[q])

        @blk.tensor
        def _(t):
            for hb in range(HB1):
                bb = hb % 2
                ctx.wait(t, f"qL{bb}", c0[f"qL{bb}"] + 32 * (hb // 2 + 1))
                if hb >= 2:
                    ctx.wait(t, "vV", c0["vV"] + 2 * (hb - 1))
                for ab in range(2):
                    w = T[f"p5_w1{bb}"] if ab == 0 else T[f"p5_w3{bb}"]
                    for c in range(32):
                        for th in range(2):
                            bank = ctx.psum[(hb % 2) * 4 + th * 2 + ab]
                            mm = t.matmul(bank[:],
                                          w[:, c * 128:(c + 1) * 128],
                                          T[src_name][:, c * NQ + th * 512:
                                                      c * NQ + (th + 1) * 512],
                                          start=(c == 0), stop=(c == 31),
                                          skip_group_check=True)
                    if ab == 0:
                        ctx.inc(mm, "tP", 2)
                    else:
                        ctx.inc(mm, "tP", 2)

        @blk.scalar
        def _(s):
            for hb in range(HB1):
                for th in range(2):
                    idx = 2 * hb + th
                    bank_a = ctx.psum[(hb % 2) * 4 + th * 2]
                    ctx.wait(s, "tP", c0["tP"] + 4 * hb + 2)
                    if idx >= 1:
                        ctx.wait(s, "vV", c0["vV"] + idx)
                    ctx.inc(s.activation(out=T["p5_sa"][:], in_=bank_a[:],
                                         func=AF.Silu), "sS", 1)

        @blk.vector
        def _(v):
            v = VProxy(v, ctx)
            for hb in range(HB1):
                bb = hb % 2
                for th in range(2):
                    bank_b = ctx.psum[(hb % 2) * 4 + th * 2 + 1]
                    ctx.wait(v, "sS", c0["sS"] + 2 * hb + th + 1)
                    ctx.wait(v, "tP", c0["tP"] + 4 * hb + 4)
                    if hb >= 2 and th == 0:
                        ctx.wait(v, f"qU{bb}", c0[f"qU{bb}"] + 16 * (hb // 2))
                    ctx.inc(v.tensor_tensor(
                        out=T[f"p5_act{bb}"][:, th * 512:(th + 1) * 512],
                        in0=T["p5_sa"][:], in1=bank_b[:], op=ALU.mult), "vV", 1)

    assert ctx.cnt["tP"] == c0["tP"] + 4 * HB1
    assert ctx.cnt["vV"] == c0["vV"] + 2 * HB1
    assert ctx.cnt["sS"] == c0["sS"] + 2 * HB1

    c1 = dict(ctx.cnt)
    with nc.Block("ffn2_" + src_name) as blk:
        @blk.sync
        def _(sy):
            for dwin in range(8):
                for hb in range(HB1):
                    k = dwin * HB1 + hb
                    bb = k % 2
                    if k >= 2:
                        ctx.wait(sy, "tP", c1["tP"] + (k - 1))
                    ctx.inc(sy.dma_start(out=T[f"p5_a{bb}"][:], in_=ctx.scr["ACT"][hb]),
                            f"qL{bb}", 16)
                    ctx.inc(sy.dma_start(
                        out=T[f"p5_w2{bb}"][:],
                        in_=ctx.scr["W2T"][dwin, hb * 128:(hb + 1) * 128, :]),
                        f"qL{bb}", 16)
                for tsub in range(QT):
                    k2 = dwin * QT + tsub
                    bb2 = k2 % 2
                    if k2 >= 2:
                        ctx.wait(sy, "vV", c1["vV"] + k2 - 1)
                    ctx.inc(sy.dma_start(
                        out=T[f"p5_xr{bb2}"][:],
                        in_=ctx.scr["XRES"][tsub * 128:(tsub + 1) * 128,
                                            dwin * 512:(dwin + 1) * 512]),
                        f"qV{bb2}", 16)
                    ctx.wait(sy, "vV", c1["vV"] + k2 + 1)
                    dst = (ctx.out_y if final else ctx.scr["XRES"])
                    ctx.inc(sy.dma_start(
                        out=dst[tsub * 128:(tsub + 1) * 128,
                                dwin * 512:(dwin + 1) * 512],
                        in_=T[f"p5_x2{bb2}"][:]), f"qU{bb2}", 16)
            for q in ("qL0", "qL1", "qU0", "qU1", "qV0", "qV1"):
                ctx.wait(sy, q, ctx.cnt[q])

        @blk.tensor
        def _(t):
            for dwin in range(8):
                for hb in range(HB1):
                    k = dwin * HB1 + hb
                    bb = k % 2
                    ctx.wait(t, f"qL{bb}", c1[f"qL{bb}"] + 32 * (k // 2 + 1))
                    if hb == 0 and dwin >= 1:
                        ctx.wait(t, "vV", c1["vV"] + QT * dwin)
                    for tsub in range(QT):
                        mm = t.matmul(ctx.psum[tsub][:],
                                      T[f"p5_a{bb}"][:, tsub * 128:(tsub + 1) * 128],
                                      T[f"p5_w2{bb}"][:],
                                      start=(hb == 0), stop=(hb == HB1 - 1))
                    ctx.inc(mm, "tP", 1)

        @blk.vector
        def _(v):
            v = VProxy(v, ctx)
            for dwin in range(8):
                ctx.wait(v, "tP", c1["tP"] + HB1 * (dwin + 1))
                for tsub in range(QT):
                    k2 = dwin * QT + tsub
                    bb2 = k2 % 2
                    ctx.wait(v, f"qV{bb2}", c1[f"qV{bb2}"] + 16 * (k2 // 2 + 1))
                    if k2 >= 2:
                        ctx.wait(v, f"qU{bb2}", c1[f"qU{bb2}"] + 16 * (k2 // 2))
                    ctx.inc(v.tensor_tensor(out=T[f"p5_x2{bb2}"][:],
                                            in0=ctx.psum[tsub][:],
                                            in1=T[f"p5_xr{bb2}"][:],
                                            op=ALU.add), "vV", 1)

    assert ctx.cnt["tP"] == c1["tP"] + 8 * HB1
    assert ctx.cnt["vV"] == c1["vV"] + 8 * QT


def emit_p6(ctx, src_name):
    nc = ctx.nc
    T = ctx.tiles
    ident = T["ident"]

    c0 = dict(ctx.cnt)
    with nc.Block("p6a") as blk:
        @blk.sync
        def _(sy):
            for mt in range(32):
                bb = 0
                if mt >= 1:
                    ctx.wait(sy, "vV", c0["vV"] + 3 * mt - 1)
                ctx.inc(sy.dma_start(out=T["p6_mk0"][:],
                                     in_=ctx.inp["ab_mask"][mt * 128:(mt + 1) * 128, :]),
                        "qL0", 16)
            ctx.wait(sy, "vV", c0["vV"] + 3 * 32 + 1)
            ctx.inc(sy.dma_start(out=ctx.scr["BMT"], in_=T["p6_bmT"][:]), "qU0", 16)
            for q in ("qL0", "qU0"):
                ctx.wait(sy, q, ctx.cnt[q])

        @blk.vector
        def _(v):
            v = VProxy(v, ctx)
            for mt in range(32):
                ctx.wait(v, "qL0", c0["qL0"] + 16 * (mt + 1))
                ctx.inc(v.tensor_reduce(
                    out=T["p6_cs"][:],
                    in_=T["p6_mk0"][:].rearrange("p (g c) -> p g c", c=32),
                    axis=AXX, op=ALU.add), "vV", 1)
                ctx.vsync(v)
                ctx.inc(v.tensor_copy(out=T["p6_csb"][:], in_=T["p6_cs"][:]), "vV", 1)
                ctx.wait(v, "tP", c0["tP"] + mt + 1)
                ctx.inc(v.tensor_reduce(
                    out=T["p6_bmT"][:, mt * 4:(mt + 1) * 4],
                    in_=ctx.psum[6][:, 0:128].rearrange("p (f c) -> p f c", c=32),
                    axis=AXX, op=ALU.add), "vV", 1)
            ctx.vsync(v)
            ctx.inc(v.tensor_scalar(out=T["p6_bmT"][:], in0=T["p6_bmT"][:],
                                    scalar1=0.0, scalar2=None, op0=ALU.is_gt), "vV", 1)

        @blk.tensor
        def _(t):
            for mt in range(32):
                ctx.wait(t, "vV", c0["vV"] + 3 * mt + 2)
                if mt >= 1:
                    ctx.wait(t, "vV", c0["vV"] + 3 * mt)
                ctx.inc(t.matmul(ctx.psum[6][:, 0:128], T["p6_csb"][:], ident[:],
                                 start=True, stop=True), "tP", 1)

    assert ctx.cnt["vV"] == c0["vV"] + 3 * 32 + 1
    assert ctx.cnt["tP"] == c0["tP"] + 32

    c1 = dict(ctx.cnt)
    with nc.Block("p6b") as blk:
        @blk.gpsimd
        def _(g):
            for ksub in range(32):
                src_bm = ctx.scr["BMT"][ksub * 4:(ksub + 1) * 4, :]
                src_bm = src_bm.unsqueeze(1).to_broadcast([4, 32, 128])
                ctx.inc(g.dma_start(
                    out=T["p6_bme"][:, ksub * 128:(ksub + 1) * 128],
                    in_=src_bm), "qG", 16)
            for nt in range(32):
                bb = nt % 2
                if nt >= 2:
                    ctx.wait(g, "tP", c1["tP"] + 32 * (nt - 1))
                ctx.inc(g.dma_start(out=T[f"p6_w{bb}"][:],
                                    in_=ctx.inp["ab_w"][nt * 128:(nt + 1) * 128, :]),
                        f"gL{bb}", 16)
            ctx.wait(g, "qG", ctx.cnt["qG"])
            for q in ("gL0", "gL1"):
                ctx.wait(g, q, ctx.cnt[q])

        @blk.tensor
        def _(t):
            for nt in range(32):
                bb = nt % 2
                ctx.wait(t, f"gL{bb}", c1[f"gL{bb}"] + 16 * (nt // 2 + 1))
                for gq in range(8):
                    bank = ctx.psum[gq % 2]
                    gidx = nt * 8 + gq
                    if gidx >= 2:
                        prev = gidx - 2
                        ctx.wait(t, "vV", c1["vV"] + 4 * prev + 4)
                    for u4 in range(4):
                        ksub = gq * 4 + u4
                        ctx.inc(t.matmul(bank[:, u4 * 128:(u4 + 1) * 128],
                                         T[f"p6_w{bb}"][:, ksub * 128:(ksub + 1) * 128],
                                         ident[:], start=True, stop=True), "tP", 1)

        @blk.vector
        def _(v):
            v = VProxy(v, ctx)
            ctx.wait(v, "qG", c1["qG"] + 16 * 32)
            for nt in range(32):
                for gq in range(8):
                    bank = ctx.psum[gq % 2]
                    gidx = nt * 8 + gq
                    ctx.wait(v, "tP", c1["tP"] + 4 * gidx + 4)
                    if gidx >= 2:
                        ctx.wait(v, f"qU{gidx % 2}",
                                 c1[f"qU{gidx % 2}"] + 16 * 4 * (gidx // 2))
                    for u4 in range(4):
                        ksub = gq * 4 + u4
                        bm = T["p6_bme"][:, ksub * 128 + nt * 4:ksub * 128 + (nt + 1) * 4]
                        bm = bm.unsqueeze(2).to_broadcast([128, 4, 32])
                        ctx.inc(v.tensor_tensor(
                            out=T[f"p6_st{gidx % 2}"][:, u4 * 128:(u4 + 1) * 128]
                                .rearrange("p (a b) -> p a b", b=32),
                            in0=bank[:, u4 * 128:(u4 + 1) * 128]
                                .rearrange("p (a b) -> p a b", b=32),
                            in1=bm, op=ALU.mult), "vV", 1)

        @blk.sync
        def _(sy):
            for nt in range(32):
                for gq in range(8):
                    gidx = nt * 8 + gq
                    ctx.wait(sy, "vV", c1["vV"] + 4 * gidx + 4)
                    for u4 in range(4):
                        ksub = gq * 4 + u4
                        ctx.inc(sy.dma_start(
                            out=ctx.scr["WMT"][ksub, nt],
                            in_=T[f"p6_st{gidx % 2}"][:, u4 * 128:(u4 + 1) * 128]),
                            f"qU{gidx % 2}", 16)
            for q in ("qU0", "qU1"):
                ctx.wait(sy, q, ctx.cnt[q])

    assert ctx.cnt["tP"] == c1["tP"] + 32 * 32
    assert ctx.cnt["vV"] == c1["vV"] + 32 * 32

    c2 = dict(ctx.cnt)
    with nc.Block("p6c") as blk:
        @blk.sync
        def _(sy):
            for nw in range(8):
                for kc in range(32):
                    k = nw * 32 + kc
                    bb = k % 2
                    if k >= 2:
                        ctx.wait(sy, "tP", c2["tP"] + (k - 1))
                    ctx.inc(sy.dma_start(
                        out=T[f"p6_r{bb}"][:],
                        in_=ctx.scr["WMT"][kc, nw * 4:(nw + 1) * 4].rearrange(
                            "n p q -> p n q")), f"qL{bb}", 16)
                for tsub in range(QT):
                    k2 = nw * QT + tsub
                    bb2 = k2 % 2
                    if k2 >= 2:
                        ctx.wait(sy, "vV", c2["vV"] + k2 - 1)
                    ctx.inc(sy.dma_start(
                        out=T[f"p5_xr{bb2}"][:],
                        in_=ctx.scr["XRES"][tsub * 128:(tsub + 1) * 128,
                                            nw * 512:(nw + 1) * 512]),
                        f"qV{bb2}", 16)
                    ctx.wait(sy, "vV", c2["vV"] + k2 + 1)
                    ctx.inc(sy.dma_start(
                        out=ctx.scr["XRES"][tsub * 128:(tsub + 1) * 128,
                                            nw * 512:(nw + 1) * 512],
                        in_=T[f"p5_x2{bb2}"][:]), f"qU{bb2}", 16)
            for q in ("qL0", "qL1", "qU0", "qU1", "qV0", "qV1"):
                ctx.wait(sy, q, ctx.cnt[q])

        @blk.tensor
        def _(t):
            for nw in range(8):
                for kc in range(32):
                    k = nw * 32 + kc
                    bb = k % 2
                    ctx.wait(t, f"qL{bb}", c2[f"qL{bb}"] + 16 * (k // 2 + 1))
                    if kc == 0 and nw >= 1:
                        ctx.wait(t, "vV", c2["vV"] + QT * nw)
                    for tsub in range(QT):
                        mm = t.matmul(ctx.psum[tsub][:],
                                      T[src_name][:, kc * NQ + tsub * 128:
                                                  kc * NQ + (tsub + 1) * 128],
                                      T[f"p6_r{bb}"][:],
                                      start=(kc == 0), stop=(kc == 31))
                    ctx.inc(mm, "tP", 1)

        @blk.vector
        def _(v):
            v = VProxy(v, ctx)
            for nw in range(8):
                ctx.wait(v, "tP", c2["tP"] + 32 * (nw + 1))
                for tsub in range(QT):
                    k2 = nw * QT + tsub
                    bb2 = k2 % 2
                    ctx.wait(v, f"qV{bb2}", c2[f"qV{bb2}"] + 16 * (k2 // 2 + 1))
                    if k2 >= 2:
                        ctx.wait(v, f"qU{bb2}", c2[f"qU{bb2}"] + 16 * (k2 // 2))
                    ctx.inc(v.tensor_tensor(out=T[f"p5_x2{bb2}"][:],
                                            in0=ctx.psum[tsub][:],
                                            in1=T[f"p5_xr{bb2}"][:],
                                            op=ALU.add), "vV", 1)

    assert ctx.cnt["tP"] == c2["tP"] + 8 * 32
    assert ctx.cnt["vV"] == c2["vV"] + 8 * QT


def build_program(enabled=("p1",), taps=()):
    nc = bass.Bass("TRN2", target_bir_lowering=False, detect_race_conditions=False)
    ctx = Ctx(nc)

    def ext(name, shape, dt=F32):
        ctx.inp[name] = nc.dram_tensor(name, shape, dt, kind="ExternalInput").ap()

    ext("xb", [S, D]); ext("xq", [NQ, D])
    ext("q_a_w", [D, QR]); ext("q_a_norm_w", [QR]); ext("q_b_w", [QR, H * QKD])
    ext("kv_a_w", [D, KVR + ROPE]); ext("kv_a_norm_w", [KVR])
    ext("kv_b_w", [KVR, H * (NOPE + VD)])
    ext("o_w", [H * VD, D])
    ext("idx_wq_w", [D, IH * IDD]); ext("idx_wk_w", [D, IDD]); ext("idx_weights_w", [D, IH])
    ext("ab_w", [D, D]); ext("ab_mask", [D, D], I32)
    ext("w1", [HID * D // GRP // 2, GRP], I32); ext("s1", [HID * D // GRP])
    ext("w2", [HID * D // GRP // 2, GRP], I32); ext("s2", [HID * D // GRP])
    ext("w3", [HID * D // GRP // 2, GRP], I32); ext("s3", [HID * D // GRP])
    ext("ident_c", [128, 128], BF16)
    ext("qpos_c", [NQ]); ext("kpos_c", [S]); ext("invf2pi_c", [128, ROPE // 2])

    out_y = nc.dram_tensor("y", [NQ, D], F32, kind="ExternalOutput").ap()
    ctx.out_y = out_y

    def scr(name, shape, dt=BF16):
        ctx.scr[name] = nc.dram_tensor("scr_" + name, shape, dt, kind="Internal").ap()

    scr("W1T", [HB1, 128, 32 * 128]); scr("W3T", [HB1, 128, 32 * 128])
    scr("W2T", [8, HID, 512])
    scr("CKV", [S, KVR + ROPE], F32)
    scr("XNQT", [32, QT, 128, 128])
    scr("KnT", [H, NOPE, S]); scr("KpT", [ROPE, S]); scr("V", [H, S, VD])
    scr("QnT", [H, QT, NOPE, 128]); scr("QpT", [H, QT, ROPE, 128])
    scr("OT", [H, QT, VD, 128])
    scr("ACT", [HB1, 128, NQ])
    scr("XRES", [NQ, D], F32)
    scr("BMT", [128, 128], F32)
    scr("WMT", [32, 32, 128, 128])
    scr("MASKDUMP", [128, 36 * KBLK])

    def snap(tag):
        if "snaps" not in enabled:
            return
        out_t = nc.dram_tensor("snap_" + tag, [NQ, D], F32, kind="ExternalOutput").ap()
        with nc.Block("snap_" + tag) as blk:
            @blk.sync
            def _(sy):
                ctx.inc(sy.dma_start(out=out_t, in_=ctx.scr["XRES"]), "qS", 16)
                ctx.wait(sy, "qS", ctx.cnt["qS"])
    ctx.snap = snap

    for n in ("qS", "qG", "qC", "tP", "vV", "sS", "gP", "vB",
              "qL0", "qL1", "qT0", "qT1", "qU0", "qU1", "qV0", "qV1",
              "gL0", "gL1"):
        ctx.sem(n)

    ARENA_B = 200 * 1024
    arena_t = nc.alloc_sbuf_tensor("arena", [128, ARENA_B], U8)
    ctx.arena = Arena(arena_t, ARENA_B)
    A = ctx.arena
    T = ctx.tiles

    T["ident"] = A.alloc(128, BF16)
    T["qpos"] = A.alloc(QT, F32)
    T["kpos"] = A.alloc(TT, F32)
    T["invf2pi"] = A.alloc(ROPE // 2, F32)
    T["kcol"] = A.alloc(KBLK, F32)
    for nm in ("st_sum", "st_mean", "st_ms", "st_m2", "st_var", "st_std",
               "st_rstd", "st_ssq"):
        T[nm] = A.alloc(1, F32)
    T["st_xsq"] = A.alloc(D, F32)
    T["kcol_i"] = T["st_xsq"][:, 0:KBLK].bitcast(I32)
    T["eps"] = A.alloc(1, F32)
    for nm in ("rp_u", "rp_ni", "rp_nf"):
        T[nm] = A.alloc(2 * ROPE, I32 if nm == "rp_ni" else F32)
    for nm in ("rp_fs", "rp_fc", "rp_cos", "rp_sin", "rp_ta", "rp_tb"):
        T[nm] = A.alloc(ROPE // 2, F32)
    T["ikT"] = A.alloc(S, BF16)

    ctx.psum = [nc.alloc_psum_tensor(f"pb{i}", [128, 512], F32) for i in range(8)]

    with nc.Block("p0") as blk:
        @blk.sync
        def _(sy):
            ctx.inc(sy.dma_start(out=T["ident"][:], in_=ctx.inp["ident_c"]), "qS", 16)
            ctx.inc(sy.dma_start(out=T["qpos"][:],
                                 in_=ctx.inp["qpos_c"].rearrange("(j p) -> p j", p=128),
                                 allow_slow_non_contiguous=True), "qS", 16)
            ctx.inc(sy.dma_start(out=T["kpos"][:],
                                 in_=ctx.inp["kpos_c"].rearrange("(t p) -> p t", p=128),
                                 allow_slow_non_contiguous=True), "qS", 16)
            ctx.inc(sy.dma_start(out=T["invf2pi"][:], in_=ctx.inp["invf2pi_c"]), "qS", 16)
            ctx.wait(sy, "qS", ctx.cnt["qS"])

        @blk.gpsimd
        def _(g):
            ctx.inc(g.iota(T["kcol_i"][:], pattern=[[1, KBLK]], base=0,
                           channel_multiplier=0), "gP", 1)

        @blk.vector
        def _(v):
            v = VProxy(v, ctx)
            ctx.wait(v, "gP", 1)
            v.memset(T["eps"][:], EPS)
            v.memset(T["rp_u"][:, 32:64], 0.0)
            ctx.inc(v.tensor_copy(out=T["kcol"][:], in_=T["kcol_i"][:]), "vV", 1)

    A.push()
    NPB = 43
    msb_sh = A.alloc(NPB * 64, I32)
    lsb_sh = A.alloc(NPB * 64, I32)
    for bb in range(2):
        T[f"p1pk{bb}"] = A.alloc(NPB * 64, I32)
        T[f"p1sc{bb}"] = A.alloc(2 * NPB, F32)
        T[f"p1s8{bb}"] = A.alloc(2 * NPB, F32)
        T[f"p1msb{bb}"] = msb_sh
        T[f"p1lsb{bb}"] = lsb_sh
        T[f"p1wt{bb}"] = A.alloc(NPB * 128, BF16)
        T[f"p1st{bb}"] = A.alloc(NPB * 128, BF16)

    if "p1" in enabled:
        emit_p1(ctx)
    A.pop()

    if "p2a" in enabled:
        A.push()
        for bb in range(2):
            T[f"p2a_xb{bb}"] = A.alloc(D, F32)
            T[f"p2a_xn{bb}"] = A.alloc(D, BF16)
            T[f"p2a_xnt{bb}"] = A.alloc(D, BF16)
            T[f"p2a_ckv{bb}"] = A.alloc(KVR + ROPE, F32)
        T["p2a_kvw"] = A.alloc(32 * KVR, BF16)
        T["p2a_pik"] = A.alloc(32 * (ROPE + IDD), BF16)
        T["p2a_iktmp"] = A.alloc(128, BF16)
        emit_p2a(ctx)
        A.pop()

    if "p2b" in enabled:
        A.push()
        for bb in range(2):
            T[f"p2b_ckv{bb}"] = A.alloc(KVR + ROPE, F32)
            T[f"p2b_kv{bb}"] = A.alloc(H * (NOPE + VD), BF16)
            T[f"p2b_knt{bb}"] = A.alloc(D, BF16)
            T[f"p2b_kpt{bb}"] = A.alloc(128, BF16)
        T["p2b_kbw"] = A.alloc(4 * H * (NOPE + VD), BF16)
        T["p2b_normw"] = A.alloc(4, F32)
        T["p2b_kvx"] = A.alloc(KVR, BF16)
        T["p2b_latT"] = A.alloc(KVR, BF16)
        T["p2b_kr"] = A.alloc(ROPE, BF16)
        emit_p2b(ctx)
        A.pop()

    if "p2c" in enabled or "p2d" in enabled or "p3" in enabled:
        A.push()
        T["iqT"] = A.alloc(IH * NQ, BF16)
        T["iw_sb"] = A.alloc(QT * IH, F32)
        SUM_PREF = sum(min(j + 1, NKB) for j in range(QT)) * KBLK
        T["mask"] = A.alloc(SUM_PREF, BF16)

    if "p2c" in enabled:
        A.push()
        T["qlatT"] = A.alloc(12 * NQ, BF16)
        for bb in range(2):
            T[f"p2c_w{bb}"] = A.alloc(512, BF16)
            T[f"p2c_xc{bb}"] = A.alloc(NQ, BF16)
            T[f"qstage{bb}"] = A.alloc(512, BF16)
            T[f"xnt_st{bb}"] = A.alloc(D, BF16)
        T["normq"] = A.alloc(12, F32)
        T["qcos"] = A.alloc(QT * 32, F32)
        T["qsin"] = A.alloc(QT * 32, F32)
        T["qblk"] = A.alloc(384, BF16)
        T["qropea"] = A.alloc(64, BF16)
        T["qropeb"] = A.alloc(64, BF16)
        emit_p2c(ctx)
        A.pop()

    if "p2d" in enabled:
        A.push()
        T["idx_acc"] = A.alloc(S, F32)
        T["p2d_cmp"] = A.alloc(S, BF16)
        for bb in range(2):
            T[f"p2d_r{bb}"] = A.alloc(KBLK, F32)
        for nm in ("p2d_th", "p2d_t2", "p2d_mx", "p2d_lo", "p2d_hi", "p2d_mid",
                   "p2d_cnt", "p2d_b", "p2d_d", "p2d_e"):
            T[nm] = A.alloc(1, F32)
        T["p2d_t"] = A.alloc(KBLK, F32)
        emit_p2d(ctx)
        A.pop()
        if "dump_mask" in enabled:
            cdm = dict(ctx.cnt)
            with nc.Block("dumpmask") as blk:
                @blk.sync
                def _(sy):
                    ctx.inc(sy.dma_start(out=ctx.scr["MASKDUMP"],
                                         in_=T["mask"][:]), "qS", 16)
                    ctx.wait(sy, "qS", ctx.cnt["qS"])

    if "p3" in enabled:
        A.push()
        for bb in range(2):
            T[f"xkn{bb}"] = A.alloc(S, BF16)
            T[f"xv{bb}"] = A.alloc(S, BF16)
            T[f"xqn{bb}"] = A.alloc(QT * 128, BF16)
            T[f"xqp{bb}"] = A.alloc(QT * 128, BF16)
            T[f"ot_sb{bb}"] = A.alloc(128, BF16)
        T["xkp"] = A.alloc(S, BF16)
        for bb in range(2):
            T[f"p3_sm{bb}"] = A.alloc(KBLK, F32)
            T[f"p3_p{bb}"] = A.alloc(KBLK, BF16)
            for nm in ("p3_d", "p3_nm", "p3_al", "p3_lkb"):
                T[f"{nm}{bb}"] = A.alloc(1, F32)
        T["p3_pt"] = A.alloc(KBLK, BF16)
        for j in range(QT):
            T[f"p3_o{j}"] = A.alloc(VD, F32)
            T[f"p3_l{j}"] = A.alloc(1, F32)
            T[f"p3_m{j}"] = A.alloc(1, F32)
        T["p3_on"] = A.alloc(VD, BF16)
        for nm in ("p3_mkb", "p3_mnew", "p3_linv"):
            T[nm] = A.alloc(1, F32)
        emit_p3(ctx)
        A.pop()

    if "p2c" in enabled or "p2d" in enabled or "p3" in enabled:
        A.pop()

    if "p4" in enabled:
        A.push()
        T["p4_ow"] = A.alloc(32 * 1024, BF16)
        for bb in range(2):
            T[f"p4_ot{bb}"] = A.alloc(H * 128, BF16)
            T[f"p4_xq{bb}"] = A.alloc(1024, F32)
            T[f"p4_x1{bb}"] = A.alloc(1024, F32)
        emit_p4a(ctx)
        A.pop()
        ctx.snap("x1")

    def alloc_ln():
        for bb in range(2):
            T[f"p2c_xq{bb}"] = A.alloc(D, F32)
        T["p2c_xn"] = A.alloc(D, BF16)

    def alloc_p5():
        for bb in range(2):
            T[f"p5_w1{bb}"] = A.alloc(32 * 128, BF16)
            T[f"p5_w3{bb}"] = A.alloc(32 * 128, BF16)
            T[f"p5_act{bb}"] = A.alloc(NQ, BF16)
            T[f"p5_a{bb}"] = A.alloc(NQ, BF16)
            T[f"p5_w2{bb}"] = A.alloc(512, BF16)
            T[f"p5_xr{bb}"] = A.alloc(512, F32)
            T[f"p5_x2{bb}"] = A.alloc(512, F32)
        T["p5_sa"] = A.alloc(512, F32)

    if "p5" in enabled:
        A.push()
        T["xn2T"] = A.alloc(32 * NQ, BF16)
        alloc_ln()
        alloc_p5()
        emit_lnpass(ctx, "xn2T")
        emit_p5(ctx, "xn2T", final=False)
        A.pop()
        ctx.snap("x2")

    if "p6" in enabled:
        A.push()
        T["xn3T"] = A.alloc(32 * NQ, BF16)
        alloc_ln()
        T["p6_mk0"] = A.alloc(D, I32)
        for bb in range(2):
            T[f"p6_w{bb}"] = A.alloc(D, BF16)
            T[f"p6_st{bb}"] = A.alloc(512, BF16)
            T[f"p6_r{bb}"] = A.alloc(512, BF16)
            T[f"p5_xr{bb}"] = A.alloc(512, F32)
            T[f"p5_x2{bb}"] = A.alloc(512, F32)
        T["p6_cs"] = A.alloc(128, F32)
        T["p6_csb"] = A.alloc(128, BF16)
        T["p6_bmT"] = A.alloc(128, F32)
        T["p6_bme"] = A.alloc(32 * 128, BF16)
        emit_lnpass(ctx, "xn3T")
        emit_p6(ctx, "xn3T")
        A.pop()
        ctx.snap("x3")

    if "p7" in enabled:
        A.push()
        T["xn4T"] = A.alloc(32 * NQ, BF16)
        alloc_ln()
        alloc_p5()
        emit_lnpass(ctx, "xn4T")
        emit_p5(ctx, "xn4T", final=True)
        A.pop()

    tap_outs = {}
    for (tname, tshape, tdt, tsrc) in taps:
        tap_outs[tname] = nc.dram_tensor("tap_" + tname, tshape, tdt,
                                         kind="ExternalOutput").ap()
    with nc.Block("fin") as blk:
        if "p7" not in enabled:
            zt = A.alloc(D, F32)

            @blk.vector
            def _(v):
                ctx.inc(v.memset(zt[:], 0.0), "vV", 1)

        @blk.sync
        def _(sy):
            n = 0
            for (tname, tshape, tdt, tsrc) in taps:
                sy.dma_start(out=tap_outs[tname], in_=tsrc(ctx)).then_inc(
                    ctx.sems["qS"], 16)
                n += 1
            if "p7" in enabled:
                pass
            else:
                ctx.wait(sy, "vV", ctx.cnt["vV"])
                for i in range(QT):
                    sy.dma_start(out=out_y[i * 128:(i + 1) * 128, :],
                                 in_=zt[:]).then_inc(ctx.sems["qS"], 16)
                    n += 1
            sy.wait_ge(ctx.sems["qS"], ctx.cnt["qS"] + 16 * n)

    return nc, tap_outs


def make_in_maps(inputs):
    x = np.ascontiguousarray(inputs["L0_attn_a_hidden_states"])
    ident = np.eye(128, dtype=ml_dtypes.bfloat16)
    kpos = np.arange(S, dtype=np.float32)
    invf = (1.0 / THETA ** (np.arange(0, ROPE, 2, dtype=np.float32) / ROPE)) / (2 * np.pi)
    invf2pi = np.tile(invf[None, :], (128, 1)).astype(np.float32)
    ims = []
    for core in range(8):
        b, c = core // 4, core % 4
        im = {
            "xb": x[b],
            "xq": x[b, c::4, :],
            "q_a_w": inputs["L0_attn_a_q_a_w"],
            "q_a_norm_w": inputs["L0_attn_a_q_a_norm_w"],
            "q_b_w": inputs["L0_attn_a_q_b_w"],
            "kv_a_w": inputs["L0_attn_a_kv_a_w"],
            "kv_a_norm_w": inputs["L0_attn_a_kv_a_norm_w"],
            "kv_b_w": inputs["L0_attn_a_kv_b_w"],
            "o_w": inputs["L0_attn_a_o_w"],
            "idx_wq_w": inputs["L0_attn_a_idx_wq_w"],
            "idx_wk_w": inputs["L0_attn_a_idx_wk_w"],
            "idx_weights_w": inputs["L0_attn_a_idx_weights_w"],
            "ab_w": inputs["L0_attn_b_weight"],
            "ab_mask": inputs["L0_attn_b_mask"],
            "w1": inputs["L0_ffn_w1"], "s1": inputs["L0_ffn_s1"],
            "w2": inputs["L0_ffn_w2"], "s2": inputs["L0_ffn_s2"],
            "w3": inputs["L0_ffn_w3"], "s3": inputs["L0_ffn_s3"],
            "ident_c": ident,
            "qpos_c": np.arange(c, S, 4, dtype=np.float32),
            "kpos_c": kpos,
            "invf2pi_c": invf2pi,
        }
        ims.append({k: np.ascontiguousarray(v) for k, v in im.items()})
    return ims


ALL_PHASES = ("p1", "p2a", "p2b", "p2c", "p2d", "p3", "p4", "p5", "p6", "p7")


def kernel(**inputs):
    ims = make_in_maps(inputs)
    nc, _ = build_program(enabled=ALL_PHASES)
    res = run_bass_kernel_spmd(nc, ims, core_ids=list(range(8)))
    out = np.empty((B, S, D), np.float32)
    for core in range(8):
        b, c = core // 4, core % 4
        out[b, c::4, :] = np.asarray(res.results[core]["y"], np.float32)
    return out


# version 06 (final)
# speedup vs baseline: 1.3655x; 1.0320x over previous
import numpy as np
import ml_dtypes
import concourse.bass as bass
import concourse.mybir as mybir
from concourse.bass_utils import run_bass_kernel_spmd

F32 = mybir.dt.float32
BF16 = mybir.dt.bfloat16
I32 = mybir.dt.int32
U8 = mybir.dt.uint8
AF = mybir.ActivationFunctionType
ALU = mybir.AluOpType
AXX = mybir.AxisListType.X

B, S, D = 2, 4096, 4096
NQ = 1024
H, NOPE, ROPE, VD = 32, 128, 64, 128
QKD = NOPE + ROPE
QR, KVR = 1536, 512
IH, IDD, TOPK = 16, 128, 1024
HID = 11008
GRP = 64
EPS = 1e-5
THETA = 1e4
NEG_BIG = -1.0e30
NEG_CAUSAL = -1.0e38
ISQ_QKD = float(QKD) ** -0.5
IW_SCALE = float(IH) ** -0.5 * float(IDD) ** -0.5

QT = NQ // 128
TT = S // 128
KBLK = 512
NKB = S // KBLK
HB1 = HID // 128
DB2 = D // 128
NP1 = D // 128
NP2 = HID // 128


VOPS = {"tensor_copy", "tensor_tensor", "tensor_scalar", "tensor_add",
        "tensor_sub", "tensor_mul", "tensor_max", "tensor_relu",
        "tensor_single_scalar", "tensor_reduce", "reduce_sum", "reduce_max",
        "scalar_tensor_tensor", "tensor_tensor_scan", "memset", "memzero",
        "select", "copy_predicated", "reciprocal", "transpose",
        "stream_shuffle", "max", "max_index", "bn_stats", "bn_aggr"}


class VProxy:

    def __init__(self, eng, ctx):
        self._e = eng
        self._c = ctx

    def __getattr__(self, name):
        f = getattr(self._e, name)
        if not callable(f) or name not in VOPS:
            return f

        def wrap(*a, **kw):
            self._c.vsync(self._e)
            r = f(*a, **kw)
            self._c.last_v = r
            self._c.last_v_inc = None
            return r
        return wrap


class Arena:
    def __init__(self, tensor, nbytes):
        self.t = tensor
        self.nbytes = nbytes
        self.off = 0
        self.marks = []

    def alloc(self, cols, dtype):
        size = {F32: 4, I32: 4, BF16: 2, U8: 1}[dtype]
        nb = cols * size
        self.off = (self.off + 63) // 64 * 64
        assert self.off + nb <= self.nbytes, \
            f"SBUF arena overflow: {self.off}+{nb} > {self.nbytes}"
        v = self.t[:, self.off:self.off + nb].bitcast(dtype)
        self.off += nb
        return v

    def push(self):
        self.marks.append(self.off)

    def pop(self):
        self.off = self.marks.pop()


class Ctx:
    def __init__(self, nc):
        self.nc = nc
        self.sems = {}
        self.cnt = {}
        self.tiles = {}
        self.scr = {}
        self.inp = {}
        self.psum = []
        self.arena = None
        self.last_v = None
        self.last_v_inc = None

    def sem(self, name):
        s = self.nc.alloc_semaphore(name)
        self.sems[name] = s
        self.cnt[name] = 0
        return s

    def inc(self, instr, sem, n=1):
        instr.then_inc(self.sems[sem], n)
        self.cnt[sem] += n
        if instr is self.last_v:
            self.last_v_inc = (sem, self.cnt[sem])
        return instr

    def wait(self, eng, sem, val):
        eng.wait_ge(self.sems[sem], val)

    def vsync(self, v):
        if self.last_v is None:
            return
        if self.last_v_inc is not None:
            sem, val = self.last_v_inc
            self.last_v = None
            self.last_v_inc = None
            v.wait_ge(self.sems[sem], val)
            return
        self.cnt["vB"] += 1
        self.last_v.then_inc(self.sems["vB"], 1)
        self.last_v = None
        self.last_v_inc = None
        v.wait_ge(self.sems["vB"], self.cnt["vB"])


def emit_p1(ctx):
    nc = ctx.nc
    T = ctx.tiles
    jobs = []
    for rt in range(HB1):
        jobs.append(("w1", rt, 0, NP1, "W1T"))
    for rt in range(HB1):
        jobs.append(("w3", rt, 0, NP1, "W3T"))
    for rt in range(DB2):
        jobs.append(("w2", rt, 0, 43, "W2T"))
        jobs.append(("w2", rt, 43, 43, "W2T"))
    njobs = len(jobs)

    ngroups = [(j[3] + 3) // 4 for j in jobs]
    g_before = np.concatenate([[0], np.cumsum(ngroups)]).astype(int)
    tp_before = np.concatenate([[0], np.cumsum([j[3] for j in jobs])]).astype(int)
    st_w13 = [1 if j[4] != "W2T" else 0 for j in jobs]
    st_w2 = [0 if j[4] != "W2T" else 43 for j in jobs]
    st_slot_w13 = [[0], [0]]
    st_slot_w2 = [[0], [0]]
    for k in range(len(jobs)):
        for bb in range(2):
            st_slot_w13[bb].append(st_slot_w13[bb][-1] + (st_w13[k] if k % 2 == bb else 0))
            st_slot_w2[bb].append(st_slot_w2[bb][-1] + (st_w2[k] if k % 2 == bb else 0))

    c0 = dict(ctx.cnt)

    with nc.Block("p1") as blk:
        @blk.sync
        def _(sy):
            def w2_store(k):
                key, rt, po, npairs, out = jobs[k]
                if out != "W2T":
                    return
                bb = k % 2
                st = T[f"p1st{bb}"]
                ctx.wait(sy, "sS", c0["sS"] + int(g_before[k + 1]))
                w2t = ctx.scr["W2T"]
                for hc in range(npairs):
                    hg = po + hc
                    dwin, dsub = rt // 4, rt % 4
                    ctx.inc(sy.dma_start(
                        out=w2t[dwin, hg * 128:(hg + 1) * 128,
                                dsub * 128:(dsub + 1) * 128],
                        in_=st[:, hc * 128:(hc + 1) * 128]), f"qV{bb}", 16)

            for k, (key, rt, po, npairs, out) in enumerate(jobs):
                bb = k % 2
                if k >= 2:
                    ctx.wait(sy, "vV", c0["vV"] + 5 * (k - 1))
                ntot = 32 if out != "W2T" else 86
                pv = ctx.inp[key].rearrange("(r n) c -> r (n c)", n=ntot)
                sv = ctx.inp[key.replace("w", "s")].rearrange(
                    "(r g) -> r g", g=2 * ntot)
                ctx.inc(sy.dma_start(out=T[f"p1pk{bb}"][:, 0:npairs * 64],
                                     in_=pv[rt * 128:(rt + 1) * 128,
                                            po * 64:(po + npairs) * 64]),
                        f"qL{bb}", 16)
                ctx.inc(sy.dma_start(out=T[f"p1sc{bb}"][:, 0:2 * npairs],
                                     in_=sv[rt * 128:(rt + 1) * 128,
                                            2 * po:2 * (po + npairs)]),
                        f"qL{bb}", 16)
                if k >= 1:
                    w2_store(k - 1)
            w2_store(len(jobs) - 1)
            for q in ("qL0", "qL1", "qV0", "qV1"):
                ctx.wait(sy, q, ctx.cnt[q])

        @blk.vector
        def _(v):
            v = VProxy(v, ctx)
            for k, (key, rt, po, npairs, out) in enumerate(jobs):
                bb = k % 2
                pk, sc = T[f"p1pk{bb}"], T[f"p1sc{bb}"]
                s8, msb, lsb, wt = T[f"p1s8{bb}"], T[f"p1msb{bb}"], T[f"p1lsb{bb}"], T[f"p1wt{bb}"]
                nco = npairs * 64
                ctx.wait(v, f"qL{bb}", c0[f"qL{bb}"] + 32 * (k // 2 + 1))
                if k >= 2:
                    ctx.wait(v, "tP", c0["tP"] + int(tp_before[k - 1]))
                v.tensor_scalar(out=msb[:, 0:nco], in0=pk[:, 0:nco], scalar1=4,
                                scalar2=None, op0=ALU.arith_shift_right)
                v.tensor_scalar(out=lsb[:, 0:nco], in0=pk[:, 0:nco], scalar1=15,
                                scalar2=8, op0=ALU.bitwise_and, op1=ALU.bitwise_xor)
                ctx.inc(v.tensor_scalar(out=lsb[:, 0:nco], in0=lsb[:, 0:nco],
                                        scalar1=8, scalar2=None, op0=ALU.subtract),
                        "vV", 3)
                wt4 = wt[:, 0:2 * nco].rearrange("p (j two ci) -> p j two ci", two=2, ci=64)
                m4 = msb[:, 0:nco].rearrange("p (j one ci) -> p j one ci", one=1, ci=64)
                l4 = lsb[:, 0:nco].rearrange("p (j one ci) -> p j one ci", one=1, ci=64)
                sc3 = sc[:, 0:2 * npairs].rearrange("p (j two) -> p j two", two=2)
                se = sc3[:, :, 0:1].unsqueeze(3).to_broadcast([128, npairs, 1, 64])
                so = sc3[:, :, 1:2].unsqueeze(3).to_broadcast([128, npairs, 1, 64])
                ctx.inc(v.tensor_tensor(out=wt4[:, :, 0:1, :], in0=m4, in1=se,
                                        op=ALU.mult), "vV", 1)
                ctx.inc(v.tensor_tensor(out=wt4[:, :, 1:2, :], in0=l4, in1=so,
                                        op=ALU.mult), "vV", 1)

        @blk.tensor
        def _(t):
            ident = T["ident"]
            for k, (key, rt, po, npairs, out) in enumerate(jobs):
                bb = k % 2
                wt = T[f"p1wt{bb}"]
                ctx.wait(t, "vV", c0["vV"] + 5 * (k + 1))
                for g4 in range(ngroups[k]):
                    gidx = int(g_before[k]) + g4
                    bank = ctx.psum[gidx % 2]
                    if gidx >= 2:
                        ctx.wait(t, "sS", c0["sS"] + gidx - 1)
                    nin = min(4, npairs - g4 * 4)
                    for i in range(nin):
                        j = g4 * 4 + i
                        ctx.inc(t.matmul(bank[:, i * 128:(i + 1) * 128],
                                         wt[:, j * 128:(j + 1) * 128],
                                         ident[:], start=True, stop=True), "tP", 1)

        @blk.scalar
        def _(s):
            for k, (key, rt, po, npairs, out) in enumerate(jobs):
                bb = k % 2
                st = T[f"p1st{bb}"]
                for g4 in range(ngroups[k]):
                    gidx = int(g_before[k]) + g4
                    bank = ctx.psum[gidx % 2]
                    nin = min(4, npairs - g4 * 4)
                    ctx.wait(s, "tP", c0["tP"] + int(tp_before[k]) + g4 * 4 + nin)
                    if k >= 2 and g4 == 0:
                        ctx.wait(s, f"qT{bb}",
                                 c0[f"qT{bb}"] + 16 * st_slot_w13[bb][k - 1])
                        ctx.wait(s, f"qV{bb}",
                                 c0[f"qV{bb}"] + 16 * st_slot_w2[bb][k - 1])
                    ctx.inc(s.activation(out=st[:, g4 * 512:g4 * 512 + nin * 128],
                                         in_=bank[:, 0:nin * 128], func=AF.Copy), "sS", 1)

        @blk.gpsimd
        def _(g):
            for k, (key, rt, po, npairs, out) in enumerate(jobs):
                bb = k % 2
                st = T[f"p1st{bb}"]
                if out == "W2T":
                    continue
                ctx.wait(g, "sS", c0["sS"] + int(g_before[k + 1]))
                ctx.inc(g.dma_start(out=ctx.scr[out][rt],
                                    in_=st[:, 0:npairs * 128]), f"qT{bb}", 16)
            ctx.wait(g, "qT0", ctx.cnt["qT0"])
            ctx.wait(g, "qT1", ctx.cnt["qT1"])


    assert ctx.cnt["vV"] == c0["vV"] + 5 * njobs
    assert ctx.cnt["tP"] == c0["tP"] + int(tp_before[-1])
    assert ctx.cnt["sS"] == c0["sS"] + int(g_before[-1])
    assert (ctx.cnt["qT0"] - c0["qT0"]) + (ctx.cnt["qT1"] - c0["qT1"]) == 16 * sum(st_w13)
    assert (ctx.cnt["qV0"] - c0["qV0"]) + (ctx.cnt["qV1"] - c0["qV1"]) == 16 * sum(st_w2)


def rope_frac_vec(ctx, v, pos_col, pref):
    T = ctx.tiles
    u, ni, nf = T[pref + "_u"], T[pref + "_ni"], T[pref + "_nf"]

    def frac_turn(dst):
        ctx.vsync(v)
        v.tensor_copy(out=ni[:, 0:64], in_=u[:, 0:64])
        ctx.vsync(v)
        v.tensor_copy(out=nf[:, 0:64], in_=ni[:, 0:64])
        ctx.vsync(v)
        v.tensor_tensor(out=dst, in0=u[:, 0:32], in1=nf[:, 0:32], op=ALU.subtract)
        ctx.vsync(v)
        v.tensor_scalar(out=nf[:, 0:32], in0=dst, scalar1=0.5, scalar2=None,
                        op0=ALU.is_gt)
        ctx.inc(v.tensor_tensor(out=dst, in0=dst, in1=nf[:, 0:32], op=ALU.subtract),
                "vV", 1)

    v.tensor_scalar(out=u[:, 0:32], in0=T["invf2pi"][:], scalar1=pos_col,
                    scalar2=None, op0=ALU.mult)
    frac_turn(T[pref + "_fs"][:])
    v.tensor_scalar(out=u[:, 0:32], in0=u[:, 0:32], scalar1=0.25, scalar2=None,
                    op0=ALU.add)
    frac_turn(T[pref + "_fc"][:])


def rope_sin_scalar(ctx, s, pref):
    T = ctx.tiles
    TWO_PI = float(2.0 * np.pi)
    ctx.inc(s.activation(out=T[pref + "_sin"][:], in_=T[pref + "_fs"][:],
                         func=AF.Sin, scale=TWO_PI), "sS", 1)
    ctx.inc(s.activation(out=T[pref + "_cos"][:], in_=T[pref + "_fc"][:],
                         func=AF.Sin, scale=TWO_PI), "sS", 1)


def rope_apply_vec(ctx, v, src_ap, dst_ap, pref, nfreq):
    T = ctx.tiles
    x0 = src_ap.rearrange("p (i two) -> p i two", two=2)[:, :, 0:1]
    x1 = src_ap.rearrange("p (i two) -> p i two", two=2)[:, :, 1:2]
    dr = dst_ap.rearrange("p (i two) -> p i two", two=2)
    c3 = T[pref + "_cos"][:, 0:nfreq].unsqueeze(2)
    s3 = T[pref + "_sin"][:, 0:nfreq].unsqueeze(2)
    ta = T[pref + "_ta"][:, 0:nfreq].unsqueeze(2)
    tb = T[pref + "_tb"][:, 0:nfreq].unsqueeze(2)
    v.tensor_tensor(out=ta, in0=x0, in1=c3, op=ALU.mult)
    v.tensor_tensor(out=tb, in0=x1, in1=s3, op=ALU.mult)
    ctx.inc(v.tensor_tensor(out=dr[:, :, 0:1], in0=ta, in1=tb, op=ALU.subtract), "vV", 1)
    v.tensor_tensor(out=ta, in0=x1, in1=c3, op=ALU.mult)
    v.tensor_tensor(out=tb, in0=x0, in1=s3, op=ALU.mult)
    ctx.inc(v.tensor_tensor(out=dr[:, :, 1:2], in0=ta, in1=tb, op=ALU.add), "vV", 1)


def emit_p2a(ctx):
    nc = ctx.nc
    T = ctx.tiles
    c0 = dict(ctx.cnt)
    nD = D // 128

    with nc.Block("p2a") as blk:
        @blk.gpsimd
        def _(g):
            kvw = ctx.inp["kv_a_w"].rearrange("(c p) f -> c p f", p=128)
            ikw = ctx.inp["idx_wk_w"].rearrange("(c p) f -> c p f", p=128)
            for c in range(nD):
                ctx.inc(g.dma_start(out=T["p2a_kvw"][:, c * 512:(c + 1) * 512],
                                    in_=kvw[c][:, 0:512]), "qG", 16)
                ctx.inc(g.dma_start(out=T["p2a_pik"][:, c * 192:c * 192 + 64],
                                    in_=kvw[c][:, 512:576]), "qG", 16)
                ctx.inc(g.dma_start(out=T["p2a_pik"][:, c * 192 + 64:(c + 1) * 192],
                                    in_=ikw[c]), "qG", 16)
            ctx.wait(g, "qG", ctx.cnt["qG"])

        @blk.sync
        def _(sy):
            for i in range(TT):
                bb = i % 2
                if i >= 2:
                    ctx.wait(sy, "vV", c0["vV"] + 5 * (i - 1))
                    ctx.wait(sy, "sS", c0["sS"] + 11 * (i - 1))
                ctx.inc(sy.dma_start(out=T[f"p2a_xb{bb}"][:],
                                     in_=ctx.inp["xb"][i * 128:(i + 1) * 128, :]),
                        f"qL{bb}", 16)
                ctx.wait(sy, "vV", c0["vV"] + 5 * (i + 1))
                ctx.inc(sy.dma_start(out=ctx.scr["CKV"][i * 128:(i + 1) * 128, :],
                                     in_=T[f"p2a_ckv{bb}"][:]), f"qU{bb}", 16)
            for q in ("qL0", "qL1", "qU0", "qU1"):
                ctx.wait(sy, q, ctx.cnt[q])

        @blk.vector
        def _(v):
            v = VProxy(v, ctx)
            for i in range(TT):
                bb = i % 2
                xb = T[f"p2a_xb{bb}"]
                ctx.wait(v, f"qL{bb}", c0[f"qL{bb}"] + 16 * (i // 2 + 1))
                v.reduce_sum(out=T["st_sum"][:], in_=xb[:], axis=AXX)
                v.tensor_scalar(out=T["st_mean"][:], in0=T["st_sum"][:],
                                scalar1=1.0 / D, scalar2=None, op0=ALU.mult)
                ctx.wait(v, "sS", c0["sS"] + 11 * i + 1)
                v.tensor_scalar(out=T["st_ms"][:], in0=T["st_ssq"][:],
                                scalar1=1.0 / D, scalar2=None, op0=ALU.mult)
                v.tensor_tensor(out=T["st_m2"][:], in0=T["st_mean"][:],
                                in1=T["st_mean"][:], op=ALU.mult)
                ctx.inc(v.tensor_tensor(out=T["st_var"][:], in0=T["st_ms"][:],
                                        in1=T["st_m2"][:], op=ALU.subtract), "vV", 1)
                ctx.wait(v, "sS", c0["sS"] + 11 * i + 2)
                v.reciprocal(out=T["st_rstd"][:], in_=T["st_std"][:])
                ctx.inc(v.tensor_scalar(out=T[f"p2a_xn{bb}"][:], in0=xb[:],
                                        scalar1=T["st_mean"][:], scalar2=T["st_rstd"][:],
                                        op0=ALU.subtract, op1=ALU.mult), "vV", 1)
                ctx.wait(v, "tP", c0["tP"] + 34 * i + 33)
                if i >= 2:
                    ctx.wait(v, f"qU{bb}", c0[f"qU{bb}"] + 16 * (i // 2))
                ctx.inc(v.tensor_copy(out=T[f"p2a_ckv{bb}"][:, 0:512],
                                      in_=ctx.psum[0][:]), "vV", 1)
                ctx.inc(v.tensor_copy(out=T[f"p2a_ckv{bb}"][:, 512:576],
                                      in_=ctx.psum[1][:, 0:64]), "vV", 1)
                ctx.inc(v.tensor_copy(out=T["p2a_iktmp"][:],
                                      in_=ctx.psum[1][:, 64:192]), "vV", 1)

        @blk.scalar
        def _(s):
            for i in range(TT):
                bb = i % 2
                xb = T[f"p2a_xb{bb}"]
                ctx.wait(s, f"qL{bb}", c0[f"qL{bb}"] + 16 * (i // 2 + 1))
                ctx.inc(s.activation(out=T["st_xsq"][:], in_=xb[:], func=AF.Square,
                                     accum_out=T["st_ssq"][:]), "sS", 1)
                ctx.wait(s, "vV", c0["vV"] + 5 * i + 1)
                ctx.inc(s.activation(out=T["st_std"][:], in_=T["st_var"][:],
                                     func=AF.Sqrt, bias=T["eps"][:]), "sS", 1)
                for gq in range(8):
                    bank = ctx.psum[2 + (gq % 2)]
                    ctx.wait(s, "tP", c0["tP"] + 34 * i + 4 * (gq + 1))
                    ctx.inc(s.activation(out=T[f"p2a_xnt{bb}"][:, gq * 512:(gq + 1) * 512],
                                         in_=bank[:], func=AF.Copy), "sS", 1)
                ctx.wait(s, "tP", c0["tP"] + 34 * (i + 1))
                ctx.inc(s.activation(out=T["ikT"][:, i * 128:(i + 1) * 128],
                                     in_=ctx.psum[4][:, 0:128], func=AF.Copy), "sS", 1)

        @blk.tensor
        def _(t):
            ident = T["ident"]
            ctx.wait(t, "qG", c0["qG"] + 16 * 3 * nD)
            for i in range(TT):
                bb = i % 2
                ctx.wait(t, "vV", c0["vV"] + 5 * i + 2)
                for gq in range(8):
                    bank = ctx.psum[2 + (gq % 2)]
                    gidx = i * 8 + gq
                    if gidx >= 2:
                        prev = gidx - 2
                        ctx.wait(t, "sS", c0["sS"] + 11 * (prev // 8) + 2 + (prev % 8) + 1)
                    for u4 in range(4):
                        cch = gq * 4 + u4
                        ctx.inc(t.matmul(bank[:, u4 * 128:(u4 + 1) * 128],
                                         T[f"p2a_xn{bb}"][:, cch * 128:(cch + 1) * 128],
                                         ident[:], start=True, stop=True), "tP", 1)
                if i >= 1:
                    ctx.wait(t, "vV", c0["vV"] + 5 * i)
                for c in range(nD):
                    ctx.wait(t, "sS", c0["sS"] + 11 * i + 2 + c // 4 + 1)
                    lhs = T[f"p2a_xnt{bb}"][:, c * 128:(c + 1) * 128]
                    t.matmul(ctx.psum[0][:], lhs,
                             T["p2a_kvw"][:, c * 512:(c + 1) * 512],
                             start=(c == 0), stop=(c == nD - 1), skip_group_check=True)
                    mm = t.matmul(ctx.psum[1][:, 0:192], lhs,
                                  T["p2a_pik"][:, c * 192:(c + 1) * 192],
                                  start=(c == 0), stop=(c == nD - 1),
                                  skip_group_check=True)
                ctx.inc(mm, "tP", 1)
                ctx.wait(t, "vV", c0["vV"] + 5 * (i + 1))
                ctx.inc(t.matmul(ctx.psum[4][:, 0:128], T["p2a_iktmp"][:], ident[:],
                                 start=True, stop=True), "tP", 1)

    assert ctx.cnt["tP"] == c0["tP"] + 34 * TT
    assert ctx.cnt["vV"] == c0["vV"] + 5 * TT
    assert ctx.cnt["sS"] == c0["sS"] + 11 * TT


def emit_p2b(ctx):
    nc = ctx.nc
    T = ctx.tiles
    c0 = dict(ctx.cnt)
    NKV = 16
    TP_T = 53

    with nc.Block("p2b") as blk:
        @blk.gpsimd
        def _(g):
            kbw = ctx.inp["kv_b_w"].rearrange("(c p) f -> c p f", p=128)
            for c in range(4):
                ctx.inc(g.dma_start(out=T["p2b_kbw"][:, c * 8192:(c + 1) * 8192],
                                    in_=kbw[c]), "qG", 16)
            ctx.inc(g.dma_start(out=T["p2b_normw"][:],
                                in_=ctx.inp["kv_a_norm_w"].rearrange("(c p) -> p c", p=128),
                                allow_slow_non_contiguous=True), "qG", 16)
            ctx.wait(g, "qG", ctx.cnt["qG"])

        @blk.vector
        def _(v):
            v = VProxy(v, ctx)
            ctx.wait(v, "qG", c0["qG"] + 16 * 5)
            for c in range(4):
                ctx.inc(v.tensor_scalar(out=T["p2b_kbw"][:, c * 8192:(c + 1) * 8192],
                                        in0=T["p2b_kbw"][:, c * 8192:(c + 1) * 8192],
                                        scalar1=T["p2b_normw"][:, c:c + 1], scalar2=None,
                                        op0=ALU.mult), "vV", 1)
            P = c0["vV"] + 4
            for i in range(TT):
                bb = i % 2
                ckv = T[f"p2b_ckv{bb}"]
                ctx.wait(v, f"qL{bb}", c0[f"qL{bb}"] + 16 * (i // 2 + 1))
                ctx.wait(v, "sS", c0["sS"] + 30 * i + 1)
                ctx.inc(v.tensor_scalar(out=T["st_ms"][:], in0=T["st_ssq"][:],
                                        scalar1=1.0 / KVR, scalar2=None,
                                        op0=ALU.mult), "vV", 1)
                ctx.wait(v, "sS", c0["sS"] + 30 * i + 2)
                v.reciprocal(out=T["st_rstd"][:], in_=T["st_std"][:])
                ctx.vsync(v)
                if i >= 2:
                    ctx.wait(v, "tP", c0["tP"] + TP_T * (i - 2) + 4)
                ctx.inc(v.tensor_scalar(out=T["p2b_kvx"][:], in0=ckv[:, 0:512],
                                        scalar1=T["st_rstd"][:], scalar2=None,
                                        op0=ALU.mult), "vV", 1)
                rope_frac_vec(ctx, v, T["kpos"][:, i:i + 1], "rp")
                ctx.wait(v, "sS", c0["sS"] + 30 * i + 4)
                if i >= 1:
                    ctx.wait(v, "tP", c0["tP"] + TP_T * i)
                rope_apply_vec(ctx, v, ckv[:, 512:576], T["p2b_kr"][:], "rp", 32)

        @blk.scalar
        def _(s):
            for i in range(TT):
                bb = i % 2
                ckv = T[f"p2b_ckv{bb}"]
                P = c0["vV"] + 4
                ctx.wait(s, f"qL{bb}", c0[f"qL{bb}"] + 16 * (i // 2 + 1))
                ctx.inc(s.activation(out=T["st_xsq"][:, 0:KVR], in_=ckv[:, 0:512],
                                     func=AF.Square, accum_out=T["st_ssq"][:]), "sS", 1)
                ctx.wait(s, "vV", P + 6 * i + 1)
                ctx.inc(s.activation(out=T["st_std"][:], in_=T["st_ms"][:],
                                     func=AF.Sqrt, bias=T["eps"][:]), "sS", 1)
                ctx.wait(s, "vV", P + 6 * i + 4)
                rope_sin_scalar(ctx, s, "rp")
                ctx.wait(s, "tP", c0["tP"] + TP_T * i + 4)
                ctx.inc(s.activation(out=T["p2b_latT"][:], in_=ctx.psum[5][:],
                                     func=AF.Copy), "sS", 1)
                if i >= 2:
                    ctx.wait(s, f"qU{bb}", c0[f"qU{bb}"] + 16 * 65 * (i // 2))
                kvsb = T[f"p2b_kv{bb}"]
                for n in range(NKV):
                    bank = ctx.psum[n % 2]
                    ctx.wait(s, "tP", c0["tP"] + TP_T * i + 4 + (n + 1))
                    ctx.inc(s.activation(out=kvsb[:, n * 512:(n + 1) * 512],
                                         in_=bank[:], func=AF.Copy), "sS", 1)
                for gq in range(8):
                    bank = ctx.psum[2 + (gq % 2)]
                    ctx.wait(s, "tP", c0["tP"] + TP_T * i + 20 + 4 * (gq + 1))
                    ctx.inc(s.activation(out=T[f"p2b_knt{bb}"][:, gq * 512:(gq + 1) * 512],
                                         in_=bank[:], func=AF.Copy), "sS", 1)
                ctx.wait(s, "tP", c0["tP"] + TP_T * (i + 1))
                ctx.inc(s.activation(out=T[f"p2b_kpt{bb}"][0:64, 0:128],
                                     in_=ctx.psum[4][0:64, 0:128], func=AF.Copy), "sS", 1)

        @blk.tensor
        def _(t):
            ident = T["ident"]
            ctx.wait(t, "vV", c0["vV"] + 4)
            P = c0["vV"] + 4
            for i in range(TT):
                bb = i % 2
                ctx.wait(t, "vV", P + 6 * i + 2)
                if i >= 1:
                    ctx.wait(t, "sS", c0["sS"] + 30 * (i - 1) + 5)
                for u4 in range(4):
                    ctx.inc(t.matmul(ctx.psum[5][:, u4 * 128:(u4 + 1) * 128],
                                     T["p2b_kvx"][:, u4 * 128:(u4 + 1) * 128],
                                     ident[:], start=True, stop=True), "tP", 1)
                ctx.wait(t, "sS", c0["sS"] + 30 * i + 5)
                for n in range(NKV):
                    bank = ctx.psum[n % 2]
                    if n >= 2:
                        ctx.wait(t, "sS", c0["sS"] + 30 * i + 5 + (n - 1))
                    for c in range(4):
                        mm = t.matmul(bank[:],
                                      T["p2b_latT"][:, c * 128:(c + 1) * 128],
                                      T["p2b_kbw"][:, c * 8192 + n * 512:c * 8192 + (n + 1) * 512],
                                      start=(c == 0), stop=(c == 3))
                    ctx.inc(mm, "tP", 1)
                for gq in range(8):
                    bank = ctx.psum[2 + (gq % 2)]
                    gidx = i * 8 + gq
                    if gidx >= 2:
                        prev = gidx - 2
                        ctx.wait(t, "sS", c0["sS"] + 30 * (prev // 8) + 21 + (prev % 8) + 1)
                    for u4 in range(4):
                        hh = gq * 4 + u4
                        ctx.wait(t, "sS", c0["sS"] + 30 * i + 5 + hh // 2 + 1)
                        ctx.inc(t.matmul(bank[:, u4 * 128:(u4 + 1) * 128],
                                         T[f"p2b_kv{bb}"][:, 256 * hh:256 * hh + 128],
                                         ident[:], start=True, stop=True), "tP", 1)
                ctx.wait(t, "vV", P + 6 * (i + 1))
                if i >= 1:
                    ctx.wait(t, "sS", c0["sS"] + 30 * i)
                ctx.inc(t.matmul(ctx.psum[4][0:64, 0:128], T["p2b_kr"][:], ident[:],
                                 start=True, stop=True), "tP", 1)

        @blk.sync
        def _(sy):
            for i in range(TT):
                bb = i % 2
                if i >= 2:
                    ctx.wait(sy, "sS", c0["sS"] + 30 * (i - 1))
                    ctx.wait(sy, "vV", c0["vV"] + 4 + 6 * (i - 1))
                ctx.inc(sy.dma_start(out=T[f"p2b_ckv{bb}"][:],
                                     in_=ctx.scr["CKV"][i * 128:(i + 1) * 128, :]),
                        f"qL{bb}", 16)
                ctx.wait(sy, "sS", c0["sS"] + 30 * i + 21)
                for hh in range(H):
                    ctx.inc(sy.dma_start(
                        out=ctx.scr["V"][hh, i * 128:(i + 1) * 128, :],
                        in_=T[f"p2b_kv{bb}"][:, 256 * hh + 128:256 * (hh + 1)]),
                        f"qU{bb}", 16)
                ctx.wait(sy, "sS", c0["sS"] + 30 * i + 29)
                for hh in range(H):
                    ctx.inc(sy.dma_start(
                        out=ctx.scr["KnT"][hh, :, i * 128:(i + 1) * 128],
                        in_=T[f"p2b_knt{bb}"][:, hh * 128:(hh + 1) * 128]),
                        f"qU{bb}", 16)
                ctx.wait(sy, "sS", c0["sS"] + 30 * (i + 1))
                ctx.inc(sy.dma_start(
                    out=ctx.scr["KpT"][:, i * 128:(i + 1) * 128],
                    in_=T[f"p2b_kpt{bb}"][0:64, 0:128]), f"qU{bb}", 16)
            for q in ("qL0", "qL1", "qU0", "qU1"):
                ctx.wait(sy, q, ctx.cnt[q])

    assert ctx.cnt["tP"] == c0["tP"] + TP_T * TT
    assert ctx.cnt["vV"] == c0["vV"] + 4 + 6 * TT
    assert ctx.cnt["sS"] == c0["sS"] + 30 * TT


def rope_apply2(ctx, v, src_ap, dst_ap, cos_ap, sin_ap):
    T = ctx.tiles
    nf = 32
    x0 = src_ap.rearrange("p (i two) -> p i two", two=2)[:, :, 0:1]
    x1 = src_ap.rearrange("p (i two) -> p i two", two=2)[:, :, 1:2]
    dr = dst_ap.rearrange("p (i two) -> p i two", two=2)
    c3 = cos_ap.unsqueeze(2)
    s3 = sin_ap.unsqueeze(2)
    ta = T["rp_ta"][:, 0:nf].unsqueeze(2)
    tb = T["rp_tb"][:, 0:nf].unsqueeze(2)
    v.tensor_tensor(out=ta, in0=x0, in1=c3, op=ALU.mult)
    v.tensor_tensor(out=tb, in0=x1, in1=s3, op=ALU.mult)
    ctx.inc(v.tensor_tensor(out=dr[:, :, 0:1], in0=ta, in1=tb, op=ALU.subtract), "vV", 1)
    v.tensor_tensor(out=ta, in0=x1, in1=c3, op=ALU.mult)
    v.tensor_tensor(out=tb, in0=x0, in1=s3, op=ALU.mult)
    ctx.inc(v.tensor_tensor(out=dr[:, :, 1:2], in0=ta, in1=tb, op=ALU.add), "vV", 1)


def emit_p2c(ctx):
    nc = ctx.nc
    T = ctx.tiles
    ident = T["ident"]

    A = ctx.arena
    A.push()
    for bb in range(2):
        T[f"p2c_xq{bb}"] = A.alloc(D, F32)
    T["p2c_xn"] = A.alloc(D, BF16)
    c0 = dict(ctx.cnt)
    with nc.Block("p2c0") as blk:
        @blk.sync
        def _(sy):
            for j in range(QT):
                bb = j % 2
                if j >= 2:
                    ctx.wait(sy, "vV", c0["vV"] + 2 * (j - 1))
                    ctx.wait(sy, "sS", c0["sS"] + 10 * (j - 1))
                ctx.inc(sy.dma_start(out=T[f"p2c_xq{bb}"][:],
                                     in_=ctx.inp["xq"][j * 128:(j + 1) * 128, :]),
                        f"qL{bb}", 16)
                ctx.wait(sy, "sS", c0["sS"] + 10 * (j + 1))
                for cch in range(32):
                    ctx.inc(sy.dma_start(out=ctx.scr["XNQT"][cch, j],
                                         in_=T[f"xnt_st{bb}"][:, cch * 128:(cch + 1) * 128]),
                            f"qU{bb}", 16)
            for q in ("qL0", "qL1", "qU0", "qU1"):
                ctx.wait(sy, q, ctx.cnt[q])

        @blk.vector
        def _(v):
            v = VProxy(v, ctx)
            for j in range(QT):
                bb = j % 2
                xq = T[f"p2c_xq{bb}"]
                ctx.wait(v, f"qL{bb}", c0[f"qL{bb}"] + 16 * (j // 2 + 1))
                v.reduce_sum(out=T["st_sum"][:], in_=xq[:], axis=AXX)
                ctx.vsync(v)
                v.tensor_scalar(out=T["st_mean"][:], in0=T["st_sum"][:],
                                scalar1=1.0 / D, scalar2=None, op0=ALU.mult)
                ctx.wait(v, "sS", c0["sS"] + 10 * j + 1)
                v.tensor_scalar(out=T["st_ms"][:], in0=T["st_ssq"][:],
                                scalar1=1.0 / D, scalar2=None, op0=ALU.mult)
                v.tensor_tensor(out=T["st_m2"][:], in0=T["st_mean"][:],
                                in1=T["st_mean"][:], op=ALU.mult)
                ctx.inc(v.tensor_tensor(out=T["st_var"][:], in0=T["st_ms"][:],
                                        in1=T["st_m2"][:], op=ALU.subtract), "vV", 1)
                ctx.wait(v, "sS", c0["sS"] + 10 * j + 2)
                v.reciprocal(out=T["st_rstd"][:], in_=T["st_std"][:])
                if j >= 1:
                    ctx.wait(v, "tP", c0["tP"] + 32 * j)
                ctx.inc(v.tensor_scalar(out=T["p2c_xn"][:], in0=xq[:],
                                        scalar1=T["st_mean"][:], scalar2=T["st_rstd"][:],
                                        op0=ALU.subtract, op1=ALU.mult), "vV", 1)

        @blk.scalar
        def _(s):
            for j in range(QT):
                bb = j % 2
                xq = T[f"p2c_xq{bb}"]
                ctx.wait(s, f"qL{bb}", c0[f"qL{bb}"] + 16 * (j // 2 + 1))
                ctx.inc(s.activation(out=T["st_xsq"][:], in_=xq[:], func=AF.Square,
                                     accum_out=T["st_ssq"][:]), "sS", 1)
                ctx.wait(s, "vV", c0["vV"] + 2 * j + 1)
                ctx.inc(s.activation(out=T["st_std"][:], in_=T["st_var"][:],
                                     func=AF.Sqrt, bias=T["eps"][:]), "sS", 1)
                for gq in range(8):
                    bank = ctx.psum[6 + (gq % 2)]
                    ctx.wait(s, "tP", c0["tP"] + 32 * j + 4 * (gq + 1))
                    if j >= 2 and gq == 0:
                        ctx.wait(s, f"qU{j % 2}",
                                 c0[f"qU{j % 2}"] + 16 * 32 * (j // 2))
                    ctx.inc(s.activation(
                        out=T[f"xnt_st{j % 2}"][:, gq * 512:(gq + 1) * 512],
                        in_=bank[:], func=AF.Copy), "sS", 1)

        @blk.tensor
        def _(t):
            for j in range(QT):
                ctx.wait(t, "vV", c0["vV"] + 2 * j + 2)
                for gq in range(8):
                    bank = ctx.psum[6 + (gq % 2)]
                    gidx = j * 8 + gq
                    if gidx >= 2:
                        prev = gidx - 2
                        ctx.wait(t, "sS", c0["sS"] + 10 * (prev // 8) + 2 + (prev % 8) + 1)
                    for u4 in range(4):
                        cch = gq * 4 + u4
                        ctx.inc(t.matmul(bank[:, u4 * 128:(u4 + 1) * 128],
                                         T["p2c_xn"][:, cch * 128:(cch + 1) * 128],
                                         ident[:], start=True, stop=True), "tP", 1)

    assert ctx.cnt["tP"] == c0["tP"] + 32 * QT
    assert ctx.cnt["vV"] == c0["vV"] + 2 * QT
    assert ctx.cnt["sS"] == c0["sS"] + 10 * QT
    A.pop()

    def streamed_proj(tag, w_key, ncols, coltile, evac_cb):
        c1 = dict(ctx.cnt)
        ntiles = ncols // coltile
        nch = 32
        with nc.Block("p2c_" + tag) as blk:
            @blk.sync
            def _(sy):
                for n in range(ntiles):
                    for c in range(nch):
                        k = n * nch + c
                        bb = k % 2
                        if k >= 2:
                            ctx.wait(sy, "tP", c1["tP"] + (k - 1))
                        ctx.inc(sy.dma_start(
                            out=T[f"p2c_xc{bb}"][:].rearrange("p (j q) -> p j q", q=128),
                            in_=ctx.scr["XNQT"][c].rearrange("j p q -> p j q")),
                            f"qU{bb}", 16)
                for q in ("qU0", "qU1"):
                    ctx.wait(sy, q, ctx.cnt[q])

            @blk.gpsimd
            def _(g):
                for n in range(ntiles):
                    for c in range(nch):
                        k = n * nch + c
                        bb = k % 2
                        if k >= 2:
                            ctx.wait(g, "tP", c1["tP"] + (k - 1))
                        wv = ctx.inp[w_key].rearrange("(c p) f -> c p f", p=128)
                        ctx.inc(g.dma_start(
                            out=T[f"p2c_w{bb}"][:, 0:coltile],
                            in_=wv[c][:, n * coltile:(n + 1) * coltile]),
                            f"gL{bb}", 16)
                for q in ("gL0", "gL1"):
                    ctx.wait(g, q, ctx.cnt[q])

            @blk.tensor
            def _(t):
                for n in range(ntiles):
                    for c in range(nch):
                        k = n * nch + c
                        bb = k % 2
                        ctx.wait(t, f"gL{bb}", c1[f"gL{bb}"] + 16 * (k // 2 + 1))
                        ctx.wait(t, f"qU{bb}", c1[f"qU{bb}"] + 16 * (k // 2 + 1))
                        if c == 0 and n >= 1:
                            ctx.wait(t, "sS", c1["sS"] + 8 * n)
                        for j in range(QT):
                            mm = t.matmul(ctx.psum[j][:, 0:coltile],
                                          T[f"p2c_xc{bb}"][:, j * 128:(j + 1) * 128],
                                          T[f"p2c_w{bb}"][:, 0:coltile],
                                          start=(c == 0), stop=(c == nch - 1))
                        ctx.inc(mm, "tP", 1)

            @blk.scalar
            def _(s):
                for n in range(ntiles):
                    ctx.wait(s, "tP", c1["tP"] + nch * (n + 1))
                    for j in range(QT):
                        evac_cb(s, n, j, ctx.psum[j])

        assert ctx.cnt["tP"] == c1["tP"] + nch * ntiles
        assert ctx.cnt["sS"] == c1["sS"] + 8 * ntiles

    A.push()
    T["iq_all"] = A.alloc(QT * 2048, BF16)
    T["iww"] = A.alloc(32 * IH, BF16)

    def iq_evac(s, n, j, bank):
        ctx.inc(s.activation(
            out=T["iq_all"][:, j * 2048 + n * 512:j * 2048 + (n + 1) * 512],
            in_=bank[:], func=AF.Copy), "sS", 1)

    streamed_proj("iq", "idx_wq_w", IH * IDD, 512, iq_evac)

    c4 = dict(ctx.cnt)
    with nc.Block("p2c_iqT") as blk:
        @blk.tensor
        def _(t):
            for j in range(QT):
                for f in range(IH):
                    bank = ctx.psum[f % 2]
                    gidx = j * IH + f
                    if gidx >= 2:
                        prev = gidx - 2
                        ctx.wait(t, "sS", c4["sS"] + prev + 1)
                    ctx.inc(t.matmul(bank[:, 0:128],
                                     T["iq_all"][:, j * 2048 + f * 128:
                                                 j * 2048 + (f + 1) * 128],
                                     ident[:], start=True, stop=True), "tP", 1)

        @blk.scalar
        def _(s):
            for j in range(QT):
                for f in range(IH):
                    bank = ctx.psum[f % 2]
                    ctx.wait(s, "tP", c4["tP"] + j * IH + f + 1)
                    ctx.inc(s.activation(
                        out=T["iqT"][:, f * NQ + j * 128:f * NQ + (j + 1) * 128],
                        in_=bank[:, 0:128], func=AF.Copy), "sS", 1)

    assert ctx.cnt["tP"] == c4["tP"] + IH * QT
    assert ctx.cnt["sS"] == c4["sS"] + IH * QT

    c5 = dict(ctx.cnt)
    with nc.Block("p2c_iw") as blk:
        @blk.gpsimd
        def _(g):
            iww = ctx.inp["idx_weights_w"].rearrange("(c p) f -> c p f", p=128)
            for c in range(32):
                ctx.inc(g.dma_start(out=T["iww"][:, c * IH:(c + 1) * IH], in_=iww[c]),
                        "qG", 16)
            ctx.wait(g, "qG", ctx.cnt["qG"])

        @blk.sync
        def _(sy):
            for c in range(32):
                bb = c % 2
                if c >= 2:
                    ctx.wait(sy, "tP", c5["tP"] + (c - 1) * QT)
                ctx.inc(sy.dma_start(
                    out=T[f"p2c_xc{bb}"][:].rearrange("p (j q) -> p j q", q=128),
                    in_=ctx.scr["XNQT"][c].rearrange("j p q -> p j q")),
                    f"qU{bb}", 16)
            for q in ("qU0", "qU1"):
                ctx.wait(sy, q, ctx.cnt[q])

        @blk.tensor
        def _(t):
            ctx.wait(t, "qG", c5["qG"] + 16 * 32)
            for c in range(32):
                bb = c % 2
                ctx.wait(t, f"qU{bb}", c5[f"qU{bb}"] + 16 * (c // 2 + 1))
                for j in range(QT):
                    bank = ctx.psum[j]
                    if c == 0 and j >= 0:
                        pass
                    mm = t.matmul(bank[:, 0:IH],
                                  T[f"p2c_xc{bb}"][:, j * 128:(j + 1) * 128],
                                  T["iww"][:, c * IH:(c + 1) * IH],
                                  start=(c == 0), stop=(c == 31))
                    ctx.inc(mm, "tP", 1)

        @blk.scalar
        def _(s):
            ctx.wait(s, "tP", c5["tP"] + 32 * QT)
            for j in range(QT):
                bank = ctx.psum[j]
                ctx.inc(s.activation(out=T["iw_sb"][:, j * IH:(j + 1) * IH],
                                     in_=bank[:, 0:IH], func=AF.Copy,
                                     scale=IW_SCALE), "sS", 1)

    assert ctx.cnt["tP"] == c5["tP"] + 32 * QT
    assert ctx.cnt["sS"] == c5["sS"] + QT
    A.pop()

    A.push()
    T["qlat"] = A.alloc(QT * QR, BF16)
    T["qlat_n"] = A.alloc(QR, BF16)
    def qa_evac(s, n, j, bank):
        ctx.inc(s.activation(out=T["qlat"][:, j * QR + n * 512:j * QR + (n + 1) * 512],
                             in_=bank[:], func=AF.Copy), "sS", 1)

    streamed_proj("qa", "q_a_w", QR, 512, qa_evac)

    c2 = dict(ctx.cnt)
    with nc.Block("p2c1b") as blk:
        @blk.scalar
        def _(s):
            for j in range(QT):
                if j >= 1:
                    ctx.wait(s, "vV", c2["vV"] + 2 * j)
                ctx.inc(s.activation(out=T["st_xsq"][:, 0:QR],
                                     in_=T["qlat"][:, j * QR:(j + 1) * QR],
                                     func=AF.Square, accum_out=T["st_ssq"][:]), "sS", 1)
                ctx.wait(s, "vV", c2["vV"] + 2 * j + 1)
                ctx.inc(s.activation(out=T["st_std"][:], in_=T["st_ms"][:],
                                     func=AF.Sqrt, bias=T["eps"][:]), "sS", 1)
                for gq in range(3):
                    bank = ctx.psum[gq % 2]
                    ctx.wait(s, "tP", c2["tP"] + 12 * j + 4 * (gq + 1))
                    ctx.inc(s.activation(
                        out=T["qlatT"].rearrange("p (c q) -> p c q", q=NQ)[
                            :, gq * 4:(gq + 1) * 4, j * 128:(j + 1) * 128],
                        in_=bank[:].rearrange("p (c q) -> p c q", q=128),
                        func=AF.Copy), "sS", 1)

        @blk.vector
        def _(v):
            v = VProxy(v, ctx)
            for j in range(QT):
                ctx.wait(v, "sS", c2["sS"] + 5 * j + 1)
                ctx.inc(v.tensor_scalar(out=T["st_ms"][:], in0=T["st_ssq"][:],
                                        scalar1=1.0 / QR, scalar2=None,
                                        op0=ALU.mult), "vV", 1)
                ctx.wait(v, "sS", c2["sS"] + 5 * j + 2)
                v.reciprocal(out=T["st_rstd"][:], in_=T["st_std"][:])
                if j >= 1:
                    ctx.wait(v, "tP", c2["tP"] + 12 * j)
                ctx.inc(v.tensor_scalar(out=T["qlat_n"][:],
                                        in0=T["qlat"][:, j * QR:(j + 1) * QR],
                                        scalar1=T["st_rstd"][:], scalar2=None,
                                        op0=ALU.mult), "vV", 1)

        @blk.tensor
        def _(t):
            for j in range(QT):
                ctx.wait(t, "vV", c2["vV"] + 2 * j + 2)
                for gq in range(3):
                    bank = ctx.psum[gq % 2]
                    gidx = j * 3 + gq
                    if gidx >= 2:
                        prev = gidx - 2
                        ctx.wait(t, "sS", c2["sS"] + 5 * (prev // 3) + 2 + (prev % 3) + 1)
                    for u4 in range(4):
                        cch = gq * 4 + u4
                        ctx.inc(t.matmul(bank[:, u4 * 128:(u4 + 1) * 128],
                                         T["qlat_n"][:, cch * 128:(cch + 1) * 128],
                                         ident[:], start=True, stop=True), "tP", 1)

    assert ctx.cnt["tP"] == c2["tP"] + 12 * QT
    assert ctx.cnt["vV"] == c2["vV"] + 2 * QT
    assert ctx.cnt["sS"] == c2["sS"] + 5 * QT
    A.pop()

    c6 = dict(ctx.cnt)
    NB3, NCH3 = 16, 12
    V_PRO, S_PRO = 24, 16

    def tp_mm(k):
        return 44 * (k // NCH3) + (k % NCH3) + 1

    def tp_T(n, j):
        return 44 * n + 12 + 4 * (j + 1)

    def vv_fold(k):
        return V_PRO + 44 * (k // NCH3) + (k % NCH3) + 1

    def vv_rope(n, j):
        return V_PRO + 44 * n + 12 + 4 * (j + 1)

    def ss_qblk(n, j):
        return S_PRO + 24 * n + 3 * j + 1

    def ss_stage(n, j):
        return S_PRO + 24 * n + 3 * (j + 1)

    with nc.Block("p2c_qb") as blk:
        @blk.gpsimd
        def _(g):
            ctx.inc(g.dma_start(out=T["normq"][:],
                                in_=ctx.inp["q_a_norm_w"].rearrange("(c p) -> p c", p=128),
                                allow_slow_non_contiguous=True), "qG", 16)
            for n in range(NB3):
                for c in range(NCH3):
                    k = n * NCH3 + c
                    bb = k % 2
                    if k >= 2:
                        ctx.wait(g, "tP", c6["tP"] + tp_mm(k - 1))
                    wv = ctx.inp["q_b_w"].rearrange("(c p) f -> c p f", p=128)
                    ctx.inc(g.dma_start(
                        out=T[f"p2c_w{bb}"][:, 0:384],
                        in_=wv[c][:, n * 384:(n + 1) * 384]), f"gL{bb}", 16)
            ctx.wait(g, "qG", c6["qG"] + 16)
            for q in ("gL0", "gL1"):
                ctx.wait(g, q, ctx.cnt[q])

        @blk.sync
        def _(sy):
            for n in range(NB3):
                for j in range(QT):
                    ctx.wait(sy, "sS", c6["sS"] + ss_stage(n, j))
                    h0 = 2 * n
                    idx = n * QT + j
                    bb2 = idx % 2
                    st = T[f"qstage{bb2}"]
                    ctx.inc(sy.dma_start(out=ctx.scr["QnT"][h0, j],
                                         in_=st[:, 0:128]), f"qU{bb2}", 16)
                    ctx.inc(sy.dma_start(out=ctx.scr["QnT"][h0 + 1, j],
                                         in_=st[:, 128:256]), f"qU{bb2}", 16)
                    ctx.inc(sy.dma_start(out=ctx.scr["QpT"][h0, j],
                                         in_=st[0:64, 256:384]), f"qU{bb2}", 16)
                    ctx.inc(sy.dma_start(out=ctx.scr["QpT"][h0 + 1, j],
                                         in_=st[0:64, 384:512]), f"qU{bb2}", 16)
            for q in ("qL0", "qL1", "qU0", "qU1"):
                ctx.wait(sy, q, ctx.cnt[q])

        @blk.vector
        def _(v):
            v = VProxy(v, ctx)
            ctx.wait(v, "qG", c6["qG"] + 16)
            for j in range(QT):
                rope_frac_vec(ctx, v, T["qpos"][:, j:j + 1], "rp")
                ctx.wait(v, "sS", c6["sS"] + 2 * (j + 1))
                v.tensor_copy(out=T["qcos"][:, j * 32:(j + 1) * 32], in_=T["rp_cos"][:])
                ctx.inc(v.tensor_copy(out=T["qsin"][:, j * 32:(j + 1) * 32],
                                      in_=T["rp_sin"][:]), "vV", 1)
            for n in range(NB3):
                for c in range(NCH3):
                    k = n * NCH3 + c
                    bb = k % 2
                    ctx.wait(v, f"gL{bb}", c6[f"gL{bb}"] + 16 * (k // 2 + 1))
                    ctx.inc(v.tensor_scalar(out=T[f"p2c_w{bb}"][:, 0:384],
                                            in0=T[f"p2c_w{bb}"][:, 0:384],
                                            scalar1=T["normq"][:, c:c + 1],
                                            scalar2=None, op0=ALU.mult), "vV", 1)
                for j in range(QT):
                    ctx.wait(v, "sS", c6["sS"] + ss_qblk(n, j))
                    cosj = T["qcos"][:, j * 32:(j + 1) * 32]
                    sinj = T["qsin"][:, j * 32:(j + 1) * 32]
                    rope_apply2(ctx, v, T["qblk"][:, 128:192], T["qropea"][:],
                                cosj, sinj)
                    rope_apply2(ctx, v, T["qblk"][:, 320:384], T["qropeb"][:],
                                cosj, sinj)

        @blk.tensor
        def _(t):
            for n in range(NB3):
                for c in range(NCH3):
                    k = n * NCH3 + c
                    ctx.wait(t, "vV", c6["vV"] + vv_fold(k))
                    if c == 0 and n >= 1:
                        ctx.wait(t, "sS", c6["sS"] + S_PRO + 24 * n)
                    for j in range(QT):
                        mm = t.matmul(ctx.psum[j][:, 0:384],
                                      T["qlatT"].rearrange("p (c q) -> p c q", q=NQ)[
                                          :, c, j * 128:(j + 1) * 128],
                                      T[f"p2c_w{(k % 2)}"][:, 0:384],
                                      start=(c == 0), stop=(c == NCH3 - 1))
                    ctx.inc(mm, "tP", 1)
                for j in range(QT):
                    ctx.wait(t, "vV", c6["vV"] + vv_rope(n, j))
                    bank = ctx.psum[j]
                    ctx.inc(t.matmul(bank[:, 0:128], T["qblk"][:, 0:128], ident[:],
                                     start=True, stop=True), "tP", 1)
                    ctx.inc(t.matmul(bank[:, 128:256], T["qblk"][:, 192:320], ident[:],
                                     start=True, stop=True), "tP", 1)
                    ctx.inc(t.matmul(bank[0:64, 256:384], T["qropea"][:], ident[:],
                                     start=True, stop=True), "tP", 1)
                    ctx.inc(t.matmul(bank[0:64, 384:512], T["qropeb"][:], ident[:],
                                     start=True, stop=True), "tP", 1)

        @blk.scalar
        def _(s):
            for j in range(QT):
                ctx.wait(s, "vV", c6["vV"] + 3 * j + 2)
                if j >= 1:
                    ctx.wait(s, "vV", c6["vV"] + 3 * j)
                rope_sin_scalar(ctx, s, "rp")
            for n in range(NB3):
                for j in range(QT):
                    idx = n * QT + j
                    if idx >= 1:
                        pn, pj = (n, j - 1) if j >= 1 else (n - 1, QT - 1)
                        ctx.wait(s, "vV", c6["vV"] + vv_rope(pn, pj))
                        ctx.wait(s, "tP", c6["tP"] + tp_T(pn, pj))
                    ctx.wait(s, "tP", c6["tP"] + 44 * n + 12)
                    ctx.inc(s.activation(out=T["qblk"][:], in_=ctx.psum[j][:, 0:384],
                                         func=AF.Copy), "sS", 1)
                    ctx.wait(s, "tP", c6["tP"] + tp_T(n, j))
                    bb2 = idx % 2
                    if idx >= 2:
                        ctx.wait(s, f"qU{bb2}", c6[f"qU{bb2}"] + 16 * 4 * (idx // 2))
                    st = T[f"qstage{bb2}"]
                    s.activation(out=st[:, 0:256], in_=ctx.psum[j][:, 0:256],
                                 func=AF.Copy)
                    ctx.inc(s.activation(out=st[0:64, 256:512],
                                         in_=ctx.psum[j][0:64, 256:512],
                                         func=AF.Copy), "sS", 2)

    assert ctx.cnt["tP"] == c6["tP"] + 44 * NB3
    assert ctx.cnt["vV"] == c6["vV"] + V_PRO + 44 * NB3
    assert ctx.cnt["sS"] == c6["sS"] + S_PRO + 24 * NB3


def emit_p2d(ctx):
    nc = ctx.nc
    T = ctx.tiles
    c0 = dict(ctx.cnt)
    NIT_A, NIT_B = 13, 22

    units = []
    for j in range(QT):
        for kb in range(j + 1):
            for f in range(IH):
                units.append((j, kb, f))

    vv_after_acc = {}
    vv = c0["vV"]
    u = 0
    for j in range(QT):
        for kb in range(j + 1):
            for f in range(IH):
                vv += 1
                vv_after_acc[u] = vv
                u += 1
            if kb == j:
                vv += 2
        vv += 7
        vv += 2 * (NIT_A + NIT_B)
        vv += j + 1
    VV_TOT = vv

    def moff(j):
        return sum(jj + 1 for jj in range(j)) * KBLK

    with nc.Block("p2d") as blk:
        @blk.tensor
        def _(t):
            for u, (j, kb, f) in enumerate(units):
                bank = ctx.psum[u % 2]
                if u >= 2:
                    ctx.wait(t, "sS", c0["sS"] + (u - 2) + 1)
                ctx.inc(t.matmul(bank[:], T["iqT"][:, f * NQ + j * 128:
                                                   f * NQ + (j + 1) * 128],
                                 T["ikT"][:, kb * KBLK:(kb + 1) * KBLK],
                                 start=True, stop=True), "tP", 1)

        @blk.scalar
        def _(s):
            for u, (j, kb, f) in enumerate(units):
                bank = ctx.psum[u % 2]
                ctx.wait(s, "tP", c0["tP"] + u + 1)
                if u >= 2:
                    ctx.wait(s, "vV", vv_after_acc[u - 2])
                ctx.inc(s.activation(out=T[f"p2d_r{u % 2}"][:], in_=bank[:],
                                     func=AF.Relu), "sS", 1)

        @blk.vector
        def _(v):
            v = VProxy(v, ctx)
            u = 0
            for j in range(QT):
                prefix = (j + 1) * KBLK
                for kb in range(j + 1):
                    cols = T["idx_acc"][:, kb * KBLK:(kb + 1) * KBLK]
                    for f in range(IH):
                        ctx.wait(v, "sS", c0["sS"] + u + 1)
                        iwcol = T["iw_sb"][:, j * IH + f:j * IH + f + 1]
                        if f == 0:
                            ctx.inc(v.tensor_scalar(out=cols, in0=T[f"p2d_r{u % 2}"][:],
                                                    scalar1=iwcol, scalar2=None,
                                                    op0=ALU.mult), "vV", 1)
                        else:
                            ctx.inc(v.scalar_tensor_tensor(out=cols,
                                                           in0=T[f"p2d_r{u % 2}"][:],
                                                           scalar=iwcol, in1=cols,
                                                           op0=ALU.mult, op1=ALU.add),
                                    "vV", 1)
                        u += 1
                    if kb == j:
                        v.tensor_scalar(out=T["p2d_th"][:], in0=T["qpos"][:, j:j + 1],
                                        scalar1=float(512 * kb), scalar2=None,
                                        op0=ALU.subtract)
                        ctx.inc(v.tensor_scalar(out=T["p2d_t"][:], in0=T["kcol"][:],
                                                scalar1=T["p2d_th"][:],
                                                scalar2=NEG_CAUSAL,
                                                op0=ALU.is_gt, op1=ALU.mult), "vV", 1)
                        ctx.inc(v.tensor_tensor(out=cols, in0=cols, in1=T["p2d_t"][:],
                                                op=ALU.add), "vV", 1)
                ctx.inc(v.reduce_max(out=T["p2d_mx"][:], in_=T["idx_acc"][:, 0:prefix],
                                     axis=AXX), "vV", 1)
                ctx.inc(v.memset(T["p2d_th"][:], -1.0e30), "vV", 1)
                ctx.inc(v.tensor_scalar(out=T["p2d_cmp"][:, 0:prefix],
                                        in0=T["idx_acc"][:, 0:prefix],
                                        scalar1=-1.0, scalar2=None,
                                        op0=ALU.mult), "vV", 1)
                ctx.vsync(v)
                ctx.inc(v.scalar_tensor_tensor(out=T["p2d_cmp"][:, 0:prefix],
                                               in0=T["idx_acc"][:, 0:prefix],
                                               scalar=T["p2d_th"][:],
                                               in1=T["p2d_cmp"][:, 0:prefix],
                                               op0=ALU.is_gt, op1=ALU.mult), "vV", 1)
                ctx.vsync(v)
                ctx.inc(v.reduce_max(out=T["p2d_cnt"][:], in_=T["p2d_cmp"][:, 0:prefix],
                                     axis=AXX), "vV", 1)
                ctx.vsync(v)
                ctx.inc(v.tensor_scalar(out=T["p2d_lo"][:], in0=T["p2d_cnt"][:],
                                        scalar1=-1.0, scalar2=1.0,
                                        op0=ALU.mult, op1=ALU.subtract), "vV", 1)
                ctx.inc(v.tensor_scalar(out=T["p2d_hi"][:], in0=T["p2d_mx"][:],
                                        scalar1=1.0, scalar2=None, op0=ALU.add),
                        "vV", 1)
                for rnd, nit in ((0, NIT_A), (1, NIT_B)):
                    if rnd == 1:
                        ctx.vsync(v)
                        v.tensor_scalar(out=T["p2d_hi"][:], in0=T["p2d_lo"][:],
                                        scalar1=0.4, scalar2=None, op0=ALU.add)
                    for it in range(nit):
                        ctx.vsync(v)
                        v.tensor_scalar(out=T["p2d_mid"][:], in0=T["p2d_lo"][:],
                                        scalar1=T["p2d_hi"][:], scalar2=0.5,
                                        op0=ALU.add, op1=ALU.mult)
                        ctx.vsync(v)
                        ctx.inc(v.tensor_scalar(out=T["p2d_cmp"][:, 0:prefix],
                                                in0=T["idx_acc"][:, 0:prefix],
                                                scalar1=T["p2d_mid"][:], scalar2=0.0,
                                                op0=ALU.is_ge, op1=ALU.add,
                                                accum_out=T["p2d_cnt"][:]), "vV", 1)
                        ctx.vsync(v)
                        v.tensor_scalar(out=T["p2d_b"][:], in0=T["p2d_cnt"][:],
                                        scalar1=float(TOPK), scalar2=None,
                                        op0=ALU.is_ge)
                        v.tensor_scalar(out=T["p2d_d"][:], in0=T["p2d_mid"][:],
                                        scalar1=T["p2d_lo"][:], scalar2=None,
                                        op0=ALU.subtract)
                        ctx.vsync(v)
                        v.tensor_scalar(out=T["p2d_e"][:], in0=T["p2d_hi"][:],
                                        scalar1=T["p2d_mid"][:], scalar2=None,
                                        op0=ALU.subtract)
                        ctx.vsync(v)
                        v.scalar_tensor_tensor(out=T["p2d_lo"][:], in0=T["p2d_b"][:],
                                               scalar=T["p2d_d"][:], in1=T["p2d_lo"][:],
                                               op0=ALU.mult, op1=ALU.add)
                        ctx.inc(v.scalar_tensor_tensor(out=T["p2d_hi"][:],
                                                       in0=T["p2d_b"][:],
                                                       scalar=T["p2d_e"][:],
                                                       in1=T["p2d_mid"][:],
                                                       op0=ALU.mult, op1=ALU.add),
                                "vV", 1)
                ctx.vsync(v)
                for kb in range(j + 1):
                    ctx.inc(v.tensor_scalar(
                        out=T["mask"][:, moff(j) + kb * KBLK:moff(j) + (kb + 1) * KBLK],
                        in0=T["idx_acc"][:, kb * KBLK:(kb + 1) * KBLK],
                        scalar1=T["p2d_lo"][:], scalar2=NEG_BIG,
                        op0=ALU.is_lt, op1=ALU.mult), "vV", 1)

    assert ctx.cnt["vV"] == VV_TOT, (ctx.cnt["vV"], VV_TOT)
    assert ctx.cnt["tP"] == c0["tP"] + len(units)
    assert ctx.cnt["sS"] == c0["sS"] + len(units)


def emit_p3(ctx):
    nc = ctx.nc
    T = ctx.tiles
    c0 = dict(ctx.cnt)
    ISQ = float(QKD) ** -0.5

    units = []
    for h in range(H):
        for kb in range(NKB):
            for j in range(kb, QT):
                units.append((h, j, kb))
    NU = len(units)

    tpS, tpPT, tpAV, tpOn = {}, {}, {}, {}
    vvSm, vvStat, vvO, vvOn = {}, {}, {}, {}
    ssA, ssP, ssPT, ssOT = {}, {}, {}, {}
    NU0 = len(units)
    tp = c0["tP"]

    def _tpS(u):
        global_ = None
    tp += 1; tpS[0] = tp
    if NU0 > 1:
        tp += 1; tpS[1] = tp
    for u, (h, j, kb) in enumerate(units):
        tp += 1; tpPT[u] = tp
        tp += 1; tpAV[u] = tp
        if kb == j:
            tp += 1; tpOn[(h, j)] = tp
        if u + 2 < NU0:
            tp += 1; tpS[u + 2] = tp
    vv = c0["vV"]
    vv += 2; vvSm[0] = vv - 1; vvStat[0] = vv
    for u, (h, j, kb) in enumerate(units):
        if u + 1 < NU0:
            vv += 2; vvSm[u + 1] = vv - 1; vvStat[u + 1] = vv
        vv += 1; vvO[u] = vv
        if kb == j:
            vv += 1; vvOn[(h, j)] = vv
    ss = c0["sS"]
    for u, (h, j, kb) in enumerate(units):
        ss += 1; ssA[u] = ss
        ss += 1; ssP[u] = ss
        ss += 1; ssPT[u] = ss
        if kb == j:
            ss += 1; ssOT[(h, j)] = ss
    TP_TOT, VV_TOT, SS_TOT = tp, vv, ss
    last_av_of_head = {h: max(tpAV[u] for u, (hh, _, _) in enumerate(units) if hh == h)
                       for h in range(H)}
    ot_order = [(h, j) for h in range(H) for j in range(QT)]

    def moff(j):
        return sum(jj + 1 for jj in range(j)) * KBLK

    with nc.Block("p3") as blk:
        @blk.sync
        def _(sy):
            ctx.inc(sy.dma_start(out=T["xkp"][0:64, :], in_=ctx.scr["KpT"]),
                    "qL0", 16)
            for h in range(H):
                bb = h % 2
                if h >= 2:
                    ctx.wait(sy, "tP", last_av_of_head[h - 2])
                ctx.inc(sy.dma_start(out=T[f"xkn{bb}"][:], in_=ctx.scr["KnT"][h]),
                        f"qL{bb}", 16)
                for tt in range(TT):
                    ctx.inc(sy.dma_start(
                        out=T[f"xv{bb}"][:, tt * VD:(tt + 1) * VD],
                        in_=ctx.scr["V"][h, tt * 128:(tt + 1) * 128, :]),
                        f"qL{bb}", 16)
                for j in range(QT):
                    ctx.inc(sy.dma_start(out=T[f"xqn{bb}"][:, j * 128:(j + 1) * 128],
                                         in_=ctx.scr["QnT"][h, j]), f"qL{bb}", 16)
                    ctx.inc(sy.dma_start(out=T[f"xqp{bb}"][0:64, j * 128:(j + 1) * 128],
                                         in_=ctx.scr["QpT"][h, j]), f"qL{bb}", 16)
                if h >= 1:
                    for j in range(QT):
                        idx = (h - 1) * QT + j
                        ctx.wait(sy, "sS", ssOT[(h - 1, j)])
                        ctx.inc(sy.dma_start(out=ctx.scr["OT"][h - 1, j],
                                             in_=T[f"ot_sb{idx % 2}"][:]),
                                f"qU{idx % 2}", 16)
            for j in range(QT):
                idx = (H - 1) * QT + j
                ctx.wait(sy, "sS", ssOT[(H - 1, j)])
                ctx.inc(sy.dma_start(out=ctx.scr["OT"][H - 1, j],
                                     in_=T[f"ot_sb{idx % 2}"][:]), f"qU{idx % 2}", 16)
            for q in ("qL0", "qL1", "qU0", "qU1"):
                ctx.wait(sy, q, ctx.cnt[q])

        @blk.tensor
        def _(t):
            ident = T["ident"]

            def emit_S(u):
                h, j, kb = units[u]
                bb = h % 2
                bA = ctx.psum[u % 2]
                if kb == 0 and j == 0:
                    nload = 16 * (1 + TT + 2 * QT) * (h // 2 + 1) + \
                            (16 if bb == 0 else 0)
                    ctx.wait(t, f"qL{bb}", c0[f"qL{bb}"] + nload)
                if u >= 2:
                    ctx.wait(t, "vV", vvSm[u - 2])
                t.matmul(bA[:], T[f"xqn{bb}"][:, j * 128:(j + 1) * 128],
                         T[f"xkn{bb}"][:, kb * KBLK:(kb + 1) * KBLK],
                         start=True, stop=False)
                ctx.inc(t.matmul(bA[:], T[f"xqp{bb}"][0:64, j * 128:(j + 1) * 128],
                                 T["xkp"][0:64, kb * KBLK:(kb + 1) * KBLK],
                                 start=False, stop=True), "tP", 1)

            def emit_PT_AV(u):
                h, j, kb = units[u]
                bb = h % 2
                bB = ctx.psum[2 + u % 2]
                bC = ctx.psum[4 + u % 2]
                ctx.wait(t, "sS", ssP[u])
                if u >= 1:
                    ctx.wait(t, "sS", ssPT[u - 1])
                for i in range(4):
                    mm = t.matmul(bB[:, i * 128:(i + 1) * 128],
                                  T[f"p3_p{u % 2}"][:, i * 128:(i + 1) * 128],
                                  ident[:], start=True, stop=True)
                ctx.inc(mm, "tP", 1)
                ctx.wait(t, "sS", ssPT[u])
                if u >= 2:
                    ctx.wait(t, "vV", vvO[u - 2])
                for i in range(4):
                    mm = t.matmul(bC[:, 0:VD], T["p3_pt"][:, i * 128:(i + 1) * 128],
                                  T[f"xv{bb}"][:, (kb * 4 + i) * VD:(kb * 4 + i + 1) * VD],
                                  start=(i == 0), stop=(i == 3))
                ctx.inc(mm, "tP", 1)
                if kb == j:
                    ctx.wait(t, "vV", vvOn[(h, j)])
                    oidx = ot_order.index((h, j))
                    if oidx >= 1:
                        ctx.wait(t, "sS", ssOT[ot_order[oidx - 1]])
                    ctx.inc(t.matmul(ctx.psum[6][:, 0:128], T["p3_on"][:], ident[:],
                                     start=True, stop=True), "tP", 1)

            emit_S(0)
            if NU > 1:
                emit_S(1)
            for u in range(NU):
                emit_PT_AV(u)
                if u + 2 < NU:
                    emit_S(u + 2)

        @blk.vector
        def _(v):
            v = VProxy(v, ctx)

            def emit_Sm_stats(u):
                h, j, kb = units[u]
                if kb == 0:
                    v.memset(T[f"p3_o{j}"][:], 0.0)
                    v.memset(T[f"p3_l{j}"][:], 0.0)
                    v.memset(T[f"p3_m{j}"][:], -1.0e30)
                bA = ctx.psum[u % 2]
                ctx.wait(v, "tP", tpS[u])
                mt = T["mask"][:, moff(j) + kb * KBLK:moff(j) + (kb + 1) * KBLK]
                ctx.inc(v.scalar_tensor_tensor(out=T[f"p3_sm{u % 2}"][:], in0=bA[:],
                                               scalar=ISQ, in1=mt,
                                               op0=ALU.mult, op1=ALU.add), "vV", 1)
                v.reduce_max(out=T["p3_mkb"][:], in_=T[f"p3_sm{u % 2}"][:], axis=AXX)
                v.tensor_tensor(out=T["p3_mnew"][:], in0=T[f"p3_m{j}"][:],
                                in1=T["p3_mkb"][:], op=ALU.max)
                v.tensor_scalar(out=T[f"p3_d{u % 2}"][:], in0=T[f"p3_m{j}"][:],
                                scalar1=T["p3_mnew"][:], scalar2=None,
                                op0=ALU.subtract)
                v.tensor_copy(out=T[f"p3_m{j}"][:], in_=T["p3_mnew"][:])
                ctx.inc(v.tensor_scalar(out=T[f"p3_nm{u % 2}"][:], in0=T["p3_mnew"][:],
                                        scalar1=-1.0, scalar2=None, op0=ALU.mult),
                        "vV", 1)

            def emit_lO(u):
                h, j, kb = units[u]
                bC = ctx.psum[4 + u % 2]
                ctx.wait(v, "sS", ssP[u])
                v.scalar_tensor_tensor(out=T[f"p3_l{j}"][:], in0=T[f"p3_l{j}"][:],
                                       scalar=T[f"p3_al{u % 2}"][:],
                                       in1=T[f"p3_lkb{u % 2}"][:],
                                       op0=ALU.mult, op1=ALU.add)
                ctx.wait(v, "tP", tpAV[u])
                ctx.inc(v.scalar_tensor_tensor(out=T[f"p3_o{j}"][:],
                                               in0=T[f"p3_o{j}"][:],
                                               scalar=T[f"p3_al{u % 2}"][:],
                                               in1=bC[:, 0:VD],
                                               op0=ALU.mult, op1=ALU.add), "vV", 1)
                if kb == j:
                    v.reciprocal(out=T["p3_linv"][:], in_=T[f"p3_l{j}"][:])
                    ctx.inc(v.tensor_scalar(out=T["p3_on"][:], in0=T[f"p3_o{j}"][:],
                                            scalar1=T["p3_linv"][:], scalar2=None,
                                            op0=ALU.mult), "vV", 1)

            emit_Sm_stats(0)
            for u in range(NU):
                if u + 1 < NU:
                    emit_Sm_stats(u + 1)
                emit_lO(u)

        @blk.scalar
        def _(s):
            for u, (h, j, kb) in enumerate(units):
                bB = ctx.psum[2 + u % 2]
                ctx.wait(s, "vV", vvStat[u])
                ctx.inc(s.activation(out=T[f"p3_al{u % 2}"][:], in_=T[f"p3_d{u % 2}"][:],
                                     func=AF.Exp), "sS", 1)
                ctx.inc(s.activation(out=T[f"p3_p{u % 2}"][:], in_=T[f"p3_sm{u % 2}"][:],
                                     func=AF.Exp, bias=T[f"p3_nm{u % 2}"][:],
                                     accum_out=T[f"p3_lkb{u % 2}"][:]), "sS", 1)
                ctx.wait(s, "tP", tpPT[u])
                ctx.inc(s.activation(out=T["p3_pt"][:], in_=bB[:], func=AF.Copy),
                        "sS", 1)
                if kb == j:
                    idx = h * QT + j
                    ctx.wait(s, "tP", tpOn[(h, j)])
                    if idx >= 2:
                        ctx.wait(s, f"qU{idx % 2}",
                                 c0[f"qU{idx % 2}"] + 16 * (idx // 2))
                    ctx.inc(s.activation(out=T[f"ot_sb{idx % 2}"][:],
                                         in_=ctx.psum[6][:, 0:128], func=AF.Copy),
                            "sS", 1)

    assert ctx.cnt["tP"] == TP_TOT, (ctx.cnt["tP"], TP_TOT)
    assert ctx.cnt["vV"] == VV_TOT, (ctx.cnt["vV"], VV_TOT)
    assert ctx.cnt["sS"] == SS_TOT, (ctx.cnt["sS"], SS_TOT)


def emit_p4a(ctx):
    nc = ctx.nc
    T = ctx.tiles
    c0 = dict(ctx.cnt)

    with nc.Block("p4a") as blk:
        @blk.gpsimd
        def _(g):
            for qtr in range(4):
                ow = ctx.inp["o_w"].rearrange("(c p) f -> c p f", p=128)
                if qtr >= 1:
                    ctx.wait(g, "tP", c0["tP"] + 16 * qtr)
                for c in range(32):
                    ctx.inc(g.dma_start(
                        out=T["p4_ow"][:, c * 1024:(c + 1) * 1024],
                        in_=ow[c][:, qtr * 1024:(qtr + 1) * 1024]), "gL0", 16)
            ctx.wait(g, "gL0", ctx.cnt["gL0"])

        @blk.sync
        def _(sy):
            for qtr in range(4):
                for j in range(QT):
                    k = qtr * QT + j
                    bb = k % 2
                    if k >= 2:
                        ctx.wait(sy, "tP", c0["tP"] + 2 * (k - 1))
                    for h in range(H):
                        ctx.inc(sy.dma_start(
                            out=T[f"p4_ot{bb}"][:, h * 128:(h + 1) * 128],
                            in_=ctx.scr["OT"][h, j]), f"qL{bb}", 16)
                    ctx.inc(sy.dma_start(
                        out=T[f"p4_xq{bb}"][:],
                        in_=ctx.inp["xq"][j * 128:(j + 1) * 128,
                                          qtr * 1024:(qtr + 1) * 1024]),
                        f"qL{bb}", 16)
                    if k >= 1:
                        pq, pj = (k - 1) // QT, (k - 1) % QT
                        ctx.wait(sy, "vV", c0["vV"] + 2 * k)
                        ctx.inc(sy.dma_start(
                            out=ctx.scr["XRES"][pj * 128:(pj + 1) * 128,
                                                pq * 1024:(pq + 1) * 1024],
                            in_=T[f"p4_x1{(k - 1) % 2}"][:]), f"qU{(k - 1) % 2}", 16)
            ctx.wait(sy, "vV", c0["vV"] + 2 * 32)
            ctx.inc(sy.dma_start(
                out=ctx.scr["XRES"][7 * 128:8 * 128, 3 * 1024:4 * 1024],
                in_=T["p4_x11"][:]), "qU1", 16)
            for q in ("qL0", "qL1", "qU0", "qU1"):
                ctx.wait(sy, q, ctx.cnt[q])

        @blk.tensor
        def _(t):
            for qtr in range(4):
                ctx.wait(t, "gL0", c0["gL0"] + 16 * 32 * (qtr + 1))
                for j in range(QT):
                    k = qtr * QT + j
                    bb = k % 2
                    ctx.wait(t, f"qL{bb}", c0[f"qL{bb}"] + 16 * 33 * (k // 2 + 1))
                    if k >= 2:
                        ctx.wait(t, "vV", c0["vV"] + 2 * (k - 1))
                    for n in range(2):
                        bank = ctx.psum[2 * (k % 2) + n]
                        for c in range(32):
                            mm = t.matmul(bank[:],
                                          T[f"p4_ot{bb}"][:, c * 128:(c + 1) * 128],
                                          T["p4_ow"][:, c * 1024 + n * 512:
                                                     c * 1024 + (n + 1) * 512],
                                          start=(c == 0), stop=(c == 31))
                        ctx.inc(mm, "tP", 1)

        @blk.vector
        def _(v):
            v = VProxy(v, ctx)
            for qtr in range(4):
                for j in range(QT):
                    k = qtr * QT + j
                    bb = k % 2
                    ctx.wait(v, "tP", c0["tP"] + 2 * (k + 1))
                    if k >= 2:
                        ctx.wait(v, f"qU{bb}", c0[f"qU{bb}"] + 16 * (k // 2))
                    for n in range(2):
                        bank = ctx.psum[2 * (k % 2) + n]
                        ctx.inc(v.tensor_tensor(
                            out=T[f"p4_x1{bb}"][:, n * 512:(n + 1) * 512],
                            in0=bank[:], in1=T[f"p4_xq{bb}"][:, n * 512:(n + 1) * 512],
                            op=ALU.add), "vV", 1)

    assert ctx.cnt["tP"] == c0["tP"] + 2 * 32
    assert ctx.cnt["vV"] == c0["vV"] + 2 * 32


def emit_lnpass(ctx, dst_name):
    nc = ctx.nc
    T = ctx.tiles
    c0 = dict(ctx.cnt)

    with nc.Block("ln_" + dst_name) as blk:
        @blk.sync
        def _(sy):
            for j in range(QT):
                bb = j % 2
                if j >= 2:
                    ctx.wait(sy, "vV", c0["vV"] + 2 * (j - 1))
                    ctx.wait(sy, "sS", c0["sS"] + 10 * (j - 1))
                ctx.inc(sy.dma_start(out=T[f"p2c_xq{bb}"][:],
                                     in_=ctx.scr["XRES"][j * 128:(j + 1) * 128, :]),
                        f"qL{bb}", 16)
            for q in ("qL0", "qL1"):
                ctx.wait(sy, q, ctx.cnt[q])

        @blk.vector
        def _(v):
            v = VProxy(v, ctx)
            for j in range(QT):
                bb = j % 2
                xq = T[f"p2c_xq{bb}"]
                ctx.wait(v, f"qL{bb}", c0[f"qL{bb}"] + 16 * (j // 2 + 1))
                v.reduce_sum(out=T["st_sum"][:], in_=xq[:], axis=AXX)
                ctx.vsync(v)
                v.tensor_scalar(out=T["st_mean"][:], in0=T["st_sum"][:],
                                scalar1=1.0 / D, scalar2=None, op0=ALU.mult)
                ctx.wait(v, "sS", c0["sS"] + 10 * j + 1)
                v.tensor_scalar(out=T["st_ms"][:], in0=T["st_ssq"][:],
                                scalar1=1.0 / D, scalar2=None, op0=ALU.mult)
                v.tensor_tensor(out=T["st_m2"][:], in0=T["st_mean"][:],
                                in1=T["st_mean"][:], op=ALU.mult)
                ctx.inc(v.tensor_tensor(out=T["st_var"][:], in0=T["st_ms"][:],
                                        in1=T["st_m2"][:], op=ALU.subtract), "vV", 1)
                ctx.wait(v, "sS", c0["sS"] + 10 * j + 2)
                v.reciprocal(out=T["st_rstd"][:], in_=T["st_std"][:])
                if j >= 1:
                    ctx.wait(v, "tP", c0["tP"] + 32 * j)
                ctx.inc(v.tensor_scalar(out=T["p2c_xn"][:], in0=xq[:],
                                        scalar1=T["st_mean"][:], scalar2=T["st_rstd"][:],
                                        op0=ALU.subtract, op1=ALU.mult), "vV", 1)

        @blk.scalar
        def _(s):
            for j in range(QT):
                bb = j % 2
                xq = T[f"p2c_xq{bb}"]
                ctx.wait(s, f"qL{bb}", c0[f"qL{bb}"] + 16 * (j // 2 + 1))
                ctx.inc(s.activation(out=T["st_xsq"][:], in_=xq[:], func=AF.Square,
                                     accum_out=T["st_ssq"][:]), "sS", 1)
                ctx.wait(s, "vV", c0["vV"] + 2 * j + 1)
                ctx.inc(s.activation(out=T["st_std"][:], in_=T["st_var"][:],
                                     func=AF.Sqrt, bias=T["eps"][:]), "sS", 1)
                for gq in range(8):
                    bank = ctx.psum[6 + (gq % 2)]
                    ctx.wait(s, "tP", c0["tP"] + 32 * j + 4 * (gq + 1))
                    ctx.inc(s.activation(
                        out=T[dst_name].rearrange("p (c q) -> p c q", q=NQ)[
                            :, gq * 4:(gq + 1) * 4, j * 128:(j + 1) * 128],
                        in_=bank[:].rearrange("p (c q) -> p c q", q=128),
                        func=AF.Copy), "sS", 1)

        @blk.tensor
        def _(t):
            ident = T["ident"]
            for j in range(QT):
                ctx.wait(t, "vV", c0["vV"] + 2 * j + 2)
                for gq in range(8):
                    bank = ctx.psum[6 + (gq % 2)]
                    gidx = j * 8 + gq
                    if gidx >= 2:
                        prev = gidx - 2
                        ctx.wait(t, "sS", c0["sS"] + 10 * (prev // 8) + 2 + (prev % 8) + 1)
                    for u4 in range(4):
                        cch = gq * 4 + u4
                        ctx.inc(t.matmul(bank[:, u4 * 128:(u4 + 1) * 128],
                                         T["p2c_xn"][:, cch * 128:(cch + 1) * 128],
                                         ident[:], start=True, stop=True), "tP", 1)

    assert ctx.cnt["tP"] == c0["tP"] + 32 * QT
    assert ctx.cnt["vV"] == c0["vV"] + 2 * QT
    assert ctx.cnt["sS"] == c0["sS"] + 10 * QT


def emit_p5(ctx, src_name, final):
    nc = ctx.nc
    T = ctx.tiles
    c0 = dict(ctx.cnt)

    with nc.Block("ffn1_" + src_name) as blk:
        @blk.sync
        def _(sy):
            for hb in range(HB1):
                bb = hb % 2
                if hb >= 2:
                    ctx.wait(sy, "tP", c0["tP"] + 4 * (hb - 1))
                ctx.inc(sy.dma_start(out=T[f"p5_w1{bb}"][:], in_=ctx.scr["W1T"][hb]),
                        f"qL{bb}", 16)
                ctx.inc(sy.dma_start(out=T[f"p5_w3{bb}"][:], in_=ctx.scr["W3T"][hb]),
                        f"qL{bb}", 16)
                if hb >= 1:
                    ctx.wait(sy, "vV", c0["vV"] + 2 * hb)
                    ctx.inc(sy.dma_start(out=ctx.scr["ACT"][hb - 1],
                                         in_=T[f"p5_act{(hb - 1) % 2}"][:]),
                            f"qU{(hb - 1) % 2}", 16)
            ctx.wait(sy, "vV", c0["vV"] + 2 * HB1)
            ctx.inc(sy.dma_start(out=ctx.scr["ACT"][HB1 - 1],
                                 in_=T[f"p5_act{(HB1 - 1) % 2}"][:]),
                    f"qU{(HB1 - 1) % 2}", 16)
            for q in ("qL0", "qL1", "qU0", "qU1"):
                ctx.wait(sy, q, ctx.cnt[q])

        @blk.tensor
        def _(t):
            for hb in range(HB1):
                bb = hb % 2
                ctx.wait(t, f"qL{bb}", c0[f"qL{bb}"] + 32 * (hb // 2 + 1))
                if hb >= 2:
                    ctx.wait(t, "vV", c0["vV"] + 2 * (hb - 1))
                for ab in range(2):
                    w = T[f"p5_w1{bb}"] if ab == 0 else T[f"p5_w3{bb}"]
                    for c in range(32):
                        for th in range(2):
                            bank = ctx.psum[(hb % 2) * 4 + th * 2 + ab]
                            mm = t.matmul(bank[:],
                                          w[:, c * 128:(c + 1) * 128],
                                          T[src_name][:, c * NQ + th * 512:
                                                      c * NQ + (th + 1) * 512],
                                          start=(c == 0), stop=(c == 31),
                                          skip_group_check=True)
                    if ab == 0:
                        ctx.inc(mm, "tP", 2)
                    else:
                        ctx.inc(mm, "tP", 2)

        @blk.scalar
        def _(s):
            for hb in range(HB1):
                for th in range(2):
                    idx = 2 * hb + th
                    bank_a = ctx.psum[(hb % 2) * 4 + th * 2]
                    ctx.wait(s, "tP", c0["tP"] + 4 * hb + 2)
                    if idx >= 1:
                        ctx.wait(s, "vV", c0["vV"] + idx)
                    ctx.inc(s.activation(out=T["p5_sa"][:], in_=bank_a[:],
                                         func=AF.Silu), "sS", 1)

        @blk.vector
        def _(v):
            v = VProxy(v, ctx)
            for hb in range(HB1):
                bb = hb % 2
                for th in range(2):
                    bank_b = ctx.psum[(hb % 2) * 4 + th * 2 + 1]
                    ctx.wait(v, "sS", c0["sS"] + 2 * hb + th + 1)
                    ctx.wait(v, "tP", c0["tP"] + 4 * hb + 4)
                    if hb >= 2 and th == 0:
                        ctx.wait(v, f"qU{bb}", c0[f"qU{bb}"] + 16 * (hb // 2))
                    ctx.inc(v.tensor_tensor(
                        out=T[f"p5_act{bb}"][:, th * 512:(th + 1) * 512],
                        in0=T["p5_sa"][:], in1=bank_b[:], op=ALU.mult), "vV", 1)

    assert ctx.cnt["tP"] == c0["tP"] + 4 * HB1
    assert ctx.cnt["vV"] == c0["vV"] + 2 * HB1
    assert ctx.cnt["sS"] == c0["sS"] + 2 * HB1

    c1 = dict(ctx.cnt)
    with nc.Block("ffn2_" + src_name) as blk:
        @blk.sync
        def _(sy):
            for dwin in range(8):
                for hb in range(HB1):
                    k = dwin * HB1 + hb
                    bb = k % 2
                    if k >= 2:
                        ctx.wait(sy, "tP", c1["tP"] + (k - 1))
                    ctx.inc(sy.dma_start(out=T[f"p5_a{bb}"][:], in_=ctx.scr["ACT"][hb]),
                            f"qL{bb}", 16)
                    ctx.inc(sy.dma_start(
                        out=T[f"p5_w2{bb}"][:],
                        in_=ctx.scr["W2T"][dwin, hb * 128:(hb + 1) * 128, :]),
                        f"qL{bb}", 16)
                for tsub in range(QT):
                    k2 = dwin * QT + tsub
                    bb2 = k2 % 2
                    if k2 >= 2:
                        ctx.wait(sy, "vV", c1["vV"] + k2 - 1)
                    ctx.inc(sy.dma_start(
                        out=T[f"p5_xr{bb2}"][:],
                        in_=ctx.scr["XRES"][tsub * 128:(tsub + 1) * 128,
                                            dwin * 512:(dwin + 1) * 512]),
                        f"qV{bb2}", 16)
                    ctx.wait(sy, "vV", c1["vV"] + k2 + 1)
                    dst = (ctx.out_y if final else ctx.scr["XRES"])
                    ctx.inc(sy.dma_start(
                        out=dst[tsub * 128:(tsub + 1) * 128,
                                dwin * 512:(dwin + 1) * 512],
                        in_=T[f"p5_x2{bb2}"][:]), f"qU{bb2}", 16)
            for q in ("qL0", "qL1", "qU0", "qU1", "qV0", "qV1"):
                ctx.wait(sy, q, ctx.cnt[q])

        @blk.tensor
        def _(t):
            for dwin in range(8):
                for hb in range(HB1):
                    k = dwin * HB1 + hb
                    bb = k % 2
                    ctx.wait(t, f"qL{bb}", c1[f"qL{bb}"] + 32 * (k // 2 + 1))
                    if hb == 0 and dwin >= 1:
                        ctx.wait(t, "vV", c1["vV"] + QT * dwin)
                    for tsub in range(QT):
                        mm = t.matmul(ctx.psum[tsub][:],
                                      T[f"p5_a{bb}"][:, tsub * 128:(tsub + 1) * 128],
                                      T[f"p5_w2{bb}"][:],
                                      start=(hb == 0), stop=(hb == HB1 - 1))
                    ctx.inc(mm, "tP", 1)

        @blk.vector
        def _(v):
            v = VProxy(v, ctx)
            for dwin in range(8):
                ctx.wait(v, "tP", c1["tP"] + HB1 * (dwin + 1))
                for tsub in range(QT):
                    k2 = dwin * QT + tsub
                    bb2 = k2 % 2
                    ctx.wait(v, f"qV{bb2}", c1[f"qV{bb2}"] + 16 * (k2 // 2 + 1))
                    if k2 >= 2:
                        ctx.wait(v, f"qU{bb2}", c1[f"qU{bb2}"] + 16 * (k2 // 2))
                    ctx.inc(v.tensor_tensor(out=T[f"p5_x2{bb2}"][:],
                                            in0=ctx.psum[tsub][:],
                                            in1=T[f"p5_xr{bb2}"][:],
                                            op=ALU.add), "vV", 1)

    assert ctx.cnt["tP"] == c1["tP"] + 8 * HB1
    assert ctx.cnt["vV"] == c1["vV"] + 8 * QT


def emit_p6(ctx, src_name):
    nc = ctx.nc
    T = ctx.tiles
    ident = T["ident"]

    c0 = dict(ctx.cnt)
    with nc.Block("p6a") as blk:
        @blk.sync
        def _(sy):
            for mt in range(32):
                bb = 0
                if mt >= 1:
                    ctx.wait(sy, "vV", c0["vV"] + 3 * mt - 1)
                ctx.inc(sy.dma_start(out=T["p6_mk0"][:],
                                     in_=ctx.inp["ab_mask"][mt * 128:(mt + 1) * 128, :]),
                        "qL0", 16)
            ctx.wait(sy, "vV", c0["vV"] + 3 * 32 + 1)
            ctx.inc(sy.dma_start(out=ctx.scr["BMT"], in_=T["p6_bmT"][:]), "qU0", 16)
            for q in ("qL0", "qU0"):
                ctx.wait(sy, q, ctx.cnt[q])

        @blk.vector
        def _(v):
            v = VProxy(v, ctx)
            for mt in range(32):
                ctx.wait(v, "qL0", c0["qL0"] + 16 * (mt + 1))
                ctx.inc(v.tensor_reduce(
                    out=T["p6_cs"][:],
                    in_=T["p6_mk0"][:].rearrange("p (g c) -> p g c", c=32),
                    axis=AXX, op=ALU.add), "vV", 1)
                ctx.vsync(v)
                ctx.inc(v.tensor_copy(out=T["p6_csb"][:], in_=T["p6_cs"][:]), "vV", 1)
                ctx.wait(v, "tP", c0["tP"] + mt + 1)
                ctx.inc(v.tensor_reduce(
                    out=T["p6_bmT"][:, mt * 4:(mt + 1) * 4],
                    in_=ctx.psum[6][:, 0:128].rearrange("p (f c) -> p f c", c=32),
                    axis=AXX, op=ALU.add), "vV", 1)
            ctx.vsync(v)
            ctx.inc(v.tensor_scalar(out=T["p6_bmT"][:], in0=T["p6_bmT"][:],
                                    scalar1=0.0, scalar2=None, op0=ALU.is_gt), "vV", 1)

        @blk.tensor
        def _(t):
            for mt in range(32):
                ctx.wait(t, "vV", c0["vV"] + 3 * mt + 2)
                if mt >= 1:
                    ctx.wait(t, "vV", c0["vV"] + 3 * mt)
                ctx.inc(t.matmul(ctx.psum[6][:, 0:128], T["p6_csb"][:], ident[:],
                                 start=True, stop=True), "tP", 1)

    assert ctx.cnt["vV"] == c0["vV"] + 3 * 32 + 1
    assert ctx.cnt["tP"] == c0["tP"] + 32

    c1 = dict(ctx.cnt)
    with nc.Block("p6b") as blk:
        @blk.gpsimd
        def _(g):
            for ksub in range(32):
                src_bm = ctx.scr["BMT"][ksub * 4:(ksub + 1) * 4, :]
                src_bm = src_bm.unsqueeze(1).to_broadcast([4, 32, 128])
                ctx.inc(g.dma_start(
                    out=T["p6_bme"][:, ksub * 128:(ksub + 1) * 128],
                    in_=src_bm), "qG", 16)
            for nt in range(32):
                bb = nt % 2
                if nt >= 2:
                    ctx.wait(g, "tP", c1["tP"] + 32 * (nt - 1))
                ctx.inc(g.dma_start(out=T[f"p6_w{bb}"][:],
                                    in_=ctx.inp["ab_w"][nt * 128:(nt + 1) * 128, :]),
                        f"gL{bb}", 16)
            ctx.wait(g, "qG", ctx.cnt["qG"])
            for q in ("gL0", "gL1"):
                ctx.wait(g, q, ctx.cnt[q])

        @blk.tensor
        def _(t):
            for nt in range(32):
                bb = nt % 2
                ctx.wait(t, f"gL{bb}", c1[f"gL{bb}"] + 16 * (nt // 2 + 1))
                for gq in range(8):
                    bank = ctx.psum[gq % 2]
                    gidx = nt * 8 + gq
                    if gidx >= 2:
                        prev = gidx - 2
                        ctx.wait(t, "vV", c1["vV"] + 4 * prev + 4)
                    for u4 in range(4):
                        ksub = gq * 4 + u4
                        ctx.inc(t.matmul(bank[:, u4 * 128:(u4 + 1) * 128],
                                         T[f"p6_w{bb}"][:, ksub * 128:(ksub + 1) * 128],
                                         ident[:], start=True, stop=True), "tP", 1)

        @blk.vector
        def _(v):
            v = VProxy(v, ctx)
            ctx.wait(v, "qG", c1["qG"] + 16 * 32)
            for nt in range(32):
                for gq in range(8):
                    bank = ctx.psum[gq % 2]
                    gidx = nt * 8 + gq
                    ctx.wait(v, "tP", c1["tP"] + 4 * gidx + 4)
                    if gidx >= 2:
                        ctx.wait(v, f"qU{gidx % 2}",
                                 c1[f"qU{gidx % 2}"] + 16 * 4 * (gidx // 2))
                    for u4 in range(4):
                        ksub = gq * 4 + u4
                        bm = T["p6_bme"][:, ksub * 128 + nt * 4:ksub * 128 + (nt + 1) * 4]
                        bm = bm.unsqueeze(2).to_broadcast([128, 4, 32])
                        ctx.inc(v.tensor_tensor(
                            out=T[f"p6_st{gidx % 2}"][:, u4 * 128:(u4 + 1) * 128]
                                .rearrange("p (a b) -> p a b", b=32),
                            in0=bank[:, u4 * 128:(u4 + 1) * 128]
                                .rearrange("p (a b) -> p a b", b=32),
                            in1=bm, op=ALU.mult), "vV", 1)

        @blk.sync
        def _(sy):
            for nt in range(32):
                for gq in range(8):
                    gidx = nt * 8 + gq
                    ctx.wait(sy, "vV", c1["vV"] + 4 * gidx + 4)
                    for u4 in range(4):
                        ksub = gq * 4 + u4
                        ctx.inc(sy.dma_start(
                            out=ctx.scr["WMT"][ksub, nt],
                            in_=T[f"p6_st{gidx % 2}"][:, u4 * 128:(u4 + 1) * 128]),
                            f"qU{gidx % 2}", 16)
            for q in ("qU0", "qU1"):
                ctx.wait(sy, q, ctx.cnt[q])

    assert ctx.cnt["tP"] == c1["tP"] + 32 * 32
    assert ctx.cnt["vV"] == c1["vV"] + 32 * 32

    c2 = dict(ctx.cnt)
    with nc.Block("p6c") as blk:
        @blk.sync
        def _(sy):
            for nw in range(8):
                for kc in range(32):
                    k = nw * 32 + kc
                    bb = k % 2
                    if k >= 2:
                        ctx.wait(sy, "tP", c2["tP"] + (k - 1))
                    ctx.inc(sy.dma_start(
                        out=T[f"p6_r{bb}"][:],
                        in_=ctx.scr["WMT"][kc, nw * 4:(nw + 1) * 4].rearrange(
                            "n p q -> p n q")), f"qL{bb}", 16)
                for tsub in range(QT):
                    k2 = nw * QT + tsub
                    bb2 = k2 % 2
                    if k2 >= 2:
                        ctx.wait(sy, "vV", c2["vV"] + k2 - 1)
                    ctx.inc(sy.dma_start(
                        out=T[f"p5_xr{bb2}"][:],
                        in_=ctx.scr["XRES"][tsub * 128:(tsub + 1) * 128,
                                            nw * 512:(nw + 1) * 512]),
                        f"qV{bb2}", 16)
                    ctx.wait(sy, "vV", c2["vV"] + k2 + 1)
                    ctx.inc(sy.dma_start(
                        out=ctx.scr["XRES"][tsub * 128:(tsub + 1) * 128,
                                            nw * 512:(nw + 1) * 512],
                        in_=T[f"p5_x2{bb2}"][:]), f"qU{bb2}", 16)
            for q in ("qL0", "qL1", "qU0", "qU1", "qV0", "qV1"):
                ctx.wait(sy, q, ctx.cnt[q])

        @blk.tensor
        def _(t):
            for nw in range(8):
                for kc in range(32):
                    k = nw * 32 + kc
                    bb = k % 2
                    ctx.wait(t, f"qL{bb}", c2[f"qL{bb}"] + 16 * (k // 2 + 1))
                    if kc == 0 and nw >= 1:
                        ctx.wait(t, "vV", c2["vV"] + QT * nw)
                    for tsub in range(QT):
                        mm = t.matmul(ctx.psum[tsub][:],
                                      T[src_name][:, kc * NQ + tsub * 128:
                                                  kc * NQ + (tsub + 1) * 128],
                                      T[f"p6_r{bb}"][:],
                                      start=(kc == 0), stop=(kc == 31))
                    ctx.inc(mm, "tP", 1)

        @blk.vector
        def _(v):
            v = VProxy(v, ctx)
            for nw in range(8):
                ctx.wait(v, "tP", c2["tP"] + 32 * (nw + 1))
                for tsub in range(QT):
                    k2 = nw * QT + tsub
                    bb2 = k2 % 2
                    ctx.wait(v, f"qV{bb2}", c2[f"qV{bb2}"] + 16 * (k2 // 2 + 1))
                    if k2 >= 2:
                        ctx.wait(v, f"qU{bb2}", c2[f"qU{bb2}"] + 16 * (k2 // 2))
                    ctx.inc(v.tensor_tensor(out=T[f"p5_x2{bb2}"][:],
                                            in0=ctx.psum[tsub][:],
                                            in1=T[f"p5_xr{bb2}"][:],
                                            op=ALU.add), "vV", 1)

    assert ctx.cnt["tP"] == c2["tP"] + 8 * 32
    assert ctx.cnt["vV"] == c2["vV"] + 8 * QT


def build_program(enabled=("p1",), taps=()):
    nc = bass.Bass("TRN2", target_bir_lowering=False, detect_race_conditions=False)
    ctx = Ctx(nc)

    def ext(name, shape, dt=F32):
        ctx.inp[name] = nc.dram_tensor(name, shape, dt, kind="ExternalInput").ap()

    ext("xb", [S, D]); ext("xq", [NQ, D])
    ext("q_a_w", [D, QR]); ext("q_a_norm_w", [QR]); ext("q_b_w", [QR, H * QKD])
    ext("kv_a_w", [D, KVR + ROPE]); ext("kv_a_norm_w", [KVR])
    ext("kv_b_w", [KVR, H * (NOPE + VD)])
    ext("o_w", [H * VD, D])
    ext("idx_wq_w", [D, IH * IDD]); ext("idx_wk_w", [D, IDD]); ext("idx_weights_w", [D, IH])
    ext("ab_w", [D, D]); ext("ab_mask", [D, D], I32)
    ext("w1", [HID * D // GRP // 2, GRP], I32); ext("s1", [HID * D // GRP])
    ext("w2", [HID * D // GRP // 2, GRP], I32); ext("s2", [HID * D // GRP])
    ext("w3", [HID * D // GRP // 2, GRP], I32); ext("s3", [HID * D // GRP])
    ext("ident_c", [128, 128], BF16)
    ext("qpos_c", [NQ]); ext("kpos_c", [S]); ext("invf2pi_c", [128, ROPE // 2])

    out_y = nc.dram_tensor("y", [NQ, D], F32, kind="ExternalOutput").ap()
    ctx.out_y = out_y

    def scr(name, shape, dt=BF16):
        ctx.scr[name] = nc.dram_tensor("scr_" + name, shape, dt, kind="Internal").ap()

    scr("W1T", [HB1, 128, 32 * 128]); scr("W3T", [HB1, 128, 32 * 128])
    scr("W2T", [8, HID, 512])
    scr("CKV", [S, KVR + ROPE], F32)
    scr("XNQT", [32, QT, 128, 128])
    scr("KnT", [H, NOPE, S]); scr("KpT", [ROPE, S]); scr("V", [H, S, VD])
    scr("QnT", [H, QT, NOPE, 128]); scr("QpT", [H, QT, ROPE, 128])
    scr("OT", [H, QT, VD, 128])
    scr("ACT", [HB1, 128, NQ])
    scr("XRES", [NQ, D], F32)
    scr("BMT", [128, 128], F32)
    scr("WMT", [32, 32, 128, 128])
    scr("MASKDUMP", [128, 36 * KBLK])

    def snap(tag):
        if "snaps" not in enabled:
            return
        out_t = nc.dram_tensor("snap_" + tag, [NQ, D], F32, kind="ExternalOutput").ap()
        with nc.Block("snap_" + tag) as blk:
            @blk.sync
            def _(sy):
                ctx.inc(sy.dma_start(out=out_t, in_=ctx.scr["XRES"]), "qS", 16)
                ctx.wait(sy, "qS", ctx.cnt["qS"])
    ctx.snap = snap

    for n in ("qS", "qG", "qC", "tP", "vV", "sS", "gP", "vB",
              "qL0", "qL1", "qT0", "qT1", "qU0", "qU1", "qV0", "qV1",
              "gL0", "gL1"):
        ctx.sem(n)

    ARENA_B = 200 * 1024
    arena_t = nc.alloc_sbuf_tensor("arena", [128, ARENA_B], U8)
    ctx.arena = Arena(arena_t, ARENA_B)
    A = ctx.arena
    T = ctx.tiles

    T["ident"] = A.alloc(128, BF16)
    T["qpos"] = A.alloc(QT, F32)
    T["kpos"] = A.alloc(TT, F32)
    T["invf2pi"] = A.alloc(ROPE // 2, F32)
    T["kcol"] = A.alloc(KBLK, F32)
    for nm in ("st_sum", "st_mean", "st_ms", "st_m2", "st_var", "st_std",
               "st_rstd", "st_ssq"):
        T[nm] = A.alloc(1, F32)
    T["st_xsq"] = A.alloc(D, F32)
    T["kcol_i"] = T["st_xsq"][:, 0:KBLK].bitcast(I32)
    T["eps"] = A.alloc(1, F32)
    for nm in ("rp_u", "rp_ni", "rp_nf"):
        T[nm] = A.alloc(2 * ROPE, I32 if nm == "rp_ni" else F32)
    for nm in ("rp_fs", "rp_fc", "rp_cos", "rp_sin", "rp_ta", "rp_tb"):
        T[nm] = A.alloc(ROPE // 2, F32)
    T["ikT"] = A.alloc(S, BF16)

    ctx.psum = [nc.alloc_psum_tensor(f"pb{i}", [128, 512], F32) for i in range(8)]

    with nc.Block("p0") as blk:
        @blk.sync
        def _(sy):
            ctx.inc(sy.dma_start(out=T["ident"][:], in_=ctx.inp["ident_c"]), "qS", 16)
            ctx.inc(sy.dma_start(out=T["qpos"][:],
                                 in_=ctx.inp["qpos_c"].rearrange("(j p) -> p j", p=128),
                                 allow_slow_non_contiguous=True), "qS", 16)
            ctx.inc(sy.dma_start(out=T["kpos"][:],
                                 in_=ctx.inp["kpos_c"].rearrange("(t p) -> p t", p=128),
                                 allow_slow_non_contiguous=True), "qS", 16)
            ctx.inc(sy.dma_start(out=T["invf2pi"][:], in_=ctx.inp["invf2pi_c"]), "qS", 16)
            ctx.wait(sy, "qS", ctx.cnt["qS"])

        @blk.gpsimd
        def _(g):
            ctx.inc(g.iota(T["kcol_i"][:], pattern=[[1, KBLK]], base=0,
                           channel_multiplier=0), "gP", 1)

        @blk.vector
        def _(v):
            v = VProxy(v, ctx)
            ctx.wait(v, "gP", 1)
            v.memset(T["eps"][:], EPS)
            v.memset(T["rp_u"][:, 32:64], 0.0)
            ctx.inc(v.tensor_copy(out=T["kcol"][:], in_=T["kcol_i"][:]), "vV", 1)

    A.push()
    NPB = 43
    msb_sh = A.alloc(NPB * 64, I32)
    lsb_sh = A.alloc(NPB * 64, I32)
    for bb in range(2):
        T[f"p1pk{bb}"] = A.alloc(NPB * 64, I32)
        T[f"p1sc{bb}"] = A.alloc(2 * NPB, F32)
        T[f"p1s8{bb}"] = A.alloc(2 * NPB, F32)
        T[f"p1msb{bb}"] = msb_sh
        T[f"p1lsb{bb}"] = lsb_sh
        T[f"p1wt{bb}"] = A.alloc(NPB * 128, BF16)
        T[f"p1st{bb}"] = A.alloc(NPB * 128, BF16)

    if "p1" in enabled:
        emit_p1(ctx)
    A.pop()

    if "p2a" in enabled:
        A.push()
        for bb in range(2):
            T[f"p2a_xb{bb}"] = A.alloc(D, F32)
            T[f"p2a_xn{bb}"] = A.alloc(D, BF16)
            T[f"p2a_xnt{bb}"] = A.alloc(D, BF16)
            T[f"p2a_ckv{bb}"] = A.alloc(KVR + ROPE, F32)
        T["p2a_kvw"] = A.alloc(32 * KVR, BF16)
        T["p2a_pik"] = A.alloc(32 * (ROPE + IDD), BF16)
        T["p2a_iktmp"] = A.alloc(128, BF16)
        emit_p2a(ctx)
        A.pop()

    if "p2b" in enabled:
        A.push()
        for bb in range(2):
            T[f"p2b_ckv{bb}"] = A.alloc(KVR + ROPE, F32)
            T[f"p2b_kv{bb}"] = A.alloc(H * (NOPE + VD), BF16)
            T[f"p2b_knt{bb}"] = A.alloc(D, BF16)
            T[f"p2b_kpt{bb}"] = A.alloc(128, BF16)
        T["p2b_kbw"] = A.alloc(4 * H * (NOPE + VD), BF16)
        T["p2b_normw"] = A.alloc(4, F32)
        T["p2b_kvx"] = A.alloc(KVR, BF16)
        T["p2b_latT"] = A.alloc(KVR, BF16)
        T["p2b_kr"] = A.alloc(ROPE, BF16)
        emit_p2b(ctx)
        A.pop()

    if "p2c" in enabled or "p2d" in enabled or "p3" in enabled:
        A.push()
        T["iqT"] = A.alloc(IH * NQ, BF16)
        T["iw_sb"] = A.alloc(QT * IH, F32)
        SUM_PREF = sum(min(j + 1, NKB) for j in range(QT)) * KBLK
        T["mask"] = A.alloc(SUM_PREF, BF16)

    if "p2c" in enabled:
        A.push()
        T["qlatT"] = A.alloc(12 * NQ, BF16)
        for bb in range(2):
            T[f"p2c_w{bb}"] = A.alloc(512, BF16)
            T[f"p2c_xc{bb}"] = A.alloc(NQ, BF16)
            T[f"qstage{bb}"] = A.alloc(512, BF16)
            T[f"xnt_st{bb}"] = A.alloc(D, BF16)
        T["normq"] = A.alloc(12, F32)
        T["qcos"] = A.alloc(QT * 32, F32)
        T["qsin"] = A.alloc(QT * 32, F32)
        T["qblk"] = A.alloc(384, BF16)
        T["qropea"] = A.alloc(64, BF16)
        T["qropeb"] = A.alloc(64, BF16)
        emit_p2c(ctx)
        A.pop()

    if "p2d" in enabled:
        A.push()
        T["idx_acc"] = A.alloc(S, F32)
        T["p2d_cmp"] = A.alloc(S, BF16)
        for bb in range(2):
            T[f"p2d_r{bb}"] = A.alloc(KBLK, F32)
        for nm in ("p2d_th", "p2d_t2", "p2d_mx", "p2d_lo", "p2d_hi", "p2d_mid",
                   "p2d_cnt", "p2d_b", "p2d_d", "p2d_e"):
            T[nm] = A.alloc(1, F32)
        T["p2d_t"] = A.alloc(KBLK, F32)
        emit_p2d(ctx)
        A.pop()
        if "dump_mask" in enabled:
            cdm = dict(ctx.cnt)
            with nc.Block("dumpmask") as blk:
                @blk.sync
                def _(sy):
                    ctx.inc(sy.dma_start(out=ctx.scr["MASKDUMP"],
                                         in_=T["mask"][:]), "qS", 16)
                    ctx.wait(sy, "qS", ctx.cnt["qS"])

    if "p3" in enabled:
        A.push()
        for bb in range(2):
            T[f"xkn{bb}"] = A.alloc(S, BF16)
            T[f"xv{bb}"] = A.alloc(S, BF16)
            T[f"xqn{bb}"] = A.alloc(QT * 128, BF16)
            T[f"xqp{bb}"] = A.alloc(QT * 128, BF16)
            T[f"ot_sb{bb}"] = A.alloc(128, BF16)
        T["xkp"] = A.alloc(S, BF16)
        for bb in range(2):
            T[f"p3_sm{bb}"] = A.alloc(KBLK, F32)
            T[f"p3_p{bb}"] = A.alloc(KBLK, BF16)
            for nm in ("p3_d", "p3_nm", "p3_al", "p3_lkb"):
                T[f"{nm}{bb}"] = A.alloc(1, F32)
        T["p3_pt"] = A.alloc(KBLK, BF16)
        for j in range(QT):
            T[f"p3_o{j}"] = A.alloc(VD, F32)
            T[f"p3_l{j}"] = A.alloc(1, F32)
            T[f"p3_m{j}"] = A.alloc(1, F32)
        T["p3_on"] = A.alloc(VD, BF16)
        for nm in ("p3_mkb", "p3_mnew", "p3_linv"):
            T[nm] = A.alloc(1, F32)
        emit_p3(ctx)
        A.pop()

    if "p2c" in enabled or "p2d" in enabled or "p3" in enabled:
        A.pop()

    if "p4" in enabled:
        A.push()
        T["p4_ow"] = A.alloc(32 * 1024, BF16)
        for bb in range(2):
            T[f"p4_ot{bb}"] = A.alloc(H * 128, BF16)
            T[f"p4_xq{bb}"] = A.alloc(1024, F32)
            T[f"p4_x1{bb}"] = A.alloc(1024, F32)
        emit_p4a(ctx)
        A.pop()
        ctx.snap("x1")

    def alloc_ln():
        for bb in range(2):
            T[f"p2c_xq{bb}"] = A.alloc(D, F32)
        T["p2c_xn"] = A.alloc(D, BF16)

    def alloc_p5():
        for bb in range(2):
            T[f"p5_w1{bb}"] = A.alloc(32 * 128, BF16)
            T[f"p5_w3{bb}"] = A.alloc(32 * 128, BF16)
            T[f"p5_act{bb}"] = A.alloc(NQ, BF16)
            T[f"p5_a{bb}"] = A.alloc(NQ, BF16)
            T[f"p5_w2{bb}"] = A.alloc(512, BF16)
            T[f"p5_xr{bb}"] = A.alloc(512, F32)
            T[f"p5_x2{bb}"] = A.alloc(512, F32)
        T["p5_sa"] = A.alloc(512, F32)

    if "p5" in enabled:
        A.push()
        T["xn2T"] = A.alloc(32 * NQ, BF16)
        alloc_ln()
        alloc_p5()
        emit_lnpass(ctx, "xn2T")
        emit_p5(ctx, "xn2T", final=False)
        A.pop()
        ctx.snap("x2")

    if "p6" in enabled:
        A.push()
        T["xn3T"] = A.alloc(32 * NQ, BF16)
        alloc_ln()
        T["p6_mk0"] = A.alloc(D, I32)
        for bb in range(2):
            T[f"p6_w{bb}"] = A.alloc(D, BF16)
            T[f"p6_st{bb}"] = A.alloc(512, BF16)
            T[f"p6_r{bb}"] = A.alloc(512, BF16)
            T[f"p5_xr{bb}"] = A.alloc(512, F32)
            T[f"p5_x2{bb}"] = A.alloc(512, F32)
        T["p6_cs"] = A.alloc(128, F32)
        T["p6_csb"] = A.alloc(128, BF16)
        T["p6_bmT"] = A.alloc(128, F32)
        T["p6_bme"] = A.alloc(32 * 128, BF16)
        emit_lnpass(ctx, "xn3T")
        emit_p6(ctx, "xn3T")
        A.pop()
        ctx.snap("x3")

    if "p7" in enabled:
        A.push()
        T["xn4T"] = A.alloc(32 * NQ, BF16)
        alloc_ln()
        alloc_p5()
        emit_lnpass(ctx, "xn4T")
        emit_p5(ctx, "xn4T", final=True)
        A.pop()

    tap_outs = {}
    for (tname, tshape, tdt, tsrc) in taps:
        tap_outs[tname] = nc.dram_tensor("tap_" + tname, tshape, tdt,
                                         kind="ExternalOutput").ap()
    with nc.Block("fin") as blk:
        if "p7" not in enabled:
            zt = A.alloc(D, F32)

            @blk.vector
            def _(v):
                ctx.inc(v.memset(zt[:], 0.0), "vV", 1)

        @blk.sync
        def _(sy):
            n = 0
            for (tname, tshape, tdt, tsrc) in taps:
                sy.dma_start(out=tap_outs[tname], in_=tsrc(ctx)).then_inc(
                    ctx.sems["qS"], 16)
                n += 1
            if "p7" in enabled:
                pass
            else:
                ctx.wait(sy, "vV", ctx.cnt["vV"])
                for i in range(QT):
                    sy.dma_start(out=out_y[i * 128:(i + 1) * 128, :],
                                 in_=zt[:]).then_inc(ctx.sems["qS"], 16)
                    n += 1
            sy.wait_ge(ctx.sems["qS"], ctx.cnt["qS"] + 16 * n)

    return nc, tap_outs


def make_in_maps(inputs):
    x = np.ascontiguousarray(inputs["L0_attn_a_hidden_states"])
    ident = np.eye(128, dtype=ml_dtypes.bfloat16)
    kpos = np.arange(S, dtype=np.float32)
    invf = (1.0 / THETA ** (np.arange(0, ROPE, 2, dtype=np.float32) / ROPE)) / (2 * np.pi)
    invf2pi = np.tile(invf[None, :], (128, 1)).astype(np.float32)
    ims = []
    for core in range(8):
        b, c = core // 4, core % 4
        im = {
            "xb": x[b],
            "xq": x[b, c::4, :],
            "q_a_w": inputs["L0_attn_a_q_a_w"],
            "q_a_norm_w": inputs["L0_attn_a_q_a_norm_w"],
            "q_b_w": inputs["L0_attn_a_q_b_w"],
            "kv_a_w": inputs["L0_attn_a_kv_a_w"],
            "kv_a_norm_w": inputs["L0_attn_a_kv_a_norm_w"],
            "kv_b_w": inputs["L0_attn_a_kv_b_w"],
            "o_w": inputs["L0_attn_a_o_w"],
            "idx_wq_w": inputs["L0_attn_a_idx_wq_w"],
            "idx_wk_w": inputs["L0_attn_a_idx_wk_w"],
            "idx_weights_w": inputs["L0_attn_a_idx_weights_w"],
            "ab_w": inputs["L0_attn_b_weight"],
            "ab_mask": inputs["L0_attn_b_mask"],
            "w1": inputs["L0_ffn_w1"], "s1": inputs["L0_ffn_s1"],
            "w2": inputs["L0_ffn_w2"], "s2": inputs["L0_ffn_s2"],
            "w3": inputs["L0_ffn_w3"], "s3": inputs["L0_ffn_s3"],
            "ident_c": ident,
            "qpos_c": np.arange(c, S, 4, dtype=np.float32),
            "kpos_c": kpos,
            "invf2pi_c": invf2pi,
        }
        ims.append({k: np.ascontiguousarray(v) for k, v in im.items()})
    return ims


ALL_PHASES = ("p1", "p2a", "p2b", "p2c", "p2d", "p3", "p4", "p5", "p6", "p7")


def kernel(**inputs):
    ims = make_in_maps(inputs)
    nc, _ = build_program(enabled=ALL_PHASES)
    res = run_bass_kernel_spmd(nc, ims, core_ids=list(range(8)))
    out = np.empty((B, S, D), np.float32)
    for core in range(8):
        b, c = core // 4, core % 4
        out[b, c::4, :] = np.asarray(res.results[core]["y"], np.float32)
    return out
